# version 08
# baseline (speedup 1.0000x reference)
import math
from contextlib import ExitStack

import numpy as np
import ml_dtypes

import concourse.bass as bass
import concourse.mybir as mybir
from concourse.bass_utils import run_bass_kernel_spmd

F32 = mybir.dt.float32
BF16 = mybir.dt.bfloat16
AF = mybir.ActivationFunctionType
ALU = mybir.AluOpType
AX = mybir.AxisListType

BSZ, SEQ, DM = 4, 2048, 2048
NCH = 16
TOWN = 1024
OWN = [[0, 15, 2, 13, 4, 11, 6, 9], [1, 14, 3, 12, 5, 10, 7, 8]]
PAIR_AB = [(1, 15), (3, 13), (5, 11), (7, 9)]
H0, NOPE, ROPE0, VDIM = 16, 128, 64, 128
QR, KVR = 1024, 512
SCALE0 = 1.0 / math.sqrt(NOPE + ROPE0)
H1, KVH, HD = 16, 4, 128
REP = H1 // KVH
NEXP, IMOE = 16, 1408
ISH = 2816
TOPK = 6
IFF = 5632

BF = ml_dtypes.bfloat16

FCONSTS = [0.0, 1.0, -1.0, 0.5, -0.5, 1e-6, 1e-5, 1e-20, 2.0,
           1.0 / QR, 1.0 / KVR, 1.0 / DM, 1.0 / 1024.0,
           math.log(1.0 / math.sqrt(NOPE + ROPE0)), math.log(1.01),
           1e-30, 1.0 / 16.0]



class Buf:

    __slots__ = ("name", "t", "size", "gran", "wr", "rd")

    def __init__(self, name, t, size, gran=1):
        self.name = name
        self.t = t
        self.size = size
        self.gran = gran
        self.wr = []
        self.rd = []

    def _rg(self, lo, hi):
        if self.gran > 1:
            lo = (lo // self.gran) * self.gran
            hi = ((hi + self.gran - 1) // self.gran) * self.gran
        return lo, hi


def _r(buf, lo=0, hi=None):
    return (buf, lo, buf.size if hi is None else hi)


class Prog:

    CSTREAMS = ("pe", "act", "dve", "pool")
    DSTREAMS = ("dsp", "dpool", "dact")
    ENG_OF = {"pe": "tensor", "act": "scalar", "dve": "vector", "pool": "gpsimd",
              "dsp": "sync", "dpool": "gpsimd", "dact": "scalar"}

    def __init__(self):
        self.ops = []
        self.pos = {s: [] for s in self.CSTREAMS + self.DSTREAMS}
        self.bufs = []
        self.bar = [-1]
        self.bar_ops = []
        self._last_by_stream = {s: -1 for s in self.CSTREAMS + self.DSTREAMS}

    def buf(self, name, t, size, gran=1):
        b = Buf(name, t, size, gran)
        self.bufs.append(b)
        return b

    def op(self, stream, fn, reads=(), writes=(), fence=False):
        idx = len(self.ops)
        deps = set()
        blast = self.bar[-1]
        for (b, lo, hi) in reads:
            lo, hi = b._rg(lo, hi)
            per = {}
            for (wl, wh, wi) in reversed(b.wr):
                if wi <= blast:
                    break
                if wl < hi and lo < wh:
                    s = self.ops[wi]["stream"]
                    if s not in per:
                        per[s] = wi
            deps.update(per.values())
        for (b, lo, hi) in writes:
            lo, hi = b._rg(lo, hi)
            per = {}
            for (wl, wh, wi) in reversed(b.wr):
                if wi <= blast:
                    break
                if wl < hi and lo < wh:
                    s = self.ops[wi]["stream"]
                    if s not in per:
                        per[s] = wi
            deps.update(per.values())
            per = {}
            for (rl, rh, ri) in reversed(b.rd):
                if ri <= blast:
                    break
                if rl < hi and lo < rh:
                    s = self.ops[ri]["stream"]
                    if s not in per or per[s] < ri:
                        per[s] = ri
            deps.update(per.values())
        for (b, lo, hi) in reads:
            lo, hi = b._rg(lo, hi)
            b.rd.append((lo, hi, idx))
        for (b, lo, hi) in writes:
            lo, hi = b._rg(lo, hi)
            b.wr.append((lo, hi, idx))
        if self.bar_ops:
            for j in self.bar_ops[-1]:
                if j >= 0:
                    deps.add(j)
        deps = {d for d in deps
                if self.ops[d]["stream"] != stream or self.ops[d]["fence"]}
        self.ops.append({"stream": stream, "fn": fn, "deps": deps, "fence": fence})
        self.pos[stream].append(idx)
        self._last_by_stream[stream] = idx
        return idx

    def barrier(self):
        self.bar.append(len(self.ops) - 1)
        self.bar_ops.append([self._last_by_stream[s] for s in self.CSTREAMS + self.DSTREAMS])
        for b in self.bufs:
            if len(b.wr) > 64 or len(b.rd) > 64:
                blast = self.bar[-1]
                b.wr = [e for e in b.wr if e[2] > blast] or b.wr[-1:]
                b.rd = [e for e in b.rd if e[2] > blast]

    def finalize(self, nc, final_stream="dsp"):
        ops = self.ops
        flagged = [False] * len(ops)
        for i, o in enumerate(ops):
            if o["stream"] in self.DSTREAMS:
                flagged[i] = True
            for d in o["deps"]:
                flagged[d] = True
        for i in self._final:
            flagged[i] = True
        cnt = {s: 0 for s in self.pos}
        thr = [0] * len(ops)
        for i, o in enumerate(ops):
            s = o["stream"]
            if flagged[i]:
                cnt[s] += 1
                thr[i] = cnt[s] * (16 if s in self.DSTREAMS else 1)
        sems = {}
        stack = ExitStack()
        for s in self.pos:
            if cnt[s]:
                sems[s] = stack.enter_context(nc.semaphore(f"sem_{s}"))
        blk = stack.enter_context(nc.Block())
        know = {s: {} for s in self.pos}
        snap = [None] * len(ops)
        emit = {s: [] for s in self.pos}
        cur = {s: 0 for s in self.pos}
        for i, o in enumerate(ops):
            s = o["stream"]
            k = know[s]
            for d in sorted(o["deps"]):
                ds = ops[d]["stream"]
                if ds == s and not ops[d]["fence"]:
                    continue
                v = thr[d]
                assert v > 0, f"dep {d} not flagged"
                if k.get(ds, 0) < v:
                    emit[s].append(("wait", ds, v))
                    k[ds] = max(k.get(ds, 0), v)
                    sn = snap[d]
                    if sn:
                        for s2, v2 in sn.items():
                            if k.get(s2, 0) < v2:
                                k[s2] = v2
            emit[s].append(("op", i))
            if flagged[i]:
                sn = dict(k)
                sn[s] = thr[i]
                snap[i] = sn
        fin = {}
        for i in self._final:
            s = ops[i]["stream"]
            fin[s] = max(fin.get(s, 0), thr[i])
        for s, v in fin.items():
            if know[final_stream].get(s, 0) < v and not (s == final_stream):
                emit[final_stream].append(("wait", s, v))
            elif s == final_stream:
                emit[final_stream].append(("wait", s, v))

        prog = self

        def run_stream(eng, s):
            for item in emit[s]:
                if item[0] == "wait":
                    eng.wait_ge(sems[item[1]], item[2])
                else:
                    i = item[1]
                    inst = ops[i]["fn"](eng)
                    if flagged[i]:
                        inst.then_inc(sems[s], 16 if s in self.DSTREAMS else 1)

        def merged(s_compute, s_dma):
            items = []
            ic, idma = 0, 0
            ec, ed = emit[s_compute], emit[s_dma]
            def chunks(lst):
                out = []
                pend = []
                for it in lst:
                    if it[0] == "wait":
                        pend.append(it)
                    else:
                        out.append((it[1], pend + [it]))
                        pend = []
                if pend:
                    out.append((10 ** 9, pend))
                return out
            cc, cd = chunks(ec), chunks(ed)
            res = []
            i = j = 0
            while i < len(cc) or j < len(cd):
                if j >= len(cd) or (i < len(cc) and cc[i][0] < cd[j][0]):
                    res.extend((s_compute, it) for it in cc[i][1]); i += 1
                else:
                    res.extend((s_dma, it) for it in cd[j][1]); j += 1
            return res

        pool_items = merged("pool", "dpool")
        act_items = merged("act", "dact")

        @blk.tensor
        def _(t):
            run_stream(t, "pe")

        @blk.vector
        def _(v):
            run_stream(v, "dve")

        @blk.scalar
        def _(sc):
            for s, item in act_items:
                if item[0] == "wait":
                    sc.wait_ge(sems[item[1]], item[2])
                else:
                    i = item[1]
                    inst = ops[i]["fn"](sc)
                    if flagged[i]:
                        inst.then_inc(sems[s], 16 if s in self.DSTREAMS else 1)

        @blk.gpsimd
        def _(g):
            for s, item in pool_items:
                if item[0] == "wait":
                    g.wait_ge(sems[item[1]], item[2])
                else:
                    i = item[1]
                    inst = ops[i]["fn"](g)
                    if flagged[i]:
                        inst.then_inc(sems[s], 16 if s in self.DSTREAMS else 1)

        @blk.sync
        def _(sy):
            run_stream(sy, "dsp")

        stack.close()

    _final = ()
    _extra_final = None

    def mark_extra_final(self, idx):
        if self._extra_final is None:
            self._extra_final = []
        self._extra_final.append(idx)

    def mark_final(self, idxs):
        self._final = list(idxs) + (self._extra_final or [])

    def dma(self, q, out_ap, in_ap, reads=(), writes=()):
        return self.op(q, lambda e: e.dma_start(out=out_ap, in_=in_ap),
                       reads=reads, writes=writes)

    def mm(self, out_ap, lhsT_ap, rhs_ap, start, stop, reads=(), writes=()):
        return self.op("pe", lambda e: e.matmul(out_ap, lhsT_ap, rhs_ap,
                                                start=start, stop=stop),
                       reads=reads, writes=writes)



def _t(a):
    return np.ascontiguousarray(np.asarray(a).T)


def _rope_perm(n):
    return np.concatenate([np.arange(0, n, 2), np.arange(1, n, 2)])


def prep_shared(inp):
    d = {}
    d["wq_downT"] = _t(inp["L0_attn_wq_down"])
    d["wq_upT"] = _t(inp["L0_attn_wq_up"])
    wqr = np.asarray(inp["L0_attn_wq_rope"]).reshape(H0, ROPE0, QR)
    perm = _rope_perm(ROPE0)
    wqr = wqr[:, perm, :].reshape(H0 * ROPE0, QR)
    d["wq_ropeT"] = _t(wqr)
    d["wkv_downT"] = _t(inp["L0_attn_wkv_down"])
    wkv = np.asarray(inp["L0_attn_wkv_up"]).reshape(H0, NOPE + VDIM, KVR)
    d["wkv_upKT"] = _t(wkv[:, :NOPE, :].reshape(H0 * NOPE, KVR))
    d["wkv_upVT"] = _t(wkv[:, NOPE:, :].reshape(H0 * VDIM, KVR))
    wkr = np.asarray(inp["L0_attn_wk_rope"])[perm, :]
    d["wk_ropeT"] = _t(wkr)
    d["woT"] = _t(inp["L0_attn_wo"])
    d["q_norm_w"] = np.ascontiguousarray(
        np.asarray(inp["L0_attn_q_norm_w"]).reshape(QR // 128, 128).T)
    d["kv_norm_w"] = np.ascontiguousarray(
        np.asarray(inp["L0_attn_kv_norm_w"]).reshape(KVR // 128, 128).T)
    d["gate_wT"] = _t(inp["L0_ffn_gate_w"])
    d["w13T"] = np.ascontiguousarray(np.asarray(inp["L0_ffn_w13"]).transpose(0, 2, 1))
    d["w2T"] = np.ascontiguousarray(np.asarray(inp["L0_ffn_w2"]).transpose(0, 2, 1))
    d["sw13T"] = _t(inp["L0_ffn_sw13"])
    d["sw2T"] = _t(inp["L0_ffn_sw2"])
    d["WqT"] = _t(inp["L1_attn_Wq"])
    d["WkT"] = _t(inp["L1_attn_Wk"])
    d["WvT"] = _t(inp["L1_attn_Wv"])
    d["WoT"] = _t(inp["L1_attn_Wo"])
    d["f_gateT"] = _t(inp["L1_ffn_gate_w"])
    d["f_upT"] = _t(inp["L1_ffn_up_w"])
    d["f_downT"] = _t(inp["L1_ffn_down_w"])
    d["f_gs"] = np.ascontiguousarray(
        np.asarray(inp["L1_ffn_gate_s"]).reshape(IFF // 128, 128).T)
    d["f_us"] = np.ascontiguousarray(
        np.asarray(inp["L1_ffn_up_s"]).reshape(IFF // 128, 128).T)
    d["f_ds"] = np.ascontiguousarray(
        np.asarray(inp["L1_ffn_down_s"]).reshape(DM // 128, 128).T)
    invf0 = 1.0 / (10000.0 ** (np.arange(0, ROPE0, 2, dtype=np.float64) / ROPE0))
    tpos = np.arange(SEQ, dtype=np.float64)
    fr0 = np.outer(invf0, tpos)
    c0 = np.cos(fr0).astype(np.float32)
    s0 = np.sin(fr0).astype(np.float32)
    d["c0tab"] = np.ascontiguousarray(np.concatenate([c0, c0], 0))
    d["s0tab"] = np.ascontiguousarray(np.concatenate([-s0, s0], 0))
    d["perm64"] = np.ascontiguousarray(
        np.block([[np.zeros((32, 32)), np.eye(32)],
                  [np.eye(32), np.zeros((32, 32))]]).astype(np.float32))
    invf1 = 1.0 / (10000.0 ** (np.arange(0, HD, 2, dtype=np.float64) / HD))
    fr1 = np.outer(invf1, tpos)
    c1 = np.cos(fr1).astype(np.float32)
    s1 = np.sin(fr1).astype(np.float32)
    d["c1tab"] = np.ascontiguousarray(np.concatenate([c1, c1], 0))
    d["s1tab"] = np.ascontiguousarray(np.concatenate([s1, s1], 0))
    eye64 = np.eye(64)
    z64 = np.zeros((64, 64))
    P1 = np.block([[z64, -eye64], [eye64, z64]])
    d["perm128"] = np.ascontiguousarray(P1.T.astype(np.float32))
    tri = (np.arange(128)[:, None] <= np.arange(128)[None, :]).astype(np.float32)
    d["tri128"] = np.ascontiguousarray(tri)
    d["tri1024"] = np.ascontiguousarray(np.tile(tri, (1, 8)))
    d["ones128"] = np.ones((128, 1), dtype=np.float32)
    d["ones_row"] = np.ones((1, 128), dtype=np.float32)
    d["ident128"] = np.eye(128, dtype=np.float32)
    d["fconsts"] = np.ascontiguousarray(
        np.tile(np.asarray(FCONSTS, dtype=np.float32).reshape(1, -1), (128, 1)))
    return d


def prep_core(inp, shared, core):
    par = core % 2
    b = core // 2
    own = OWN[par]
    x = np.asarray(inp["L0_attn_x"])[b]
    xT = _t(x)
    d = {}
    d["xT"] = xT
    xT_own = np.concatenate([xT[:, g * 128:(g + 1) * 128] for g in own], axis=1)
    d["xT_own"] = np.ascontiguousarray(xT_own)
    masks = []
    for p, (A, Bc) in enumerate(PAIR_AB):
        a_g, b_g = own[2 * p], own[2 * p + 1]
        cols = []
        for kc in range(Bc + 1):
            w = 256 if kc <= A else 128
            m = np.zeros((128, w), dtype=np.float32)
            kpos = kc * 128 + np.arange(128)
            if w == 256:
                qa = a_g * 128 + np.arange(128)
                qb = b_g * 128 + np.arange(128)
                m[:, :128] = (kpos[:, None] <= qa[None, :])
                m[:, 128:] = (kpos[:, None] <= qb[None, :])
            else:
                qb = b_g * 128 + np.arange(128)
                m[:, :] = (kpos[:, None] <= qb[None, :])
            cols.append(m)
        masks.append(np.concatenate(cols, axis=1))
    d["l0mask"] = np.ascontiguousarray(np.stack(masks, 0))
    ocols = np.concatenate([np.arange(g * 128, (g + 1) * 128) for g in own])
    d["c0_own"] = np.ascontiguousarray(shared["c0tab"][:, ocols])
    d["s0_own"] = np.ascontiguousarray(shared["s0tab"][:, ocols])
    d["c1_own"] = np.ascontiguousarray(shared["c1tab"][:, ocols])
    d["s1_own"] = np.ascontiguousarray(shared["s1tab"][:, ocols])
    d["c1tok"] = np.ascontiguousarray(shared["c1tab"][:, ocols].T)
    d["s1tok"] = np.ascontiguousarray(shared["s1tab"][:, ocols].T)
    slot_global = [OWN[0][i] for i in range(8)] + [OWN[1][i] for i in range(8)]
    wmat = np.zeros((128, 8 * 16), dtype=np.float32)
    for c in range(8):
        gc = own[c]
        for sl in range(16):
            if slot_global[sl] < gc:
                wmat[:, c * 16 + sl] = 1.0
    d["l1prefw"] = np.ascontiguousarray(wmat)
    return d



class Builder:
    def __init__(self, nc, P, stack):
        self.nc = nc
        self.P = P
        self.stack = stack
        self.dram = {}
        self.sb = {}

    def din(self, name, shape, dtype=F32):
        t = self.nc.dram_tensor(name, list(shape), dtype, kind="ExternalInput")
        self.dram[name] = t.ap()
        return self.dram[name]

    def dout(self, name, shape, dtype=F32):
        t = self.nc.dram_tensor(name, list(shape), dtype, kind="ExternalOutput")
        self.dram[name] = t.ap()
        return self.dram[name]

    def dint(self, name, shape, dtype=F32):
        t = self.nc.dram_tensor(name, list(shape), dtype, kind="Internal")
        self.dram[name] = t.ap()
        return self.dram[name]

    def sbuf(self, name, shape, dtype=F32):
        t = self.stack.enter_context(self.nc.sbuf_tensor(name, list(shape), dtype))
        b = self.P.buf(name, t, shape[-1] if len(shape) == 2 else int(np.prod(shape[1:])))
        self.sb[name] = (t, b)
        return t, b

    def psum(self, name, shape, dtype=F32):
        t = self.stack.enter_context(self.nc.psum_tensor(name, list(shape), dtype))
        b = self.P.buf(name, t, shape[-1], gran=512)
        self.sb[name] = (t, b)
        return t, b



AR16_COLS = 54272
AR32_COLS = 23104


class _ArView:

    __slots__ = ("t", "off", "part", "cols", "b")

    def __init__(self, t, off, part, cols, b):
        self.t = t
        self.off = off
        self.part = part
        self.cols = cols
        self.b = b

    def __getitem__(self, idx):
        p, c = idx
        cs = c.start or 0
        ce = c.stop if c.stop is not None else self.cols
        assert ce <= self.cols
        return self.t[p, self.off + cs: self.off + ce]

    def ap(self):
        return self.t[0:self.part, self.off:self.off + self.cols]

    def r(self, lo=0, hi=None):
        return (self.b, lo, self.cols if hi is None else hi)


class Arena:
    def __init__(self, bd, name, cols, dtype):
        self.bd = bd
        self.name = name
        self.t = bd.stack.enter_context(bd.nc.sbuf_tensor(name, [128, cols], dtype))
        self.cols = cols
        self.off = 0
        self.marks = []

    def alloc(self, name, part, cols):
        assert self.off + cols <= self.cols, \
            f"arena {self.name}/{name}: {self.off}+{cols} > {self.cols}"
        v = self.view_at(name, self.off, part, cols)
        self.off += cols
        return v

    def view_at(self, name, off, part, cols):
        b = self.bd.P.buf(f"{self.name}.{name}", self.t, cols)
        return _ArView(self.t, off, part, cols, b)

    def mark(self):
        self.marks.append(self.off)

    def release(self):
        self.off = self.marks.pop()



def _seg_layout(pair):
    A, Bc = PAIR_AB[pair]
    segs = []
    off = 0
    for kc in range(Bc + 1):
        w = 256 if kc <= A else 128
        segs.append((kc, w, off))
        off += w
    assert off == 2304
    return segs


def emit_consts(bd):
    P = bd.P
    c = {}

    def ld(name, shape, dt, src_ap):
        t, b = bd.sbuf("c_" + name, shape, dt)
        qq = "dpool" if dt == BF16 else "dsp"
        P.dma(qq, t[:], src_ap, writes=[_r(b)])
        c[name] = t

    ld("ones_b", [128, 1], BF16, bd.dram["ones128"])
    ld("ones_f", [128, 1], F32, bd.dram["ones128"])
    ld("onesrow_f", [1, 128], F32, bd.dram["ones_row"])
    ld("tri", [128, 128], BF16, bd.dram["tri128"])
    ld("perm64", [64, 64], BF16, bd.dram["perm64"])
    ld("perm128", [128, 128], BF16, bd.dram["perm128"])
    ld("c0_own", [64, TOWN], BF16, bd.dram["c0_own"])
    ld("s0_own", [64, TOWN], BF16, bd.dram["s0_own"])
    ld("ident_b", [128, 128], BF16, bd.dram["ident128"])
    ld("trirep", [128, 1024], BF16, bd.dram["tri1024"])
    ld("ident_f", [128, 128], F32, bd.dram["ident128"])
    ld("onesrow_b", [1, 128], BF16, bd.dram["ones_row"])
    ld("qnw", [128, 8], F32, bd.dram["q_norm_w"])
    ld("kvnw", [128, 4], F32, bd.dram["kv_norm_w"])
    fv, fv_b = bd.sbuf("c_fvals", [128, len(FCONSTS)], F32)
    P.dma("dsp", fv[:], bd.dram["fconsts"], writes=[_r(fv_b)])
    for i, v in enumerate(FCONSTS):
        bd.nc.const_aps.register(F32, v, fv[:, i:i + 1])
        bd.nc.const_aps.register(F32, float(np.float32(v)), fv[:, i:i + 1])
    bd.C = c
    P.barrier()


def _postln(bd, resid_from_psum_fn, n_oc=16):
    pass


def emit_l0_attn(bd):
    nc, P, C = bd.nc, bd.P, bd.C
    S5, S5_b = bd.ps["S5"]
    M2, M2_b = bd.ps["M2"]
    A1, A1_b = bd.ps["A1"]
    A16, A32 = bd.A16, bd.A32
    X = bd.X
    Xb = bd.Xb

    A16.mark(); A32.mark()
    qcn = A16.alloc("qcn", 128, 8 * TOWN)
    kvn = A16.alloc("kvn", 128, 4 * SEQ)
    kra = A16.alloc("kra", 65, SEQ)
    krss = A32.alloc("krss", 1, SEQ)
    attnout = A16.view_at("attnout", Xb.off, 128, 16 * TOWN)
    xbuf = A16.view_at("xbuf", Xb.off, 128, 16 * TOWN)

    A16.mark(); A32.mark()
    wdbl = [A16.alloc("wdbl0", 128, 16 * 128), A16.alloc("wdbl1", 128, 16 * 128)]
    pre = A16.alloc("pre", 128, 8 * 1024)
    c0f = A16.alloc("c0f", 64, SEQ)
    s0f = A16.alloc("s0f", 64, SEQ)
    krx = A16.alloc("krx", 64, TOWN)
    sq = A32.alloc("sqB", 128, 1024)
    rt1 = A32.alloc("rt1", 64, TOWN)
    rt2 = A32.alloc("rt2", 64, TOWN)
    rrow = A32.alloc("rrowB", 1, TOWN)

    P.dma("dpool", c0f.ap(), bd.dram["c0tab"], writes=[c0f.r()])
    P.dma("dpool", s0f.ap(), bd.dram["s0tab"], writes=[s0f.r()])
    P.dma("dpool", xbuf.ap().rearrange("p (k t) -> p k t", k=16),
          bd.dram["xT_own"].rearrange("(k p) t -> p k t", p=128),
          writes=[xbuf.r()])

    def proj_norm(wT_name, nout, rank, norm_w, ln_bias, out, out_T, hf):
        for j in range(nout):
            wb = wdbl[j % 2]
            P.dma("dpool", wb.ap().rearrange("p (k o) -> p k o", k=16),
                  bd.dram[wT_name].rearrange("(k p) o -> p k o", p=128)[:, :, j * 128:(j + 1) * 128],
                  writes=[wb.r()])
            for nt in range(2):
                for k in range(16):
                    P.mm(M2[:, nt * 512:(nt + 1) * 512],
                         wb[:, k * 128:(k + 1) * 128],
                         xbuf[:, k * TOWN + nt * 512: k * TOWN + (nt + 1) * 512],
                         start=(k == 0), stop=(k == 15),
                         reads=[wb.r(k * 128, (k + 1) * 128),
                                xbuf.r(k * TOWN + nt * 512, k * TOWN + (nt + 1) * 512)],
                         writes=[_r(M2_b, nt * 512, (nt + 1) * 512)])
            P.op("act", lambda e, j=j: e.copy(pre[:, j * 1024:(j + 1) * 1024], M2[:]),
                 reads=[_r(M2_b)], writes=[pre.r(j * 1024, (j + 1) * 1024)])
            P.op("dve", lambda e, j=j: e.tensor_tensor(
                    sq.ap(), pre[:, j * 1024:(j + 1) * 1024],
                    pre[:, j * 1024:(j + 1) * 1024], op=ALU.mult),
                 reads=[pre.r(j * 1024, (j + 1) * 1024)], writes=[sq.r()])
            for nt in range(2):
                P.mm(S5[0:1, nt * 512:(nt + 1) * 512], C["ones_f"][:],
                     sq[:, nt * 512:(nt + 1) * 512],
                     start=(j == 0), stop=(j == nout - 1),
                     reads=[sq.r(nt * 512, (nt + 1) * 512)],
                     writes=[_r(S5_b, nt * 512, (nt + 1) * 512)])
        P.op("act", lambda e: e.activation(rrow.ap(), S5[0:1, 0:1024], AF.Ln,
                                           scale=1.0 / rank, bias=1e-6),
             reads=[_r(S5_b, 0, 1024)], writes=[rrow.r()])
        P.op("act", lambda e: e.activation(rrow.ap(), rrow.ap(), AF.Exp,
                                           scale=-0.5, bias=ln_bias),
             reads=[rrow.r()], writes=[rrow.r()])
        for nt in range(2):
            P.mm(M2[:, nt * 512:(nt + 1) * 512], C["onesrow_f"][:],
                 rrow[0:1, nt * 512:(nt + 1) * 512], start=True, stop=True,
                 reads=[rrow.r()], writes=[_r(M2_b, nt * 512, (nt + 1) * 512)])
        for j in range(nout):
            P.op("dve", lambda e, j=j: e.scalar_tensor_tensor(
                    out[:, j * out_T + hf * 1024: j * out_T + hf * 1024 + 1024],
                    pre[:, j * 1024:(j + 1) * 1024],
                    norm_w[:, j:j + 1], M2[:],
                    op0=ALU.mult, op1=ALU.mult),
                 reads=[pre.r(j * 1024, (j + 1) * 1024), _r(M2_b)],
                 writes=[out.r(j * out_T + hf * 1024, j * out_T + hf * 1024 + 1024)])
        P.barrier()

    proj_norm("wq_downT", 8, QR, C["qnw"], math.log(SCALE0), qcn, TOWN, 0)

    for hf in range(2):
        P.dma("dpool", xbuf.ap().rearrange("p (k t) -> p k t", k=16),
              bd.dram["xT"].rearrange("(k p) t -> p k t", p=128)[:, :, hf * 1024:(hf + 1) * 1024],
              writes=[xbuf.r()])
        proj_norm("wkv_downT", 4, KVR, C["kvnw"], 0.0, kvn, SEQ, hf)
        wb = wdbl[0]
        P.dma("dpool", wb.ap().rearrange("p (k o) -> p k o", k=16)[:, :, 0:64],
              bd.dram["wk_ropeT"].rearrange("(k p) o -> p k o", p=128),
              writes=[wb.r()])
        for nt in range(2):
            for k in range(16):
                P.mm(M2[0:64, nt * 512:(nt + 1) * 512],
                     wb[:, k * 128:k * 128 + 64],
                     xbuf[:, k * TOWN + nt * 512: k * TOWN + (nt + 1) * 512],
                     start=(k == 0), stop=(k == 15),
                     reads=[wb.r(k * 128, k * 128 + 64),
                            xbuf.r(k * TOWN + nt * 512, k * TOWN + (nt + 1) * 512)],
                     writes=[_r(M2_b, nt * 512, (nt + 1) * 512)])
        P.op("act", lambda e: e.copy(krx.ap(), M2[0:64, 0:1024]),
             reads=[_r(M2_b)], writes=[krx.r()])
        for nt in range(2):
            P.mm(S5[0:64, nt * 512:(nt + 1) * 512], C["perm64"][:],
                 krx[0:64, nt * 512:(nt + 1) * 512], start=True, stop=True,
                 reads=[krx.r(nt * 512, (nt + 1) * 512)],
                 writes=[_r(S5_b, nt * 512, (nt + 1) * 512)])
        P.op("dve", lambda e, hf=hf: e.tensor_tensor(
                rt1.ap(), krx.ap(), c0f[0:64, hf * 1024:(hf + 1) * 1024], op=ALU.mult),
             reads=[krx.r(), c0f.r()], writes=[rt1.r()])
        P.op("dve", lambda e, hf=hf: e.tensor_tensor(
                rt2.ap(), S5[0:64, 0:1024], s0f[0:64, hf * 1024:(hf + 1) * 1024],
                op=ALU.mult),
             reads=[_r(S5_b, 0, 1024), s0f.r()], writes=[rt2.r()])
        P.op("dve", lambda e, hf=hf: e.tensor_tensor(
                kra[0:64, hf * 1024:(hf + 1) * 1024], rt1.ap(), rt2.ap(), op=ALU.add),
             reads=[rt1.r(), rt2.r()],
             writes=[kra.r(hf * 1024, (hf + 1) * 1024)])
        P.op("dve", lambda e, hf=hf: e.tensor_tensor(
                rt1.ap(), kra[0:64, hf * 1024:(hf + 1) * 1024],
                kra[0:64, hf * 1024:(hf + 1) * 1024], op=ALU.mult),
             reads=[kra.r(hf * 1024, (hf + 1) * 1024)], writes=[rt1.r()])
        for nt in range(2):
            P.mm(M2[0:1, nt * 512:(nt + 1) * 512], C["ones_f"][0:64, :],
                 rt1[0:64, nt * 512:(nt + 1) * 512], start=True, stop=True,
                 reads=[rt1.r(nt * 512, (nt + 1) * 512)],
                 writes=[_r(M2_b, nt * 512, (nt + 1) * 512)])
        P.op("dve", lambda e, hf=hf: e.tensor_copy(
                krss[0:1, hf * 1024:(hf + 1) * 1024], M2[0:1, 0:1024]),
             reads=[_r(M2_b, 0, 1024)], writes=[krss.r(hf * 1024, (hf + 1) * 1024)])
        P.barrier()
    P.op("dve", lambda e: e.memset(kra[64:65, 0:SEQ], -1.0), writes=[kra.r()])
    A16.release(); A32.release()
    P.barrier()

    A16.mark(); A32.mark()
    NG = 16
    v_g = A16.alloc("v_g", 128, 16 * 128)
    kh_g = A16.alloc("kh_g", 128, SEQ)
    qn_g = A16.alloc("qn_g", 128, TOWN)
    qa_g = A16.alloc("qa_g", 65, TOWN)
    qx = A16.alloc("qx", 64, TOWN)
    PT = A16.alloc("PT", 128, 2304)
    msk = A16.alloc("msk", 128, 2304)
    wqu = A16.alloc("wqu", 128, 8 * 128)
    wqr = A16.alloc("wqr", 128, 8 * 64)
    wkK = A16.alloc("wkK", 128, 4 * 128)
    wkV = A16.alloc("wkV", 128, 4 * 128)
    sqc = A32.alloc("sqC", 128, 1024)
    rt1c = A32.alloc("rt1C", 64, TOWN)
    rt2c = A32.alloc("rt2C", 64, TOWN)
    cqr = A32.alloc("cqr", 1, TOWN)
    mrow = A32.alloc("mrow", 1, 8)
    bcsb = A32.alloc("bcsb", 128, 256)
    rcp = A32.alloc("rcp", 1, 256)
    cqb_t, cqb_b = bd.sbuf("cqb", [1, TOWN], BF16)

    for g in range(NG):
        h = g
        P.dma("dpool", wqu.ap().rearrange("p (k o) -> p k o", k=8),
              bd.dram["wq_upT"].rearrange("(k p) o -> p k o", p=128)[:, :, h * 128:(h + 1) * 128],
              writes=[wqu.r()])
        P.dma("dpool", wqr.ap().rearrange("p (k o) -> p k o", k=8),
              bd.dram["wq_ropeT"].rearrange("(k p) o -> p k o", p=128)[:, :, h * 64:(h + 1) * 64],
              writes=[wqr.r()])
        P.dma("dpool", wkK.ap().rearrange("p (k o) -> p k o", k=4),
              bd.dram["wkv_upKT"].rearrange("(k p) o -> p k o", p=128)[:, :, h * 128:(h + 1) * 128],
              writes=[wkK.r()])
        P.dma("dpool", wkV.ap().rearrange("p (k o) -> p k o", k=4),
              bd.dram["wkv_upVT"].rearrange("(k p) o -> p k o", p=128)[:, :, h * 128:(h + 1) * 128],
              writes=[wkV.r()])
        for tc in range(16):
            for kch in range(4):
                P.mm(M2[:, 0:128],
                     kvn[:, kch * SEQ + tc * 128: kch * SEQ + (tc + 1) * 128],
                     wkV[:, kch * 128:(kch + 1) * 128],
                     start=(kch == 0), stop=(kch == 3),
                     reads=[kvn.r(kch * SEQ + tc * 128, kch * SEQ + (tc + 1) * 128),
                            wkV.r(kch * 128, (kch + 1) * 128)],
                     writes=[_r(M2_b, 0, 128)])
            P.op("act", lambda e, tc=tc: e.copy(v_g[:, tc * 128:(tc + 1) * 128],
                                                M2[:, 0:128]),
                 reads=[_r(M2_b, 0, 128)], writes=[v_g.r(tc * 128, (tc + 1) * 128)])
        for nt in range(4):
            for kch in range(4):
                P.mm(S5[:, nt * 512:(nt + 1) * 512],
                     wkK[:, kch * 128:(kch + 1) * 128],
                     kvn[:, kch * SEQ + nt * 512: kch * SEQ + (nt + 1) * 512],
                     start=(kch == 0), stop=(kch == 3),
                     reads=[wkK.r(kch * 128, (kch + 1) * 128),
                            kvn.r(kch * SEQ + nt * 512, kch * SEQ + (nt + 1) * 512)],
                     writes=[_r(S5_b, nt * 512, (nt + 1) * 512)])
        P.op("act", lambda e: e.copy(kh_g.ap(), S5[:, 0:SEQ]),
             reads=[_r(S5_b, 0, SEQ)], writes=[kh_g.r()])
        for half in range(2):
            P.op("dve", lambda e, half=half: e.tensor_tensor(
                    sqc.ap(), kh_g[:, half * 1024: half * 1024 + 1024],
                    kh_g[:, half * 1024: half * 1024 + 1024], op=ALU.mult),
                 reads=[kh_g.r(half * 1024, half * 1024 + 1024)],
                 writes=[sqc.r()])
            for nt in range(2):
                P.mm(S5[0:1, half * 1024 + nt * 512: half * 1024 + (nt + 1) * 512],
                     C["ones_f"][:], sqc[:, nt * 512:(nt + 1) * 512],
                     start=True, stop=True,
                     reads=[sqc.r(nt * 512, (nt + 1) * 512)],
                     writes=[_r(S5_b, half * 1024 + nt * 512, half * 1024 + (nt + 1) * 512)])
        P.op("dve", lambda e: e.tensor_tensor(
                S5[0:1, 0:SEQ], S5[0:1, 0:SEQ], krss[0:1, 0:SEQ], op=ALU.add),
             reads=[_r(S5_b, 0, SEQ), krss.r()], writes=[_r(S5_b, 0, SEQ)])
        P.op("dve", lambda e: e.tensor_reduce(
                mrow[0:1, 0:1], S5[0:1, 0:SEQ], axis=AX.X, op=ALU.max),
             reads=[_r(S5_b, 0, SEQ)], writes=[mrow.r(0, 1)], fence=True)
        P.op("act", lambda e: e.activation(
                mrow[0:1, 1:2], mrow[0:1, 0:1], AF.Ln, bias=1e-20),
             reads=[mrow.r(0, 1)], writes=[mrow.r(1, 2)])
        P.op("dve", lambda e: e.tensor_scalar(
                mrow[0:1, 2:3], mrow[0:1, 1:2],
                0.5, math.log(1.01), op0=ALU.mult, op1=ALU.add),
             reads=[mrow.r(1, 2)], writes=[mrow.r(2, 3)])
        for nt in range(2):
            for k in range(8):
                P.mm(M2[:, nt * 512:(nt + 1) * 512],
                     wqu[:, k * 128:(k + 1) * 128],
                     qcn[:, k * TOWN + nt * 512: k * TOWN + (nt + 1) * 512],
                     start=(k == 0), stop=(k == 7),
                     reads=[wqu.r(k * 128, (k + 1) * 128),
                            qcn.r(k * TOWN + nt * 512, k * TOWN + (nt + 1) * 512)],
                     writes=[_r(M2_b, nt * 512, (nt + 1) * 512)])
        P.op("act", lambda e: e.copy(qn_g.ap(), M2[:]),
             reads=[_r(M2_b)], writes=[qn_g.r()])
        for nt in range(2):
            for k in range(8):
                P.mm(S5[0:64, nt * 512:(nt + 1) * 512],
                     wqr[:, k * 64:(k + 1) * 64],
                     qcn[:, k * TOWN + nt * 512: k * TOWN + (nt + 1) * 512],
                     start=(k == 0), stop=(k == 7),
                     reads=[wqr.r(k * 64, (k + 1) * 64),
                            qcn.r(k * TOWN + nt * 512, k * TOWN + (nt + 1) * 512)],
                     writes=[_r(S5_b, nt * 512, (nt + 1) * 512)])
        P.op("act", lambda e: e.copy(qx.ap(), S5[0:64, 0:1024]),
             reads=[_r(S5_b, 0, 1024)], writes=[qx.r()])
        for nt in range(2):
            P.mm(S5[0:64, 1024 + nt * 512: 1024 + (nt + 1) * 512], C["perm64"][:],
                 qx[0:64, nt * 512:(nt + 1) * 512], start=True, stop=True,
                 reads=[qx.r(nt * 512, (nt + 1) * 512)],
                 writes=[_r(S5_b, 1024 + nt * 512, 1024 + (nt + 1) * 512)])
        P.op("dve", lambda e: e.tensor_tensor(
                rt1c.ap(), qx.ap(), C["c0_own"][:], op=ALU.mult),
             reads=[qx.r()], writes=[rt1c.r()])
        P.op("dve", lambda e: e.tensor_tensor(
                rt2c.ap(), S5[0:64, 1024:2048], C["s0_own"][:], op=ALU.mult),
             reads=[_r(S5_b, 1024, 2048)], writes=[rt2c.r()])
        P.op("dve", lambda e: e.tensor_tensor(
                qa_g[0:64, 0:TOWN], rt1c.ap(), rt2c.ap(), op=ALU.add),
             reads=[rt1c.r(), rt2c.r()], writes=[qa_g.r()])
        P.op("dve", lambda e: e.tensor_tensor(
                sqc.ap(), qn_g.ap(), qn_g.ap(), op=ALU.mult),
             reads=[qn_g.r()], writes=[sqc.r()])
        for nt in range(2):
            P.mm(S5[0:1, nt * 512:(nt + 1) * 512], C["ones_f"][:],
                 sqc[:, nt * 512:(nt + 1) * 512], start=True, stop=False,
                 reads=[sqc.r(nt * 512, (nt + 1) * 512)],
                 writes=[_r(S5_b, nt * 512, (nt + 1) * 512)])
        P.op("dve", lambda e: e.tensor_tensor(
                rt1c.ap(), qa_g[0:64, 0:TOWN], qa_g[0:64, 0:TOWN], op=ALU.mult),
             reads=[qa_g.r()], writes=[rt1c.r()])
        for nt in range(2):
            P.mm(S5[0:1, nt * 512:(nt + 1) * 512], C["ones_f"][0:64, :],
                 rt1c[0:64, nt * 512:(nt + 1) * 512], start=False, stop=True,
                 reads=[rt1c.r(nt * 512, (nt + 1) * 512)],
                 writes=[_r(S5_b, nt * 512, (nt + 1) * 512)])
        P.op("act", lambda e: e.activation(cqr.ap(), S5[0:1, 0:1024], AF.Ln,
                                           bias=1e-20),
             reads=[_r(S5_b, 0, 1024)], writes=[cqr.r()])
        P.op("act", lambda e: e.activation(
                cqb_t[:], cqr.ap(), AF.Exp, scale=0.5, bias=mrow[0:1, 2:3]),
             reads=[cqr.r(), mrow.r(2, 3)], writes=[_r(cqb_b)])
        P.dma("dsp", qa_g[64:65, 0:TOWN], cqb_t[:],
              reads=[_r(cqb_b)], writes=[qa_g.r()])
        for p in range(4):
            segs = _seg_layout(p)
            P.dma("dpool", msk.ap(), bd.dram["l0mask"][p], writes=[msk.r()])
            for si, (kc, w, off) in enumerate(segs):
                qoff = p * 256 + (256 - w)
                P.mm(S5[:, off:off + w],
                     kh_g[:, kc * 128:(kc + 1) * 128],
                     qn_g[:, qoff: p * 256 + 256],
                     start=True, stop=False,
                     reads=[kh_g.r(kc * 128, (kc + 1) * 128),
                            qn_g.r(qoff, p * 256 + 256)],
                     writes=[_r(S5_b, off, off + w)])
                P.mm(S5[:, off:off + w],
                     kra[0:65, kc * 128:(kc + 1) * 128],
                     qa_g[0:65, qoff: p * 256 + 256],
                     start=False, stop=True,
                     reads=[kra.r(kc * 128, (kc + 1) * 128),
                            qa_g.r(qoff, p * 256 + 256)],
                     writes=[_r(S5_b, off, off + w)])
            P.op("act", lambda e: e.activation(PT.ap(), S5[:, 0:2304], AF.Exp),
                 reads=[_r(S5_b, 0, 2304)], writes=[PT.r()])
            P.op("dve", lambda e: e.tensor_tensor(PT.ap(), PT.ap(), msk.ap(),
                                                  op=ALU.mult),
                 reads=[PT.r(), msk.r()], writes=[PT.r()])
            for si, (kc, w, off) in enumerate(segs):
                P.mm(A1[0:1, 256 - w:256], C["ones_b"][:], PT[:, off:off + w],
                     start=(si == 0), stop=(si == len(segs) - 1),
                     reads=[PT.r(off, off + w)], writes=[_r(A1_b, 0, 256)])
            for si, (kc, w, off) in enumerate(segs):
                P.mm(M2[:, 256 - w:256],
                     v_g[:, kc * 128:(kc + 1) * 128],
                     PT[:, off:off + w],
                     start=(si == 0), stop=(si == len(segs) - 1),
                     reads=[v_g.r(kc * 128, (kc + 1) * 128),
                            PT.r(off, off + w)],
                     writes=[_r(M2_b, 0, 256)])
            P.op("dve", lambda e: e.reciprocal(rcp.ap(), A1[0:1, 0:256]),
                 reads=[_r(A1_b, 0, 256)], writes=[rcp.r()], fence=True)
            P.mm(A1[:, 256:512], C["onesrow_f"][:], rcp.ap(),
                 start=True, stop=True,
                 reads=[rcp.r()], writes=[_r(A1_b, 256, 512)])
            P.op("act", lambda e: e.copy(bcsb.ap(), A1[:, 256:512]),
                 reads=[_r(A1_b, 256, 512)], writes=[bcsb.r()])
            P.op("dve", lambda e, h=h, p=p: e.tensor_tensor(
                    attnout[:, h * TOWN + p * 256: h * TOWN + (p + 1) * 256],
                    M2[:, 0:256], bcsb.ap(), op=ALU.mult),
                 reads=[_r(M2_b, 0, 256), bcsb.r()],
                 writes=[attnout.r(h * TOWN + p * 256, h * TOWN + (p + 1) * 256)])
    A16.release(); A32.release()
    P.barrier()

    P.dma("dsp", X.ap().rearrange("p (k t) -> p k t", k=16),
          bd.dram["xT_own"].rearrange("(k p) t -> p k t", p=128),
          writes=[X.r()])
    A16.mark(); A32.mark()
    wo2 = [A16.alloc("wo0", 128, 16 * 128), A16.alloc("wo1", 128, 16 * 128)]
    emit_matmul_residual(bd, "woT", attnout, wo2, X)
    A16.release(); A32.release()
    A32.release()
    emit_postln_inplace(bd)
    A16.release()
    P.barrier()


def emit_matmul_residual(bd, wT_name, src_b16, wo2, X, src_T=TOWN, n_k=16):
    P, C = bd.P, bd.C
    S5, S5_b = bd.ps["S5"]
    M2, M2_b = bd.ps["M2"]
    for oc in range(16):
        wb = wo2[oc % 2]
        P.dma("dpool", wb.ap().rearrange("p (k o) -> p k o", k=n_k),
              bd.dram[wT_name].rearrange("(k p) o -> p k o", p=128)[:, :, oc * 128:(oc + 1) * 128],
              writes=[wb.r()])
        ps, ps_b, po = [(M2, M2_b, 0), (S5, S5_b, 0), (S5, S5_b, 1024)][oc % 3]
        for nt in range(2):
            for k in range(n_k):
                P.mm(ps[:, po + nt * 512: po + (nt + 1) * 512],
                     wb[:, k * 128:(k + 1) * 128],
                     src_b16[:, k * src_T + nt * 512: k * src_T + (nt + 1) * 512],
                     start=(k == 0), stop=(k == n_k - 1),
                     reads=[wb.r(k * 128, (k + 1) * 128),
                            src_b16.r(k * src_T + nt * 512, k * src_T + (nt + 1) * 512)],
                     writes=[_r(ps_b, po + nt * 512, po + (nt + 1) * 512)])
        P.op("dve", lambda e, oc=oc, ps=ps, po=po: e.tensor_tensor(
                X[:, oc * TOWN:(oc + 1) * TOWN], X[:, oc * TOWN:(oc + 1) * TOWN],
                ps[:, po:po + 1024], op=ALU.add),
             reads=[X.r(oc * TOWN, (oc + 1) * TOWN), _r(ps_b, po, po + 1024)],
             writes=[X.r(oc * TOWN, (oc + 1) * TOWN)])


def emit_postln_inplace(bd):
    nc, P, C = bd.nc, bd.P, bd.C
    S5, S5_b = bd.ps["S5"]
    M2, M2_b = bd.ps["M2"]
    A16, A32 = bd.A16, bd.A32
    X, Xb = bd.X, bd.Xb
    P.barrier()
    A32.mark()
    sql = A32.alloc("sqLN", 128, 1024)
    rows = A32.alloc("rowsLN", 1, 4 * 1024)
    for oc in range(16):
        P.op("dve", lambda e, oc=oc: e.tensor_tensor(
                sql.ap(), X[:, oc * TOWN:(oc + 1) * TOWN],
                X[:, oc * TOWN:(oc + 1) * TOWN], op=ALU.mult),
             reads=[X.r(oc * TOWN, (oc + 1) * TOWN)], writes=[sql.r()])
        for nt in range(2):
            P.mm(M2[0:1, nt * 512:(nt + 1) * 512], C["ones_f"][:],
                 X[:, oc * TOWN + nt * 512: oc * TOWN + (nt + 1) * 512],
                 start=(oc == 0), stop=(oc == 15),
                 reads=[X.r(oc * TOWN + nt * 512, oc * TOWN + (nt + 1) * 512)],
                 writes=[_r(M2_b, nt * 512, (nt + 1) * 512)])
            P.mm(S5[0:1, nt * 512:(nt + 1) * 512], C["ones_f"][:],
                 sql[:, nt * 512:(nt + 1) * 512],
                 start=(oc == 0), stop=(oc == 15),
                 reads=[sql.r(nt * 512, (nt + 1) * 512)],
                 writes=[_r(S5_b, nt * 512, (nt + 1) * 512)])
    mu = lambda: rows[0:1, 0:1024]
    msq = lambda: rows[0:1, 1024:2048]
    rstd = lambda: rows[0:1, 2048:3072]
    murs = lambda: rows[0:1, 3072:4096]
    P.op("dve", lambda e: e.tensor_scalar(mu(), M2[0:1, 0:1024], 1.0 / DM, None,
                                          op0=ALU.mult),
         reads=[_r(M2_b, 0, 1024)], writes=[rows.r(0, 1024)])
    P.op("dve", lambda e: e.tensor_scalar(msq(), S5[0:1, 0:1024], 1.0 / DM, None,
                                          op0=ALU.mult),
         reads=[_r(S5_b, 0, 1024)], writes=[rows.r(1024, 2048)])
    P.op("dve", lambda e: e.tensor_tensor(rstd(), mu(), mu(), op=ALU.mult),
         reads=[rows.r(0, 1024)], writes=[rows.r(2048, 3072)])
    P.op("dve", lambda e: e.tensor_tensor(msq(), msq(), rstd(), op=ALU.subtract),
         reads=[rows.r(1024, 3072)], writes=[rows.r(1024, 2048)])
    P.op("act", lambda e: e.activation(rstd(), msq(), AF.Ln, bias=1e-5),
         reads=[rows.r(1024, 2048)], writes=[rows.r(2048, 3072)])
    P.op("act", lambda e: e.activation(rstd(), rstd(), AF.Exp, scale=-0.5),
         reads=[rows.r(2048, 3072)], writes=[rows.r(2048, 3072)])
    P.op("dve", lambda e: e.tensor_tensor(murs(), mu(), rstd(), op=ALU.mult),
         reads=[rows.r(0, 3072)], writes=[rows.r(3072, 4096)])
    for nt in range(2):
        P.mm(M2[:, nt * 512:(nt + 1) * 512], C["onesrow_f"][:],
             rows[0:1, 2048 + nt * 512:2048 + (nt + 1) * 512], start=True, stop=True,
             reads=[rows.r(2048, 3072)], writes=[_r(M2_b, nt * 512, (nt + 1) * 512)])
        P.mm(S5[:, nt * 512:(nt + 1) * 512], C["onesrow_f"][:],
             rows[0:1, 3072 + nt * 512:3072 + (nt + 1) * 512], start=True, stop=True,
             reads=[rows.r(3072, 4096)], writes=[_r(S5_b, nt * 512, (nt + 1) * 512)])
    for oc in range(16):
        P.op("dve", lambda e, oc=oc: e.tensor_tensor(
                X[:, oc * TOWN:(oc + 1) * TOWN], X[:, oc * TOWN:(oc + 1) * TOWN],
                M2[:, 0:1024], op=ALU.mult),
             reads=[X.r(oc * TOWN, (oc + 1) * TOWN), _r(M2_b, 0, 1024)],
             writes=[X.r(oc * TOWN, (oc + 1) * TOWN)])
        P.op("dve", lambda e, oc=oc: e.tensor_tensor(
                X[:, oc * TOWN:(oc + 1) * TOWN], X[:, oc * TOWN:(oc + 1) * TOWN],
                S5[:, 0:1024], op=ALU.subtract),
             reads=[X.r(oc * TOWN, (oc + 1) * TOWN), _r(S5_b, 0, 1024)],
             writes=[X.r(oc * TOWN, (oc + 1) * TOWN)])
        P.op("pool", lambda e, oc=oc: e.tensor_copy(
                Xb[:, oc * TOWN:(oc + 1) * TOWN], X[:, oc * TOWN:(oc + 1) * TOWN]),
             reads=[X.r(oc * TOWN, (oc + 1) * TOWN)],
             writes=[Xb.r(oc * TOWN, (oc + 1) * TOWN)])
    A32.release()
    P.barrier()
def emit_l0_moe(bd):
    nc, P, C = bd.nc, bd.P, bd.C
    S5, S5_b = bd.ps["S5"]
    M2, M2_b = bd.ps["M2"]
    A1, A1_b = bd.ps["A1"]
    A16, A32 = bd.A16, bd.A32
    X, Xb = bd.X, bd.Xb

    A16.mark(); A32.mark()
    gw = A32.alloc("gwf", 128, 16 * 16)
    ct = A32.alloc("ct", 16, TOWN)
    ctrow = A32.alloc("ctrow", 1, TOWN)
    cb = A32.alloc("cb", 128, TOWN)
    h = A16.alloc("h", 128, 11 * 1024)
    wg = [A16.alloc("wg0", 128, 16 * 128), A16.alloc("wg1", 128, 16 * 128)]
    wu = [A16.alloc("wu0", 128, 16 * 128), A16.alloc("wu1", 128, 16 * 128)]
    wd = [A16.alloc("wd0", 128, 11 * 128), A16.alloc("wd1", 128, 11 * 128)]
    probs = A32.alloc("probs", 128, 16)
    m16 = A32.alloc("m16", 128, 16)
    top8 = A32.alloc("top8", 128, 8)
    gr = A32.alloc("gr", 128, 4)
    sg = A32.alloc("sg", 128, 1024)

    P.dma("dsp", gw.ap().rearrange("p (k o) -> p k o", k=16),
          bd.dram["gate_wT"].rearrange("(k p) o -> p k o", p=128),
          writes=[gw.r()])
    for i in range(8):
        for k in range(16):
            P.mm(A1[:, 0:16],
                 X[:, k * TOWN + i * 128: k * TOWN + (i + 1) * 128],
                 gw[:, k * 16:(k + 1) * 16],
                 start=(k == 0), stop=(k == 15),
                 reads=[X.r(k * TOWN + i * 128, k * TOWN + (i + 1) * 128),
                        gw.r(k * 16, (k + 1) * 16)],
                 writes=[_r(A1_b, 0, 16)])
        P.op("dve", lambda e: e.tensor_reduce(gr[:, 0:1], A1[:, 0:16], axis=AX.X,
                                              op=ALU.max, negate=True),
             reads=[_r(A1_b, 0, 16)], writes=[gr.r(0, 1)], fence=True)
        P.op("act", lambda e: e.activation(probs.ap(), A1[:, 0:16], AF.Exp,
                                           bias=gr[:, 0:1]),
             reads=[_r(A1_b, 0, 16), gr.r(0, 1)], writes=[probs.r()])
        P.op("dve", lambda e: e.max(top8.ap(), probs.ap()),
             reads=[probs.r()], writes=[top8.r()], fence=True)
        P.op("dve", lambda e: e.tensor_reduce(gr[:, 2:3], top8[:, 0:6], axis=AX.X,
                                              op=ALU.add),
             reads=[top8.r()], writes=[gr.r(2, 3)], fence=True)
        P.op("dve", lambda e: e.tensor_scalar(gr[:, 2:3], gr[:, 2:3], 1e-20, None,
                                              op0=ALU.add),
             reads=[gr.r(2, 3)], writes=[gr.r(2, 3)])
        P.op("dve", lambda e: e.reciprocal(gr[:, 3:4], gr[:, 2:3]),
             reads=[gr.r(2, 3)], writes=[gr.r(3, 4)], fence=True)
        P.op("dve", lambda e: e.tensor_scalar(m16.ap(), probs.ap(),
                                              top8[:, 5:6], None, op0=ALU.is_ge),
             reads=[probs.r(), top8.r()], writes=[m16.r()])
        P.op("dve", lambda e: e.scalar_tensor_tensor(
                m16.ap(), probs.ap(), gr[:, 3:4], m16.ap(),
                op0=ALU.mult, op1=ALU.mult),
             reads=[probs.r(), gr.r(3, 4), m16.r()], writes=[m16.r()])
        P.mm(A1[0:16, 128:256], m16.ap(), C["ident_f"][:],
             start=True, stop=True,
             reads=[m16.r()], writes=[_r(A1_b, 128, 256)])
        P.op("act", lambda e, i=i: e.copy(ct[0:16, i * 128:(i + 1) * 128],
                                          A1[0:16, 128:256]),
             reads=[_r(A1_b, 128, 256)], writes=[ct.r(i * 128, (i + 1) * 128)])
    P.barrier()
    if getattr(bd, "dbg", False):
        dbg_ct = bd.dout("dbg_ct", (16, TOWN))
        bd.P.mark_extra_final(P.dma("dsp", dbg_ct, ct.ap(), reads=[ct.r()]))

    ROT = [(M2, M2_b, 0), (S5, S5_b, 0), (S5, S5_b, 1024)]

    def swiglu_expert(e_idx, gate_src, up_src, down_src, n_i, scale_by_c):
        for j in range(n_i):
            P.dma("dpool", wg[j % 2].ap().rearrange("p (k o) -> p k o", k=16),
                  gate_src(j), writes=[wg[j % 2].r()])
            P.dma("dpool", wu[j % 2].ap().rearrange("p (k o) -> p k o", k=16),
                  up_src(j), writes=[wu[j % 2].r()])
            gp, gp_b, go = ROT[j % 3]
            up, up_b, uo = ROT[(j + 2) % 3]
            for nt in range(2):
                for k in range(16):
                    P.mm(gp[:, go + nt * 512: go + (nt + 1) * 512],
                         wg[j % 2][:, k * 128:(k + 1) * 128],
                         Xb[:, k * TOWN + nt * 512: k * TOWN + (nt + 1) * 512],
                         start=(k == 0), stop=(k == 15),
                         reads=[wg[j % 2].r(k * 128, (k + 1) * 128),
                                Xb.r(k * TOWN + nt * 512, k * TOWN + (nt + 1) * 512)],
                         writes=[_r(gp_b, go + nt * 512, go + (nt + 1) * 512)])
            for nt in range(2):
                for k in range(16):
                    P.mm(up[:, uo + nt * 512: uo + (nt + 1) * 512],
                         wu[j % 2][:, k * 128:(k + 1) * 128],
                         Xb[:, k * TOWN + nt * 512: k * TOWN + (nt + 1) * 512],
                         start=(k == 0), stop=(k == 15),
                         reads=[wu[j % 2].r(k * 128, (k + 1) * 128),
                                Xb.r(k * TOWN + nt * 512, k * TOWN + (nt + 1) * 512)],
                         writes=[_r(up_b, uo + nt * 512, uo + (nt + 1) * 512)])
            P.op("act", lambda e, gp=gp, gp_b=gp_b, go=go: e.activation(
                    sg.ap(), gp[:, go:go + 1024], AF.Silu),
                 reads=[_r(gp_b, go, go + 1024)], writes=[sg.r()])
            P.op("dve", lambda e, up=up, up_b=up_b, uo=uo: e.tensor_tensor(
                    sg.ap(), up[:, uo:uo + 1024], sg.ap(), op=ALU.mult),
                 reads=[_r(up_b, uo, uo + 1024), sg.r()], writes=[sg.r()])
            if scale_by_c:
                P.op("dve", lambda e, j=j: e.tensor_tensor(
                        h[:, j * 1024:(j + 1) * 1024], sg.ap(), cb.ap(), op=ALU.mult),
                     reads=[sg.r(), cb.r()], writes=[h.r(j * 1024, (j + 1) * 1024)])
            else:
                P.op("dve", lambda e, j=j: e.tensor_copy(
                        h[:, j * 1024:(j + 1) * 1024], sg.ap()),
                     reads=[sg.r()], writes=[h.r(j * 1024, (j + 1) * 1024)])
        for oc in range(16):
            P.dma("dpool", wd[oc % 2].ap().rearrange("p (k o) -> p k o", k=n_i),
                  down_src(oc), writes=[wd[oc % 2].r()])
            ps, ps_b, po = ROT[oc % 3]
            for nt in range(2):
                for k in range(n_i):
                    P.mm(ps[:, po + nt * 512: po + (nt + 1) * 512],
                         wd[oc % 2][:, k * 128:(k + 1) * 128],
                         h[:, k * 1024 + nt * 512: k * 1024 + (nt + 1) * 512],
                         start=(k == 0), stop=(k == n_i - 1),
                         reads=[wd[oc % 2].r(k * 128, (k + 1) * 128),
                                h.r(k * 1024 + nt * 512, k * 1024 + (nt + 1) * 512)],
                         writes=[_r(ps_b, po + nt * 512, po + (nt + 1) * 512)])
            P.op("dve", lambda e, oc=oc, ps=ps, ps_b=ps_b, po=po: e.tensor_tensor(
                    X[:, oc * TOWN:(oc + 1) * TOWN], X[:, oc * TOWN:(oc + 1) * TOWN],
                    ps[:, po:po + 1024], op=ALU.add),
                 reads=[X.r(oc * TOWN, (oc + 1) * TOWN), _r(ps_b, po, po + 1024)],
                 writes=[X.r(oc * TOWN, (oc + 1) * TOWN)])

    w13r = bd.dram["w13T"].rearrange("e (k p) o -> e p k o", p=128)
    w2r = bd.dram["w2T"].rearrange("e (k p) o -> e p k o", p=128)
    for e_idx in range(NEXP):
        P.dma("dsp", ctrow.ap(), ct[e_idx:e_idx + 1, 0:TOWN],
              reads=[ct.r()], writes=[ctrow.r()])
        for nt in range(2):
            P.mm(A1[:, 0:512],
                 C["onesrow_f"][:], ctrow[0:1, nt * 512:(nt + 1) * 512],
                 start=True, stop=True,
                 reads=[ctrow.r()], writes=[_r(A1_b, 0, 512)])
            P.op("act", lambda e, nt=nt: e.copy(
                    cb[:, nt * 512:(nt + 1) * 512], A1[:, 0:512]),
                 reads=[_r(A1_b, 0, 512)],
                 writes=[cb.r(nt * 512, (nt + 1) * 512)])
        swiglu_expert(
            e_idx,
            lambda j, e_idx=e_idx: w13r[e_idx][:, :, j * 128:(j + 1) * 128],
            lambda j, e_idx=e_idx: w13r[e_idx][:, :, IMOE + j * 128: IMOE + (j + 1) * 128],
            lambda oc, e_idx=e_idx: w2r[e_idx][:, :, oc * 128:(oc + 1) * 128],
            11, True)
    sw13r = bd.dram["sw13T"].rearrange("(k p) o -> p k o", p=128)
    sw2r = bd.dram["sw2T"].rearrange("(k p) o -> p k o", p=128)
    for s in range(2):
        swiglu_expert(
            100 + s,
            lambda j, s=s: sw13r[:, :, s * 1408 + j * 128: s * 1408 + (j + 1) * 128],
            lambda j, s=s: sw13r[:, :, ISH + s * 1408 + j * 128: ISH + s * 1408 + (j + 1) * 128],
            lambda oc, s=s: sw2r[:, s * 11:(s + 1) * 11, oc * 128:(oc + 1) * 128],
            11, False)
    A16.release(); A32.release()
    P.barrier()
    emit_postln_inplace(bd)


def emit_l1_attn(bd):
    nc, P, C = bd.nc, bd.P, bd.C
    S5, S5_b = bd.ps["S5"]
    M2, M2_b = bd.ps["M2"]
    A1, A1_b = bd.ps["A1"]
    A16, A32 = bd.A16, bd.A32
    X, Xb = bd.X, bd.Xb
    EPS = 1e-6

    cc_in = bd.dint("cc_in", (4096, 128), BF16)
    cc_out = bd.dint("cc_out", (8192, 128), BF16)
    cc_inz = bd.dint("cc_inz", (128, 32), BF16)
    cc_outz = bd.dint("cc_outz", (256, 32), BF16)
    cc_in_b = P.buf("cc_in", None, 128)
    cc_out_b = P.buf("cc_out", None, 128)
    cc_inz_b = P.buf("cc_inz", None, 32)
    cc_outz_b = P.buf("cc_outz", None, 32)

    A16.mark(); A32.mark()
    c1f = A16.alloc("c1f", 128, TOWN)
    s1f = A16.alloc("s1f", 128, TOWN)
    KfF = A16.alloc("KfF", 128, 4 * TOWN)
    VfT = A16.alloc("VfT", 128, 8 * 512)
    Sprev = A16.alloc("Sprev", 128, 8 * 512)
    zp = A16.alloc("zp", 128, 8 * 4)
    kx = A16.alloc("kx", 128, TOWN)
    wpref = A32.alloc("wpref", 128, 128)
    rtA = A32.alloc("rtA", 128, TOWN)
    rtB = A32.alloc("rtB", 128, TOWN)
    rtC = A32.alloc("rtC", 128, TOWN)
    Sacc = A32.alloc("Sacc", 128, 1024)
    zacc = A32.alloc("zacc", 128, 8)
    zbuf = A32.alloc("zbuf", 128, 32)

    P.dma("dpool", c1f.ap(), bd.dram["c1_own"], writes=[c1f.r()])
    P.dma("dpool", s1f.ap(), bd.dram["s1_own"], writes=[s1f.r()])
    P.dma("dsp", wpref.ap(), bd.dram["l1prefw"], writes=[wpref.r()])

    def elu1(v_in, rd_in, v_out, wr_out, sa, sa_r, sb, sb_r):
        P.op("dve", lambda e: e.tensor_scalar(sa, v_in, 0.0, None, op0=ALU.min),
             reads=rd_in, writes=sa_r)
        P.op("act", lambda e: e.activation(sa, sa, AF.Exp), reads=sa_r, writes=sa_r)
        P.op("dve", lambda e: e.tensor_scalar(sb, v_in, 0.0, None, op0=ALU.max),
             reads=rd_in, writes=sb_r)
        P.op("dve", lambda e: e.tensor_tensor(v_out, sa, sb, op=ALU.add),
             reads=sa_r + sb_r, writes=wr_out)

    A16.mark()
    wk2 = [A16.alloc("wk20", 128, 16 * 128), A16.alloc("wk21", 128, 16 * 128)]
    c1t = A16.alloc("c1t", 128, 8 * 128)
    s1t = A16.alloc("s1t", 128, 8 * 128)
    P.dma("dpool", c1t.ap().rearrange("p (tc d) -> p tc d", tc=8),
          bd.dram["c1tok"].rearrange("(tc p) d -> p tc d", p=128), writes=[c1t.r()])
    P.dma("dpool", s1t.ap().rearrange("p (tc d) -> p tc d", tc=8),
          bd.dram["s1tok"].rearrange("(tc p) d -> p tc d", p=128), writes=[s1t.r()])
    for kvh in range(KVH):
        wb = wk2[kvh % 2]
        P.dma("dpool", wb.ap().rearrange("p (k o) -> p k o", k=16),
              bd.dram["WkT"].rearrange("(k p) o -> p k o", p=128)[:, :, kvh * 128:(kvh + 1) * 128],
              writes=[wb.r()])
        for nt in range(2):
            for k in range(16):
                P.mm(M2[:, nt * 512:(nt + 1) * 512],
                     wb[:, k * 128:(k + 1) * 128],
                     Xb[:, k * TOWN + nt * 512: k * TOWN + (nt + 1) * 512],
                     start=(k == 0), stop=(k == 15),
                     reads=[wb.r(k * 128, (k + 1) * 128),
                            Xb.r(k * TOWN + nt * 512, k * TOWN + (nt + 1) * 512)],
                     writes=[_r(M2_b, nt * 512, (nt + 1) * 512)])
        P.op("act", lambda e: e.copy(kx.ap(), M2[:]), reads=[_r(M2_b)], writes=[kx.r()])
        for nt in range(2):
            P.mm(S5[:, nt * 512:(nt + 1) * 512], C["perm128"][:],
                 kx[:, nt * 512:(nt + 1) * 512], start=True, stop=True,
                 reads=[kx.r(nt * 512, (nt + 1) * 512)],
                 writes=[_r(S5_b, nt * 512, (nt + 1) * 512)])
        P.op("dve", lambda e: e.tensor_tensor(rtA.ap(), kx.ap(), c1f.ap(), op=ALU.mult),
             reads=[kx.r(), c1f.r()], writes=[rtA.r()])
        P.op("dve", lambda e: e.tensor_tensor(rtB.ap(), S5[:, 0:1024], s1f.ap(), op=ALU.mult),
             reads=[_r(S5_b, 0, 1024), s1f.r()], writes=[rtB.r()])
        P.op("dve", lambda e: e.tensor_tensor(rtA.ap(), rtA.ap(), rtB.ap(), op=ALU.add),
             reads=[rtA.r(), rtB.r()], writes=[rtA.r()])
        elu1(rtA.ap(), [rtA.r()], KfF[:, kvh * TOWN:(kvh + 1) * TOWN],
             [KfF.r(kvh * TOWN, (kvh + 1) * TOWN)],
             rtB.ap(), [rtB.r()], rtC.ap(), [rtC.r()])

    KfT = A16.alloc("KfT", 128, 8 * 512)
    ccs = A16.alloc("ccs", 128, 8 * 512)
    wv2 = [A16.alloc("wv20", 128, 16 * 128), A16.alloc("wv21", 128, 16 * 128)]
    ktx = A16.alloc("ktx", 128, 128)
    sc1 = A32.alloc("sc1", 128, 128)
    sc2 = A32.alloc("sc2", 128, 128)
    sc3 = A32.alloc("sc3", 128, 128)
    for kvh in range(KVH):
        P.dma("dpool", wv2[0].ap().rearrange("p (k o) -> p k o", k=16),
              bd.dram["WvT"].rearrange("(k p) o -> p k o", p=128)[:, :, kvh * 128:(kvh + 1) * 128],
              writes=[wv2[0].r()])
        P.dma("dpool", wv2[1].ap().rearrange("p (k o) -> p k o", k=16),
              bd.dram["WkT"].rearrange("(k p) o -> p k o", p=128)[:, :, kvh * 128:(kvh + 1) * 128],
              writes=[wv2[1].r()])
        for tc in range(8):
            for k in range(16):
                P.mm(M2[:, 0:128],
                     Xb[:, k * TOWN + tc * 128: k * TOWN + (tc + 1) * 128],
                     wv2[0][:, k * 128:(k + 1) * 128],
                     start=(k == 0), stop=(k == 15),
                     reads=[Xb.r(k * TOWN + tc * 128, k * TOWN + (tc + 1) * 128),
                            wv2[0].r(k * 128, (k + 1) * 128)],
                     writes=[_r(M2_b, 0, 128)])
            P.op("act", lambda e, tc=tc, kvh=kvh: e.copy(
                    VfT[:, tc * 512 + kvh * 128: tc * 512 + (kvh + 1) * 128],
                    M2[:, 0:128]),
                 reads=[_r(M2_b, 0, 128)],
                 writes=[VfT.r(tc * 512 + kvh * 128, tc * 512 + (kvh + 1) * 128)])
            for k in range(16):
                P.mm(M2[:, 128:256],
                     Xb[:, k * TOWN + tc * 128: k * TOWN + (tc + 1) * 128],
                     wv2[1][:, k * 128:(k + 1) * 128],
                     start=(k == 0), stop=(k == 15),
                     reads=[Xb.r(k * TOWN + tc * 128, k * TOWN + (tc + 1) * 128),
                            wv2[1].r(k * 128, (k + 1) * 128)],
                     writes=[_r(M2_b, 128, 256)])
            P.op("act", lambda e: e.copy(ktx.ap(), M2[:, 128:256]),
                 reads=[_r(M2_b, 128, 256)], writes=[ktx.r()])
            P.op("dve", lambda e, tc=tc: e.tensor_tensor(
                    sc1[:, 0:64], ktx[:, 0:64], c1t[:, tc * 128: tc * 128 + 64],
                    op=ALU.mult),
                 reads=[ktx.r(), c1t.r()], writes=[sc1.r(0, 64)])
            P.op("dve", lambda e, tc=tc: e.tensor_tensor(
                    sc2[:, 0:64], ktx[:, 64:128], s1t[:, tc * 128: tc * 128 + 64],
                    op=ALU.mult),
                 reads=[ktx.r(), s1t.r()], writes=[sc2.r(0, 64)])
            P.op("dve", lambda e: e.tensor_tensor(
                    sc1[:, 0:64], sc1[:, 0:64], sc2[:, 0:64], op=ALU.subtract),
                 reads=[sc1.r(0, 64), sc2.r(0, 64)], writes=[sc1.r(0, 64)])
            P.op("dve", lambda e, tc=tc: e.tensor_tensor(
                    sc1[:, 64:128], ktx[:, 64:128], c1t[:, tc * 128 + 64: tc * 128 + 128],
                    op=ALU.mult),
                 reads=[ktx.r(), c1t.r()], writes=[sc1.r(64, 128)])
            P.op("dve", lambda e, tc=tc: e.tensor_tensor(
                    sc2[:, 64:128], ktx[:, 0:64], s1t[:, tc * 128 + 64: tc * 128 + 128],
                    op=ALU.mult),
                 reads=[ktx.r(), s1t.r()], writes=[sc2.r(64, 128)])
            P.op("dve", lambda e: e.tensor_tensor(
                    sc1[:, 64:128], sc1[:, 64:128], sc2[:, 64:128], op=ALU.add),
                 reads=[sc1.r(64, 128), sc2.r(64, 128)], writes=[sc1.r(64, 128)])
            elu1(sc1.ap(), [sc1.r()],
                 KfT[:, tc * 512 + kvh * 128: tc * 512 + (kvh + 1) * 128],
                 [KfT.r(tc * 512 + kvh * 128, tc * 512 + (kvh + 1) * 128)],
                 sc2.ap(), [sc2.r()], sc3.ap(), [sc3.r()])
            P.mm(A1[:, 0:128],
                 KfT[:, tc * 512 + kvh * 128: tc * 512 + (kvh + 1) * 128],
                 VfT[:, tc * 512 + kvh * 128: tc * 512 + (kvh + 1) * 128],
                 start=True, stop=True,
                 reads=[KfT.r(tc * 512 + kvh * 128, tc * 512 + (kvh + 1) * 128),
                        VfT.r(tc * 512 + kvh * 128, tc * 512 + (kvh + 1) * 128)],
                 writes=[_r(A1_b, 0, 128)])
            P.op("act", lambda e, tc=tc, kvh=kvh: e.copy(
                    ccs[:, tc * 512 + kvh * 128: tc * 512 + (kvh + 1) * 128],
                    A1[:, 0:128]),
                 reads=[_r(A1_b, 0, 128)],
                 writes=[ccs.r(tc * 512 + kvh * 128, tc * 512 + (kvh + 1) * 128)])
            P.op("dve", lambda e, tc=tc, kvh=kvh: e.tensor_reduce(
                    zbuf[:, tc * 4 + kvh: tc * 4 + kvh + 1],
                    KfF[:, kvh * TOWN + tc * 128: kvh * TOWN + (tc + 1) * 128],
                    axis=AX.X, op=ALU.add),
                 reads=[KfF.r(kvh * TOWN + tc * 128, kvh * TOWN + (tc + 1) * 128)],
                 writes=[zbuf.r(tc * 4 + kvh, tc * 4 + kvh + 1)], fence=True)

    P.dma("dsp", cc_in.rearrange("(tc kvh p) d -> p tc kvh d", tc=8, kvh=4),
          ccs.ap().rearrange("p (tc kvh d) -> p tc kvh d", tc=8, kvh=4),
          reads=[ccs.r()], writes=[_r(cc_in_b)])
    P.dma("dpool", cc_inz, zbuf.ap(), reads=[zbuf.r()], writes=[_r(cc_inz_b)])
    rg = [[0, 1], [2, 3], [4, 5], [6, 7]]
    P.op("pool", lambda e: e.collective_compute(
            "AllGather", ALU.bypass, replica_groups=rg,
            ins=[bd.dram["cc_in"].opt()], outs=[bd.dram["cc_out"].opt()]),
         reads=[_r(cc_in_b)], writes=[_r(cc_out_b)], fence=True)
    P.op("pool", lambda e: e.collective_compute(
            "AllGather", ALU.bypass, replica_groups=rg,
            ins=[bd.dram["cc_inz"].opt()], outs=[bd.dram["cc_outz"].opt()]),
         reads=[_r(cc_inz_b)], writes=[_r(cc_outz_b)], fence=True)

    A16.release()
    A16.mark()
    GS = A16.alloc("GS", 128, 16 * 512)
    zg = A16.alloc("zg", 128, 64)
    P.dma("dsp", GS.ap().rearrange("p (r tc kvh d) -> p r tc kvh d", r=2, tc=8, kvh=4),
          bd.dram["cc_out"].rearrange("(r tc kvh p) d -> p r tc kvh d", r=2, tc=8, kvh=4),
          reads=[_r(cc_out_b)], writes=[GS.r()])
    P.dma("dsp", zg.ap().rearrange("p (r c) -> p r c", r=2),
          bd.dram["cc_outz"].rearrange("(r p) c -> p r c", r=2),
          reads=[_r(cc_outz_b)], writes=[zg.r()])
    for c in range(8):
        sA = lambda i: Sacc[:, (i % 2) * 512:(i % 2) * 512 + 512]
        sAr = lambda i: [Sacc.r((i % 2) * 512, (i % 2) * 512 + 512)]
        zA = lambda i: zacc[:, (i % 2) * 4:(i % 2) * 4 + 4]
        zAr = lambda i: [zacc.r((i % 2) * 4, (i % 2) * 4 + 4)]
        P.op("dve", lambda e: e.memset(sA(0), 0.0), writes=sAr(0))
        P.op("dve", lambda e: e.memset(zA(0), 0.0), writes=zAr(0))
        for sl in range(16):
            P.op("dve", lambda e, c=c, sl=sl: e.scalar_tensor_tensor(
                    sA(sl + 1), GS[:, sl * 512:(sl + 1) * 512],
                    wpref[:, c * 16 + sl: c * 16 + sl + 1], sA(sl),
                    op0=ALU.mult, op1=ALU.add),
                 reads=[GS.r(sl * 512, (sl + 1) * 512), wpref.r()] + sAr(sl),
                 writes=sAr(sl + 1))
            r_, tc_ = sl // 8, sl % 8
            P.op("dve", lambda e, c=c, sl=sl, r_=r_, tc_=tc_: e.scalar_tensor_tensor(
                    zA(sl + 1), zg[:, r_ * 32 + tc_ * 4: r_ * 32 + tc_ * 4 + 4],
                    wpref[:, c * 16 + sl: c * 16 + sl + 1], zA(sl),
                    op0=ALU.mult, op1=ALU.add),
                 reads=[zg.r(), wpref.r()] + zAr(sl), writes=zAr(sl + 1))
        P.op("dve", lambda e, c=c: e.tensor_copy(Sprev[:, c * 512:(c + 1) * 512],
                                                 sA(16)),
             reads=sAr(16), writes=[Sprev.r(c * 512, (c + 1) * 512)])
        P.op("dve", lambda e, c=c: e.tensor_copy(zp[:, c * 4:(c + 1) * 4], zA(16)),
             reads=zAr(16), writes=[zp.r(c * 4, (c + 1) * 4)])
    A16.release()
    P.barrier()

    A16.mark()
    attnL1 = A16.alloc("attnL1", 128, 16 * TOWN)
    A16.mark()
    wq2 = [A16.alloc("wq20", 128, 16 * 128)]
    Qfh = A16.alloc("Qfh", 128, TOWN)
    ATb = A16.alloc("ATb", 128, TOWN)
    den = A32.alloc("denL1", 1, TOWN)
    bcl = A32.alloc("bcl", 128, TOWN)
    for h in range(H1):
        kvh = h // REP
        wb = wq2[0]
        P.dma("dpool", wb.ap().rearrange("p (k o) -> p k o", k=16),
              bd.dram["WqT"].rearrange("(k p) o -> p k o", p=128)[:, :, h * 128:(h + 1) * 128],
              writes=[wb.r()])
        for nt in range(2):
            for k in range(16):
                P.mm(M2[:, nt * 512:(nt + 1) * 512],
                     wb[:, k * 128:(k + 1) * 128],
                     Xb[:, k * TOWN + nt * 512: k * TOWN + (nt + 1) * 512],
                     start=(k == 0), stop=(k == 15),
                     reads=[wb.r(k * 128, (k + 1) * 128),
                            Xb.r(k * TOWN + nt * 512, k * TOWN + (nt + 1) * 512)],
                     writes=[_r(M2_b, nt * 512, (nt + 1) * 512)])
        P.op("act", lambda e: e.copy(kx.ap(), M2[:]), reads=[_r(M2_b)], writes=[kx.r()])
        for nt in range(2):
            P.mm(S5[:, nt * 512:(nt + 1) * 512], C["perm128"][:],
                 kx[:, nt * 512:(nt + 1) * 512], start=True, stop=True,
                 reads=[kx.r(nt * 512, (nt + 1) * 512)],
                 writes=[_r(S5_b, nt * 512, (nt + 1) * 512)])
        P.op("dve", lambda e: e.tensor_tensor(rtA.ap(), kx.ap(), c1f.ap(), op=ALU.mult),
             reads=[kx.r(), c1f.r()], writes=[rtA.r()])
        P.op("dve", lambda e: e.tensor_tensor(rtB.ap(), S5[:, 0:1024], s1f.ap(), op=ALU.mult),
             reads=[_r(S5_b, 0, 1024), s1f.r()], writes=[rtB.r()])
        P.op("dve", lambda e: e.tensor_tensor(rtA.ap(), rtA.ap(), rtB.ap(), op=ALU.add),
             reads=[rtA.r(), rtB.r()], writes=[rtA.r()])
        elu1(rtA.ap(), [rtA.r()], Qfh.ap(), [Qfh.r()],
             rtB.ap(), [rtB.r()], rtC.ap(), [rtC.r()])
        for c in range(8):
            P.mm(S5[:, c * 128:(c + 1) * 128],
                 KfF[:, kvh * TOWN + c * 128: kvh * TOWN + (c + 1) * 128],
                 Qfh[:, c * 128:(c + 1) * 128],
                 start=True, stop=True,
                 reads=[KfF.r(kvh * TOWN + c * 128, kvh * TOWN + (c + 1) * 128),
                        Qfh.r(c * 128, (c + 1) * 128)],
                 writes=[_r(S5_b, c * 128, (c + 1) * 128)])
        P.op("dve", lambda e: e.tensor_tensor(ATb.ap(), S5[:, 0:1024],
                                              C["trirep"][:], op=ALU.mult),
             reads=[_r(S5_b, 0, 1024)], writes=[ATb.r()])
        for c in range(8):
            P.mm(M2[:, c * 128:(c + 1) * 128],
                 VfT[:, c * 512 + kvh * 128: c * 512 + (kvh + 1) * 128],
                 ATb[:, c * 128:(c + 1) * 128], start=True, stop=False,
                 reads=[VfT.r(c * 512 + kvh * 128, c * 512 + (kvh + 1) * 128),
                        ATb.r(c * 128, (c + 1) * 128)],
                 writes=[_r(M2_b, c * 128, (c + 1) * 128)])
            P.mm(M2[:, c * 128:(c + 1) * 128],
                 Sprev[:, c * 512 + kvh * 128: c * 512 + (kvh + 1) * 128],
                 Qfh[:, c * 128:(c + 1) * 128],
                 start=False, stop=True,
                 reads=[Sprev.r(c * 512 + kvh * 128, c * 512 + (kvh + 1) * 128),
                        Qfh.r(c * 128, (c + 1) * 128)],
                 writes=[_r(M2_b, c * 128, (c + 1) * 128)])
            P.mm(S5[0:1, 1024 + c * 128: 1024 + (c + 1) * 128], C["ones_b"][:],
                 ATb[:, c * 128:(c + 1) * 128],
                 start=True, stop=False,
                 reads=[ATb.r(c * 128, (c + 1) * 128)],
                 writes=[_r(S5_b, 1024 + c * 128, 1024 + (c + 1) * 128)])
            P.mm(S5[0:1, 1024 + c * 128: 1024 + (c + 1) * 128],
                 zp[:, c * 4 + kvh: c * 4 + kvh + 1],
                 Qfh[:, c * 128:(c + 1) * 128],
                 start=False, stop=True,
                 reads=[zp.r(c * 4 + kvh, c * 4 + kvh + 1),
                        Qfh.r(c * 128, (c + 1) * 128)],
                 writes=[_r(S5_b, 1024 + c * 128, 1024 + (c + 1) * 128)])
        P.op("dve", lambda e: e.tensor_scalar(den.ap(), S5[0:1, 1024:2048],
                                              EPS, None, op0=ALU.add),
             reads=[_r(S5_b, 1024, 2048)], writes=[den.r()])
        P.op("dve", lambda e: e.reciprocal(den.ap(), den.ap()),
             reads=[den.r()], writes=[den.r()], fence=True)
        for nt in range(2):
            P.mm(S5[:, 1024 + nt * 512: 1024 + (nt + 1) * 512], C["onesrow_f"][:],
                 den[0:1, nt * 512:(nt + 1) * 512], start=True, stop=True,
                 reads=[den.r()], writes=[_r(S5_b, 1024 + nt * 512, 1024 + (nt + 1) * 512)])
        P.op("act", lambda e: e.copy(bcl.ap(), S5[:, 1024:2048]),
             reads=[_r(S5_b, 1024, 2048)], writes=[bcl.r()])
        P.op("dve", lambda e, h=h: e.tensor_tensor(
                attnL1[:, h * TOWN:(h + 1) * TOWN],
                M2[:, 0:1024], bcl.ap(), op=ALU.mult),
             reads=[_r(M2_b, 0, 1024), bcl.r()],
             writes=[attnL1.r(h * TOWN, (h + 1) * TOWN)])
    P.barrier()
    A16.release()
    A16.mark()
    wo2 = [A16.alloc("woL10", 128, 16 * 128), A16.alloc("woL11", 128, 16 * 128)]
    emit_matmul_residual(bd, "WoT", attnL1, wo2, X)
    A16.release()
    A16.release()
    A16.release(); A32.release()
    emit_postln_inplace(bd)


def emit_l1_ffn(bd):
    nc, P, C = bd.nc, bd.P, bd.C
    S5, S5_b = bd.ps["S5"]
    M2, M2_b = bd.ps["M2"]
    A16, A32 = bd.A16, bd.A32
    X, Xb = bd.X, bd.Xb

    A16.mark(); A32.mark()
    h4 = A16.alloc("h4", 128, 11 * 1024)
    wg2 = [A16.alloc("fg0", 128, 16 * 128), A16.alloc("fg1", 128, 16 * 128)]
    wu2 = [A16.alloc("fu0", 128, 16 * 128), A16.alloc("fu1", 128, 16 * 128)]
    wd2 = [A16.alloc("fd0", 128, 11 * 128), A16.alloc("fd1", 128, 11 * 128)]
    gsc = A32.alloc("gsc", 128, 44)
    usc = A32.alloc("usc", 128, 44)
    dsc = A32.alloc("dsc", 128, 16)
    sgl = A32.alloc("sgl", 128, 1024)
    P.dma("dsp", gsc.ap(), bd.dram["f_gs"], writes=[gsc.r()])
    P.dma("dsp", usc.ap(), bd.dram["f_us"], writes=[usc.r()])
    P.dma("dsp", dsc.ap(), bd.dram["f_ds"], writes=[dsc.r()])
    fgr = bd.dram["f_gateT"].rearrange("(k p) o -> p k o", p=128)
    fur = bd.dram["f_upT"].rearrange("(k p) o -> p k o", p=128)
    fdr = bd.dram["f_downT"].rearrange("(k p) o -> p k o", p=128)
    ROT = [(M2, M2_b, 0), (S5, S5_b, 0), (S5, S5_b, 1024)]
    for q in range(4):
        for j in range(11):
            jj = q * 11 + j
            P.dma("dpool", wg2[j % 2].ap().rearrange("p (k o) -> p k o", k=16),
                  fgr[:, :, jj * 128:(jj + 1) * 128], writes=[wg2[j % 2].r()])
            P.dma("dpool", wu2[j % 2].ap().rearrange("p (k o) -> p k o", k=16),
                  fur[:, :, jj * 128:(jj + 1) * 128], writes=[wu2[j % 2].r()])
            gp, gp_b, go = ROT[jj % 3]
            up, up_b, uo = ROT[(jj + 2) % 3]
            for nt in range(2):
                for k in range(16):
                    P.mm(gp[:, go + nt * 512: go + (nt + 1) * 512],
                         wg2[j % 2][:, k * 128:(k + 1) * 128],
                         Xb[:, k * TOWN + nt * 512: k * TOWN + (nt + 1) * 512],
                         start=(k == 0), stop=(k == 15),
                         reads=[wg2[j % 2].r(k * 128, (k + 1) * 128),
                                Xb.r(k * TOWN + nt * 512, k * TOWN + (nt + 1) * 512)],
                         writes=[_r(gp_b, go + nt * 512, go + (nt + 1) * 512)])
            for nt in range(2):
                for k in range(16):
                    P.mm(up[:, uo + nt * 512: uo + (nt + 1) * 512],
                         wu2[j % 2][:, k * 128:(k + 1) * 128],
                         Xb[:, k * TOWN + nt * 512: k * TOWN + (nt + 1) * 512],
                         start=(k == 0), stop=(k == 15),
                         reads=[wu2[j % 2].r(k * 128, (k + 1) * 128),
                                Xb.r(k * TOWN + nt * 512, k * TOWN + (nt + 1) * 512)],
                         writes=[_r(up_b, uo + nt * 512, uo + (nt + 1) * 512)])
            P.op("dve", lambda e, jj=jj, gp=gp, gp_b=gp_b, go=go: e.tensor_scalar(
                    sgl.ap(), gp[:, go:go + 1024], gsc[:, jj:jj + 1], None,
                    op0=ALU.mult),
                 reads=[_r(gp_b, go, go + 1024), gsc.r()], writes=[sgl.r()])
            P.op("act", lambda e: e.activation(sgl.ap(), sgl.ap(), AF.Silu),
                 reads=[sgl.r()], writes=[sgl.r()])
            P.op("dve", lambda e, j=j, jj=jj, up=up, up_b=up_b, uo=uo: e.scalar_tensor_tensor(
                    h4[:, j * 1024:(j + 1) * 1024], up[:, uo:uo + 1024],
                    usc[:, jj:jj + 1], sgl.ap(),
                    op0=ALU.mult, op1=ALU.mult),
                 reads=[_r(up_b, uo, uo + 1024), usc.r(), sgl.r()],
                 writes=[h4.r(j * 1024, (j + 1) * 1024)])
        for oc in range(16):
            P.dma("dpool", wd2[oc % 2].ap().rearrange("p (k o) -> p k o", k=11),
                  fdr[:, q * 11:(q + 1) * 11, oc * 128:(oc + 1) * 128],
                  writes=[wd2[oc % 2].r()])
            ps, ps_b, po = ROT[oc % 3]
            for nt in range(2):
                for k in range(11):
                    P.mm(ps[:, po + nt * 512: po + (nt + 1) * 512],
                         wd2[oc % 2][:, k * 128:(k + 1) * 128],
                         h4[:, k * 1024 + nt * 512: k * 1024 + (nt + 1) * 512],
                         start=(k == 0), stop=(k == 10),
                         reads=[wd2[oc % 2].r(k * 128, (k + 1) * 128),
                                h4.r(k * 1024 + nt * 512, k * 1024 + (nt + 1) * 512)],
                         writes=[_r(ps_b, po + nt * 512, po + (nt + 1) * 512)])
            P.op("dve", lambda e, oc=oc, ps=ps, ps_b=ps_b, po=po: e.scalar_tensor_tensor(
                    X[:, oc * TOWN:(oc + 1) * TOWN], ps[:, po:po + 1024],
                    dsc[:, oc:oc + 1], X[:, oc * TOWN:(oc + 1) * TOWN],
                    op0=ALU.mult, op1=ALU.add),
                 reads=[_r(ps_b, po, po + 1024), dsc.r(), X.r(oc * TOWN, (oc + 1) * TOWN)],
                 writes=[X.r(oc * TOWN, (oc + 1) * TOWN)])
    A16.release(); A32.release()
    P.barrier()
    emit_postln_inplace(bd)



SHARED_SPECS = {
    "wq_downT": (DM, QR), "wq_upT": (QR, DM), "wq_ropeT": (QR, H0 * ROPE0),
    "wkv_downT": (DM, KVR), "wkv_upKT": (KVR, H0 * NOPE), "wkv_upVT": (KVR, H0 * VDIM),
    "wk_ropeT": (DM, ROPE0), "woT": (DM, DM),
    "q_norm_w": (128, 8), "kv_norm_w": (128, 4),
    "gate_wT": (DM, NEXP), "w13T": (NEXP, DM, 2 * IMOE), "w2T": (NEXP, IMOE, DM),
    "sw13T": (DM, 2 * ISH), "sw2T": (ISH, DM),
    "WqT": (DM, DM), "WkT": (DM, KVH * HD), "WvT": (DM, KVH * HD), "WoT": (DM, DM),
    "f_gateT": (DM, IFF), "f_upT": (DM, IFF), "f_downT": (IFF, DM),
    "f_gs": (128, 44), "f_us": (128, 44), "f_ds": (128, 16),
    "c0tab": (64, SEQ), "s0tab": (64, SEQ), "c1tab": (128, SEQ), "s1tab": (128, SEQ),
    "perm64": (64, 64), "perm128": (128, 128),
    "tri128": (128, 128), "tri1024": (128, 1024), "ones128": (128, 1), "ones_row": (1, 128),
    "ident128": (128, 128),
    "fconsts": (128, len(FCONSTS)),
}
CORE_SPECS = {
    "xT": (DM, SEQ), "xT_own": (DM, TOWN), "l0mask": (4, 128, 2304),
    "c0_own": (64, TOWN), "s0_own": (64, TOWN),
    "c1_own": (128, TOWN), "s1_own": (128, TOWN), "l1prefw": (128, 128),
    "c1tok": (TOWN, 128), "s1tok": (TOWN, 128),
}


def make_builder(phases):
    nc = bass.Bass("TRN2", target_bir_lowering=False)
    P = Prog()
    stack = ExitStack()
    bd = Builder(nc, P, stack)
    for name, shape in SHARED_SPECS.items():
        bd.din(name, shape)
    for name, shape in CORE_SPECS.items():
        bd.din(name, shape)
    bd.ps = {
        "S5": bd.psum("S5", [128, 2560]),
        "M2": bd.psum("M2", [128, 1024]),
        "A1": bd.psum("A1", [128, 512]),
    }
    bd.A16 = Arena(bd, "AR16", AR16_COLS, BF16)
    bd.A32 = Arena(bd, "AR32", AR32_COLS, F32)
    bd.Xb = bd.A16.alloc("Xb", 128, 16 * TOWN)
    bd.X = bd.A32.alloc("X", 128, 16 * TOWN)
    return nc, P, stack, bd


def build_program(phases, stream_in=None, stream_out=True, dbg=False):
    nc, P, stack, bd = make_builder(phases)
    bd.dbg = bool(dbg)
    emit_consts(bd)
    if stream_in:
        xi = bd.din("x_in", (DM, TOWN))
        P.dma("dsp", bd.X.ap().rearrange("p (k t) -> p k t", k=16),
              xi.rearrange("(k p) t -> p k t", p=128), writes=[bd.X.r()])
        P.dma("dpool", bd.Xb.ap().rearrange("p (k t) -> p k t", k=16),
              xi.rearrange("(k p) t -> p k t", p=128), writes=[bd.Xb.r()])
        P.barrier()
    for ph in phases:
        if ph == "l0attn":
            emit_l0_attn(bd)
        elif ph == "moe":
            emit_l0_moe(bd)
        elif ph == "l1attn":
            emit_l1_attn(bd)
        elif ph == "l1ffn":
            emit_l1_ffn(bd)
    finals = []
    if stream_out:
        xo = bd.dout("x_out", (DM, TOWN))
        finals.append(P.dma("dsp", xo.rearrange("(k p) t -> p k t", p=128),
                            bd.X.ap().rearrange("p (k t) -> p k t", k=16),
                            reads=[bd.X.r()]))
    P.mark_final(finals)
    P.finalize(nc)
    stack.close()
    return nc



def _core_inmaps(inputs):
    shared = prep_shared(inputs)
    maps = []
    for core in range(8):
        d = dict(shared)
        d.update(prep_core(inputs, shared, core))
        maps.append(d)
    return maps, shared


def _own_cols(core):
    own = OWN[core % 2]
    return np.concatenate([np.arange(g * 128, (g + 1) * 128) for g in own])


def kernel(**inputs):
    maps, _ = _core_inmaps(inputs)
    phases = ["l0attn", "moe", "l1attn", "l1ffn"]
    nc = build_program(phases)
    res = run_bass_kernel_spmd(nc, maps, core_ids=list(range(8)))
    out = np.zeros((BSZ, SEQ, DM), dtype=np.float32)
    for core in range(8):
        xo = np.asarray(res.results[core]["x_out"])
        out[core // 2, _own_cols(core), :] = xo.T
    return out


# version 09
# speedup vs baseline: 1.0710x; 1.0710x over previous
import math
from contextlib import ExitStack

import numpy as np
import ml_dtypes

import concourse.bass as bass
import concourse.mybir as mybir
from concourse.bass_utils import run_bass_kernel_spmd

F32 = mybir.dt.float32
BF16 = mybir.dt.bfloat16
AF = mybir.ActivationFunctionType
ALU = mybir.AluOpType
AX = mybir.AxisListType

BSZ, SEQ, DM = 4, 2048, 2048
NCH = 16
TOWN = 1024
OWN = [[0, 15, 2, 13, 4, 11, 6, 9], [1, 14, 3, 12, 5, 10, 7, 8]]
PAIR_AB = [(1, 15), (3, 13), (5, 11), (7, 9)]
H0, NOPE, ROPE0, VDIM = 16, 128, 64, 128
QR, KVR = 1024, 512
SCALE0 = 1.0 / math.sqrt(NOPE + ROPE0)
H1, KVH, HD = 16, 4, 128
REP = H1 // KVH
NEXP, IMOE = 16, 1408
ISH = 2816
TOPK = 6
IFF = 5632

BF = ml_dtypes.bfloat16

FCONSTS = [0.0, 1.0, -1.0, 0.5, -0.5, 1e-6, 1e-5, 1e-20, 2.0,
           1.0 / QR, 1.0 / KVR, 1.0 / DM, 1.0 / 1024.0,
           math.log(1.0 / math.sqrt(NOPE + ROPE0)), math.log(1.01),
           1e-30, 1.0 / 16.0]



class Buf:

    __slots__ = ("name", "t", "size", "gran", "wr", "rd")

    def __init__(self, name, t, size, gran=1):
        self.name = name
        self.t = t
        self.size = size
        self.gran = gran
        self.wr = []
        self.rd = []

    def _rg(self, lo, hi):
        if self.gran > 1:
            lo = (lo // self.gran) * self.gran
            hi = ((hi + self.gran - 1) // self.gran) * self.gran
        return lo, hi


def _r(buf, lo=0, hi=None):
    return (buf, lo, buf.size if hi is None else hi)


class Prog:

    CSTREAMS = ("pe", "act", "dve", "pool")
    DSTREAMS = ("dsp", "dpool", "dact")
    ENG_OF = {"pe": "tensor", "act": "scalar", "dve": "vector", "pool": "gpsimd",
              "dsp": "sync", "dpool": "gpsimd", "dact": "scalar"}

    def __init__(self):
        self.ops = []
        self.pos = {s: [] for s in self.CSTREAMS + self.DSTREAMS}
        self.bufs = []
        self.bar = [-1]
        self.bar_ops = []
        self._last_by_stream = {s: -1 for s in self.CSTREAMS + self.DSTREAMS}

    def buf(self, name, t, size, gran=1):
        b = Buf(name, t, size, gran)
        self.bufs.append(b)
        return b

    def op(self, stream, fn, reads=(), writes=(), fence=False):
        idx = len(self.ops)
        deps = set()
        blast = self.bar[-1]
        for (b, lo, hi) in reads:
            lo, hi = b._rg(lo, hi)
            per = {}
            for (wl, wh, wi) in reversed(b.wr):
                if wi <= blast:
                    break
                if wl < hi and lo < wh:
                    s = self.ops[wi]["stream"]
                    if s not in per:
                        per[s] = wi
            deps.update(per.values())
        for (b, lo, hi) in writes:
            lo, hi = b._rg(lo, hi)
            per = {}
            for (wl, wh, wi) in reversed(b.wr):
                if wi <= blast:
                    break
                if wl < hi and lo < wh:
                    s = self.ops[wi]["stream"]
                    if s not in per:
                        per[s] = wi
            deps.update(per.values())
            per = {}
            for (rl, rh, ri) in reversed(b.rd):
                if ri <= blast:
                    break
                if rl < hi and lo < rh:
                    s = self.ops[ri]["stream"]
                    if s not in per or per[s] < ri:
                        per[s] = ri
            deps.update(per.values())
        for (b, lo, hi) in reads:
            lo, hi = b._rg(lo, hi)
            b.rd.append((lo, hi, idx))
        for (b, lo, hi) in writes:
            lo, hi = b._rg(lo, hi)
            b.wr.append((lo, hi, idx))
        if self.bar_ops:
            for j in self.bar_ops[-1]:
                if j >= 0:
                    deps.add(j)
        deps = {d for d in deps
                if self.ops[d]["stream"] != stream or self.ops[d]["fence"]}
        self.ops.append({"stream": stream, "fn": fn, "deps": deps, "fence": fence})
        self.pos[stream].append(idx)
        self._last_by_stream[stream] = idx
        return idx

    def barrier(self):
        self.bar.append(len(self.ops) - 1)
        self.bar_ops.append([self._last_by_stream[s] for s in self.CSTREAMS + self.DSTREAMS])
        for b in self.bufs:
            if len(b.wr) > 64 or len(b.rd) > 64:
                blast = self.bar[-1]
                b.wr = [e for e in b.wr if e[2] > blast] or b.wr[-1:]
                b.rd = [e for e in b.rd if e[2] > blast]

    def finalize(self, nc, final_stream="dsp"):
        ops = self.ops
        flagged = [False] * len(ops)
        for i, o in enumerate(ops):
            if o["stream"] in self.DSTREAMS:
                flagged[i] = True
            for d in o["deps"]:
                flagged[d] = True
        for i in self._final:
            flagged[i] = True
        cnt = {s: 0 for s in self.pos}
        thr = [0] * len(ops)
        for i, o in enumerate(ops):
            s = o["stream"]
            if flagged[i]:
                cnt[s] += 1
                thr[i] = cnt[s] * (16 if s in self.DSTREAMS else 1)
        sems = {}
        stack = ExitStack()
        for s in self.pos:
            if cnt[s]:
                sems[s] = stack.enter_context(nc.semaphore(f"sem_{s}"))
        blk = stack.enter_context(nc.Block())
        know = {s: {} for s in self.pos}
        snap = [None] * len(ops)
        emit = {s: [] for s in self.pos}
        cur = {s: 0 for s in self.pos}
        for i, o in enumerate(ops):
            s = o["stream"]
            k = know[s]
            for d in sorted(o["deps"]):
                ds = ops[d]["stream"]
                if ds == s and not ops[d]["fence"]:
                    continue
                v = thr[d]
                assert v > 0, f"dep {d} not flagged"
                if k.get(ds, 0) < v:
                    emit[s].append(("wait", ds, v))
                    k[ds] = max(k.get(ds, 0), v)
                    sn = snap[d]
                    if sn:
                        for s2, v2 in sn.items():
                            if k.get(s2, 0) < v2:
                                k[s2] = v2
            emit[s].append(("op", i))
            if flagged[i]:
                sn = dict(k)
                sn[s] = thr[i]
                snap[i] = sn
        fin = {}
        for i in self._final:
            s = ops[i]["stream"]
            fin[s] = max(fin.get(s, 0), thr[i])
        for s, v in fin.items():
            if know[final_stream].get(s, 0) < v and not (s == final_stream):
                emit[final_stream].append(("wait", s, v))
            elif s == final_stream:
                emit[final_stream].append(("wait", s, v))

        prog = self

        def run_stream(eng, s):
            for item in emit[s]:
                if item[0] == "wait":
                    eng.wait_ge(sems[item[1]], item[2])
                else:
                    i = item[1]
                    inst = ops[i]["fn"](eng)
                    if flagged[i]:
                        inst.then_inc(sems[s], 16 if s in self.DSTREAMS else 1)

        def merged(s_compute, s_dma):
            items = []
            ic, idma = 0, 0
            ec, ed = emit[s_compute], emit[s_dma]
            def chunks(lst):
                out = []
                pend = []
                for it in lst:
                    if it[0] == "wait":
                        pend.append(it)
                    else:
                        out.append((it[1], pend + [it]))
                        pend = []
                if pend:
                    out.append((10 ** 9, pend))
                return out
            cc, cd = chunks(ec), chunks(ed)
            res = []
            i = j = 0
            while i < len(cc) or j < len(cd):
                if j >= len(cd) or (i < len(cc) and cc[i][0] < cd[j][0]):
                    res.extend((s_compute, it) for it in cc[i][1]); i += 1
                else:
                    res.extend((s_dma, it) for it in cd[j][1]); j += 1
            return res

        pool_items = merged("pool", "dpool")
        act_items = merged("act", "dact")

        @blk.tensor
        def _(t):
            run_stream(t, "pe")

        @blk.vector
        def _(v):
            run_stream(v, "dve")

        @blk.scalar
        def _(sc):
            for s, item in act_items:
                if item[0] == "wait":
                    sc.wait_ge(sems[item[1]], item[2])
                else:
                    i = item[1]
                    inst = ops[i]["fn"](sc)
                    if flagged[i]:
                        inst.then_inc(sems[s], 16 if s in self.DSTREAMS else 1)

        @blk.gpsimd
        def _(g):
            for s, item in pool_items:
                if item[0] == "wait":
                    g.wait_ge(sems[item[1]], item[2])
                else:
                    i = item[1]
                    inst = ops[i]["fn"](g)
                    if flagged[i]:
                        inst.then_inc(sems[s], 16 if s in self.DSTREAMS else 1)

        @blk.sync
        def _(sy):
            run_stream(sy, "dsp")

        stack.close()

    _final = ()
    _extra_final = None

    def mark_extra_final(self, idx):
        if self._extra_final is None:
            self._extra_final = []
        self._extra_final.append(idx)

    def mark_final(self, idxs):
        self._final = list(idxs) + (self._extra_final or [])

    def dma(self, q, out_ap, in_ap, reads=(), writes=()):
        return self.op(q, lambda e: e.dma_start(out=out_ap, in_=in_ap),
                       reads=reads, writes=writes)

    def mm(self, out_ap, lhsT_ap, rhs_ap, start, stop, reads=(), writes=()):
        return self.op("pe", lambda e: e.matmul(out_ap, lhsT_ap, rhs_ap,
                                                start=start, stop=stop),
                       reads=reads, writes=writes)



def _t(a):
    return np.ascontiguousarray(np.asarray(a).T)


def _rope_perm(n):
    return np.concatenate([np.arange(0, n, 2), np.arange(1, n, 2)])


def prep_shared(inp):
    d = {}
    d["wq_downT"] = _t(inp["L0_attn_wq_down"])
    d["wq_upT"] = _t(inp["L0_attn_wq_up"])
    wqr = np.asarray(inp["L0_attn_wq_rope"]).reshape(H0, ROPE0, QR)
    perm = _rope_perm(ROPE0)
    wqr = wqr[:, perm, :].reshape(H0 * ROPE0, QR)
    d["wq_ropeT"] = _t(wqr)
    d["wkv_downT"] = _t(inp["L0_attn_wkv_down"])
    wkv = np.asarray(inp["L0_attn_wkv_up"]).reshape(H0, NOPE + VDIM, KVR)
    d["wkv_upKT"] = _t(wkv[:, :NOPE, :].reshape(H0 * NOPE, KVR))
    d["wkv_upVT"] = _t(wkv[:, NOPE:, :].reshape(H0 * VDIM, KVR))
    wkr = np.asarray(inp["L0_attn_wk_rope"])[perm, :]
    d["wk_ropeT"] = _t(wkr)
    d["woT"] = _t(inp["L0_attn_wo"])
    d["q_norm_w"] = np.ascontiguousarray(
        np.asarray(inp["L0_attn_q_norm_w"]).reshape(QR // 128, 128).T)
    d["kv_norm_w"] = np.ascontiguousarray(
        np.asarray(inp["L0_attn_kv_norm_w"]).reshape(KVR // 128, 128).T)
    d["gate_wT"] = _t(inp["L0_ffn_gate_w"])
    d["w13T"] = np.ascontiguousarray(np.asarray(inp["L0_ffn_w13"]).transpose(0, 2, 1))
    d["w2T"] = np.ascontiguousarray(np.asarray(inp["L0_ffn_w2"]).transpose(0, 2, 1))
    d["sw13T"] = _t(inp["L0_ffn_sw13"])
    d["sw2T"] = _t(inp["L0_ffn_sw2"])
    d["WqT"] = _t(inp["L1_attn_Wq"])
    d["WkT"] = _t(inp["L1_attn_Wk"])
    d["WvT"] = _t(inp["L1_attn_Wv"])
    d["WoT"] = _t(inp["L1_attn_Wo"])
    d["f_gateT"] = _t(inp["L1_ffn_gate_w"])
    d["f_upT"] = _t(inp["L1_ffn_up_w"])
    d["f_downT"] = _t(inp["L1_ffn_down_w"])
    d["f_gs"] = np.ascontiguousarray(
        np.asarray(inp["L1_ffn_gate_s"]).reshape(IFF // 128, 128).T)
    d["f_us"] = np.ascontiguousarray(
        np.asarray(inp["L1_ffn_up_s"]).reshape(IFF // 128, 128).T)
    d["f_ds"] = np.ascontiguousarray(
        np.asarray(inp["L1_ffn_down_s"]).reshape(DM // 128, 128).T)
    invf0 = 1.0 / (10000.0 ** (np.arange(0, ROPE0, 2, dtype=np.float64) / ROPE0))
    tpos = np.arange(SEQ, dtype=np.float64)
    fr0 = np.outer(invf0, tpos)
    c0 = np.cos(fr0).astype(np.float32)
    s0 = np.sin(fr0).astype(np.float32)
    d["c0tab"] = np.ascontiguousarray(np.concatenate([c0, c0], 0))
    d["s0tab"] = np.ascontiguousarray(np.concatenate([-s0, s0], 0))
    d["perm64"] = np.ascontiguousarray(
        np.block([[np.zeros((32, 32)), np.eye(32)],
                  [np.eye(32), np.zeros((32, 32))]]).astype(np.float32))
    invf1 = 1.0 / (10000.0 ** (np.arange(0, HD, 2, dtype=np.float64) / HD))
    fr1 = np.outer(invf1, tpos)
    c1 = np.cos(fr1).astype(np.float32)
    s1 = np.sin(fr1).astype(np.float32)
    d["c1tab"] = np.ascontiguousarray(np.concatenate([c1, c1], 0))
    d["s1tab"] = np.ascontiguousarray(np.concatenate([s1, s1], 0))
    eye64 = np.eye(64)
    z64 = np.zeros((64, 64))
    P1 = np.block([[z64, -eye64], [eye64, z64]])
    d["perm128"] = np.ascontiguousarray(P1.T.astype(np.float32))
    tri = (np.arange(128)[:, None] <= np.arange(128)[None, :]).astype(np.float32)
    d["tri128"] = np.ascontiguousarray(tri)
    d["tri1024"] = np.ascontiguousarray(np.tile(tri, (1, 8)))
    d["ones128"] = np.ones((128, 1), dtype=np.float32)
    d["ones_row"] = np.ones((1, 128), dtype=np.float32)
    d["ident128"] = np.eye(128, dtype=np.float32)
    d["fconsts"] = np.ascontiguousarray(
        np.tile(np.asarray(FCONSTS, dtype=np.float32).reshape(1, -1), (128, 1)))
    return d


def prep_core(inp, shared, core):
    par = core % 2
    b = core // 2
    own = OWN[par]
    x = np.asarray(inp["L0_attn_x"])[b]
    xT = _t(x)
    d = {}
    d["xT"] = xT
    xT_own = np.concatenate([xT[:, g * 128:(g + 1) * 128] for g in own], axis=1)
    d["xT_own"] = np.ascontiguousarray(xT_own)
    masks = []
    for p, (A, Bc) in enumerate(PAIR_AB):
        a_g, b_g = own[2 * p], own[2 * p + 1]
        cols = []
        for kc in range(Bc + 1):
            w = 256 if kc <= A else 128
            m = np.zeros((128, w), dtype=np.float32)
            kpos = kc * 128 + np.arange(128)
            if w == 256:
                qa = a_g * 128 + np.arange(128)
                qb = b_g * 128 + np.arange(128)
                m[:, :128] = (kpos[:, None] <= qa[None, :])
                m[:, 128:] = (kpos[:, None] <= qb[None, :])
            else:
                qb = b_g * 128 + np.arange(128)
                m[:, :] = (kpos[:, None] <= qb[None, :])
            cols.append(m)
        masks.append(np.concatenate(cols, axis=1))
    d["l0mask"] = np.ascontiguousarray(np.stack(masks, 0))
    ocols = np.concatenate([np.arange(g * 128, (g + 1) * 128) for g in own])
    d["c0_own"] = np.ascontiguousarray(shared["c0tab"][:, ocols])
    d["s0_own"] = np.ascontiguousarray(shared["s0tab"][:, ocols])
    d["c1_own"] = np.ascontiguousarray(shared["c1tab"][:, ocols])
    d["s1_own"] = np.ascontiguousarray(shared["s1tab"][:, ocols])
    d["c1tok"] = np.ascontiguousarray(shared["c1tab"][:, ocols].T)
    d["s1tok"] = np.ascontiguousarray(shared["s1tab"][:, ocols].T)
    slot_global = [OWN[0][i] for i in range(8)] + [OWN[1][i] for i in range(8)]
    wmat = np.zeros((128, 8 * 16), dtype=np.float32)
    for c in range(8):
        gc = own[c]
        for sl in range(16):
            if slot_global[sl] < gc:
                wmat[:, c * 16 + sl] = 1.0
    d["l1prefw"] = np.ascontiguousarray(wmat)
    return d



class Builder:
    def __init__(self, nc, P, stack):
        self.nc = nc
        self.P = P
        self.stack = stack
        self.dram = {}
        self.sb = {}

    def din(self, name, shape, dtype=F32):
        t = self.nc.dram_tensor(name, list(shape), dtype, kind="ExternalInput")
        self.dram[name] = t.ap()
        return self.dram[name]

    def dout(self, name, shape, dtype=F32):
        t = self.nc.dram_tensor(name, list(shape), dtype, kind="ExternalOutput")
        self.dram[name] = t.ap()
        return self.dram[name]

    def dint(self, name, shape, dtype=F32):
        t = self.nc.dram_tensor(name, list(shape), dtype, kind="Internal")
        self.dram[name] = t.ap()
        return self.dram[name]

    def sbuf(self, name, shape, dtype=F32):
        t = self.stack.enter_context(self.nc.sbuf_tensor(name, list(shape), dtype))
        b = self.P.buf(name, t, shape[-1] if len(shape) == 2 else int(np.prod(shape[1:])))
        self.sb[name] = (t, b)
        return t, b

    def psum(self, name, shape, dtype=F32):
        t = self.stack.enter_context(self.nc.psum_tensor(name, list(shape), dtype))
        b = self.P.buf(name, t, shape[-1], gran=512)
        self.sb[name] = (t, b)
        return t, b



AR16_COLS = 54272
AR32_COLS = 23104


class _ArView:

    __slots__ = ("t", "off", "part", "cols", "b")

    def __init__(self, t, off, part, cols, b):
        self.t = t
        self.off = off
        self.part = part
        self.cols = cols
        self.b = b

    def __getitem__(self, idx):
        p, c = idx
        cs = c.start or 0
        ce = c.stop if c.stop is not None else self.cols
        assert ce <= self.cols
        return self.t[p, self.off + cs: self.off + ce]

    def ap(self):
        return self.t[0:self.part, self.off:self.off + self.cols]

    def r(self, lo=0, hi=None):
        return (self.b, lo, self.cols if hi is None else hi)


class Arena:
    def __init__(self, bd, name, cols, dtype):
        self.bd = bd
        self.name = name
        self.t = bd.stack.enter_context(bd.nc.sbuf_tensor(name, [128, cols], dtype))
        self.cols = cols
        self.off = 0
        self.marks = []

    def alloc(self, name, part, cols):
        assert self.off + cols <= self.cols, \
            f"arena {self.name}/{name}: {self.off}+{cols} > {self.cols}"
        v = self.view_at(name, self.off, part, cols)
        self.off += cols
        return v

    def view_at(self, name, off, part, cols):
        b = self.bd.P.buf(f"{self.name}.{name}", self.t, cols)
        return _ArView(self.t, off, part, cols, b)

    def mark(self):
        self.marks.append(self.off)

    def release(self):
        self.off = self.marks.pop()



def _seg_layout(pair):
    A, Bc = PAIR_AB[pair]
    segs = []
    off = 0
    for kc in range(Bc + 1):
        w = 256 if kc <= A else 128
        segs.append((kc, w, off))
        off += w
    assert off == 2304
    return segs


def emit_consts(bd):
    P = bd.P
    c = {}

    def ld(name, shape, dt, src_ap):
        t, b = bd.sbuf("c_" + name, shape, dt)
        qq = "dpool" if dt == BF16 else "dsp"
        P.dma(qq, t[:], src_ap, writes=[_r(b)])
        c[name] = t

    ld("ones_b", [128, 1], BF16, bd.dram["ones128"])
    ld("ones_f", [128, 1], F32, bd.dram["ones128"])
    ld("onesrow_f", [1, 128], F32, bd.dram["ones_row"])
    ld("tri", [128, 128], BF16, bd.dram["tri128"])
    ld("perm64", [64, 64], BF16, bd.dram["perm64"])
    ld("perm128", [128, 128], BF16, bd.dram["perm128"])
    ld("c0_own", [64, TOWN], BF16, bd.dram["c0_own"])
    ld("s0_own", [64, TOWN], BF16, bd.dram["s0_own"])
    ld("ident_b", [128, 128], BF16, bd.dram["ident128"])
    ld("trirep", [128, 1024], BF16, bd.dram["tri1024"])
    ld("ident_f", [128, 128], F32, bd.dram["ident128"])
    ld("onesrow_b", [1, 128], BF16, bd.dram["ones_row"])
    ld("qnw", [128, 8], F32, bd.dram["q_norm_w"])
    ld("kvnw", [128, 4], F32, bd.dram["kv_norm_w"])
    fv, fv_b = bd.sbuf("c_fvals", [128, len(FCONSTS)], F32)
    P.dma("dsp", fv[:], bd.dram["fconsts"], writes=[_r(fv_b)])
    for i, v in enumerate(FCONSTS):
        bd.nc.const_aps.register(F32, v, fv[:, i:i + 1])
        bd.nc.const_aps.register(F32, float(np.float32(v)), fv[:, i:i + 1])
    bd.C = c
    P.barrier()


def _postln(bd, resid_from_psum_fn, n_oc=16):
    pass


def emit_l0_attn(bd):
    nc, P, C = bd.nc, bd.P, bd.C
    S5, S5_b = bd.ps["S5"]
    M2, M2_b = bd.ps["M2"]
    A1, A1_b = bd.ps["A1"]
    A16, A32 = bd.A16, bd.A32
    X = bd.X
    Xb = bd.Xb

    A16.mark(); A32.mark()
    qcn = A16.alloc("qcn", 128, 8 * TOWN)
    kvn = A16.alloc("kvn", 128, 4 * SEQ)
    kra = A16.alloc("kra", 65, SEQ)
    krss = A32.alloc("krss", 1, SEQ)
    attnout = A16.view_at("attnout", Xb.off, 128, 16 * TOWN)
    xbuf = A16.view_at("xbuf", Xb.off, 128, 16 * TOWN)

    A16.mark(); A32.mark()
    wdbl = [A16.alloc("wdbl0", 128, 16 * 128), A16.alloc("wdbl1", 128, 16 * 128)]
    pre = A16.alloc("pre", 128, 8 * 1024)
    c0f = A16.alloc("c0f", 64, SEQ)
    s0f = A16.alloc("s0f", 64, SEQ)
    krx = A16.alloc("krx", 64, TOWN)
    sq = A32.alloc("sqB", 128, 1024)
    rt1 = A32.alloc("rt1", 64, TOWN)
    rt2 = A32.alloc("rt2", 64, TOWN)
    rrow = A32.alloc("rrowB", 1, TOWN)

    P.dma("dpool", c0f.ap(), bd.dram["c0tab"], writes=[c0f.r()])
    P.dma("dpool", s0f.ap(), bd.dram["s0tab"], writes=[s0f.r()])
    P.dma("dpool", xbuf.ap().rearrange("p (k t) -> p k t", k=16),
          bd.dram["xT_own"].rearrange("(k p) t -> p k t", p=128),
          writes=[xbuf.r()])

    def proj_norm(wT_name, nout, rank, norm_w, ln_bias, out, out_T, hf):
        for j in range(nout):
            wb = wdbl[j % 2]
            P.dma("dpool", wb.ap().rearrange("p (k o) -> p k o", k=16),
                  bd.dram[wT_name].rearrange("(k p) o -> p k o", p=128)[:, :, j * 128:(j + 1) * 128],
                  writes=[wb.r()])
            for nt in range(2):
                for k in range(16):
                    P.mm(M2[:, nt * 512:(nt + 1) * 512],
                         wb[:, k * 128:(k + 1) * 128],
                         xbuf[:, k * TOWN + nt * 512: k * TOWN + (nt + 1) * 512],
                         start=(k == 0), stop=(k == 15),
                         reads=[wb.r(k * 128, (k + 1) * 128),
                                xbuf.r(k * TOWN + nt * 512, k * TOWN + (nt + 1) * 512)],
                         writes=[_r(M2_b, nt * 512, (nt + 1) * 512)])
            P.op("act", lambda e, j=j: e.copy(pre[:, j * 1024:(j + 1) * 1024], M2[:]),
                 reads=[_r(M2_b)], writes=[pre.r(j * 1024, (j + 1) * 1024)])
            P.op("dve", lambda e, j=j: e.tensor_tensor(
                    sq.ap(), pre[:, j * 1024:(j + 1) * 1024],
                    pre[:, j * 1024:(j + 1) * 1024], op=ALU.mult),
                 reads=[pre.r(j * 1024, (j + 1) * 1024)], writes=[sq.r()])
            for nt in range(2):
                P.mm(S5[0:1, nt * 512:(nt + 1) * 512], C["ones_f"][:],
                     sq[:, nt * 512:(nt + 1) * 512],
                     start=(j == 0), stop=(j == nout - 1),
                     reads=[sq.r(nt * 512, (nt + 1) * 512)],
                     writes=[_r(S5_b, nt * 512, (nt + 1) * 512)])
        P.op("act", lambda e: e.activation(rrow.ap(), S5[0:1, 0:1024], AF.Ln,
                                           scale=1.0 / rank, bias=1e-6),
             reads=[_r(S5_b, 0, 1024)], writes=[rrow.r()])
        P.op("act", lambda e: e.activation(rrow.ap(), rrow.ap(), AF.Exp,
                                           scale=-0.5, bias=ln_bias),
             reads=[rrow.r()], writes=[rrow.r()])
        for nt in range(2):
            P.mm(M2[:, nt * 512:(nt + 1) * 512], C["onesrow_f"][:],
                 rrow[0:1, nt * 512:(nt + 1) * 512], start=True, stop=True,
                 reads=[rrow.r()], writes=[_r(M2_b, nt * 512, (nt + 1) * 512)])
        for j in range(nout):
            P.op("dve", lambda e, j=j: e.scalar_tensor_tensor(
                    out[:, j * out_T + hf * 1024: j * out_T + hf * 1024 + 1024],
                    pre[:, j * 1024:(j + 1) * 1024],
                    norm_w[:, j:j + 1], M2[:],
                    op0=ALU.mult, op1=ALU.mult),
                 reads=[pre.r(j * 1024, (j + 1) * 1024), _r(M2_b)],
                 writes=[out.r(j * out_T + hf * 1024, j * out_T + hf * 1024 + 1024)])
        P.barrier()

    proj_norm("wq_downT", 8, QR, C["qnw"], math.log(SCALE0), qcn, TOWN, 0)

    for hf in range(2):
        P.dma("dpool", xbuf.ap().rearrange("p (k t) -> p k t", k=16),
              bd.dram["xT"].rearrange("(k p) t -> p k t", p=128)[:, :, hf * 1024:(hf + 1) * 1024],
              writes=[xbuf.r()])
        proj_norm("wkv_downT", 4, KVR, C["kvnw"], 0.0, kvn, SEQ, hf)
        wb = wdbl[0]
        P.dma("dpool", wb.ap().rearrange("p (k o) -> p k o", k=16)[:, :, 0:64],
              bd.dram["wk_ropeT"].rearrange("(k p) o -> p k o", p=128),
              writes=[wb.r()])
        for nt in range(2):
            for k in range(16):
                P.mm(M2[0:64, nt * 512:(nt + 1) * 512],
                     wb[:, k * 128:k * 128 + 64],
                     xbuf[:, k * TOWN + nt * 512: k * TOWN + (nt + 1) * 512],
                     start=(k == 0), stop=(k == 15),
                     reads=[wb.r(k * 128, k * 128 + 64),
                            xbuf.r(k * TOWN + nt * 512, k * TOWN + (nt + 1) * 512)],
                     writes=[_r(M2_b, nt * 512, (nt + 1) * 512)])
        P.op("act", lambda e: e.copy(krx.ap(), M2[0:64, 0:1024]),
             reads=[_r(M2_b)], writes=[krx.r()])
        for nt in range(2):
            P.mm(S5[0:64, nt * 512:(nt + 1) * 512], C["perm64"][:],
                 krx[0:64, nt * 512:(nt + 1) * 512], start=True, stop=True,
                 reads=[krx.r(nt * 512, (nt + 1) * 512)],
                 writes=[_r(S5_b, nt * 512, (nt + 1) * 512)])
        P.op("dve", lambda e, hf=hf: e.tensor_tensor(
                rt1.ap(), krx.ap(), c0f[0:64, hf * 1024:(hf + 1) * 1024], op=ALU.mult),
             reads=[krx.r(), c0f.r()], writes=[rt1.r()])
        P.op("dve", lambda e, hf=hf: e.tensor_tensor(
                rt2.ap(), S5[0:64, 0:1024], s0f[0:64, hf * 1024:(hf + 1) * 1024],
                op=ALU.mult),
             reads=[_r(S5_b, 0, 1024), s0f.r()], writes=[rt2.r()])
        P.op("dve", lambda e, hf=hf: e.tensor_tensor(
                kra[0:64, hf * 1024:(hf + 1) * 1024], rt1.ap(), rt2.ap(), op=ALU.add),
             reads=[rt1.r(), rt2.r()],
             writes=[kra.r(hf * 1024, (hf + 1) * 1024)])
        P.op("dve", lambda e, hf=hf: e.tensor_tensor(
                rt1.ap(), kra[0:64, hf * 1024:(hf + 1) * 1024],
                kra[0:64, hf * 1024:(hf + 1) * 1024], op=ALU.mult),
             reads=[kra.r(hf * 1024, (hf + 1) * 1024)], writes=[rt1.r()])
        for nt in range(2):
            P.mm(M2[0:1, nt * 512:(nt + 1) * 512], C["ones_f"][0:64, :],
                 rt1[0:64, nt * 512:(nt + 1) * 512], start=True, stop=True,
                 reads=[rt1.r(nt * 512, (nt + 1) * 512)],
                 writes=[_r(M2_b, nt * 512, (nt + 1) * 512)])
        P.op("dve", lambda e, hf=hf: e.tensor_copy(
                krss[0:1, hf * 1024:(hf + 1) * 1024], M2[0:1, 0:1024]),
             reads=[_r(M2_b, 0, 1024)], writes=[krss.r(hf * 1024, (hf + 1) * 1024)])
        P.barrier()
    P.op("dve", lambda e: e.memset(kra[64:65, 0:SEQ], -1.0), writes=[kra.r()])
    A16.release(); A32.release()
    P.barrier()

    A16.mark(); A32.mark()
    NG = 16
    v_g = A16.alloc("v_g", 128, 16 * 128)
    kh_g = A16.alloc("kh_g", 128, SEQ)
    qn_g = A16.alloc("qn_g", 128, TOWN)
    qa_g = A16.alloc("qa_g", 65, TOWN)
    qx = A16.alloc("qx", 64, TOWN)
    PT = A16.alloc("PT", 128, 2304)
    msk = A16.alloc("msk", 128, 2304)
    wqu2 = [A16.alloc("wquA", 128, 8 * 128), A16.alloc("wquB", 128, 8 * 128)]
    wqr2 = [A16.alloc("wqrA", 128, 8 * 64), A16.alloc("wqrB", 128, 8 * 64)]
    wkK2 = [A16.alloc("wkKA", 128, 4 * 128), A16.alloc("wkKB", 128, 4 * 128)]
    wkV2 = [A16.alloc("wkVA", 128, 4 * 128), A16.alloc("wkVB", 128, 4 * 128)]
    sqc = A32.alloc("sqC", 128, 1024)
    rt1c = A32.alloc("rt1C", 64, TOWN)
    rt2c = A32.alloc("rt2C", 64, TOWN)
    cqr = A32.alloc("cqr", 1, TOWN)
    mrow = A32.alloc("mrow", 1, 8)
    bcsb = A32.alloc("bcsb", 128, 256)
    rcp = A32.alloc("rcp", 1, 256)
    cqb_t, cqb_b = bd.sbuf("cqb", [1, TOWN], BF16)

    def load_head_weights(h):
        wqu, wqr, wkK, wkV = wqu2[h % 2], wqr2[h % 2], wkK2[h % 2], wkV2[h % 2]
        P.dma("dpool", wqu.ap().rearrange("p (k o) -> p k o", k=8),
              bd.dram["wq_upT"].rearrange("(k p) o -> p k o", p=128)[:, :, h * 128:(h + 1) * 128],
              writes=[wqu.r()])
        P.dma("dpool", wqr.ap().rearrange("p (k o) -> p k o", k=8),
              bd.dram["wq_ropeT"].rearrange("(k p) o -> p k o", p=128)[:, :, h * 64:(h + 1) * 64],
              writes=[wqr.r()])
        P.dma("dpool", wkK.ap().rearrange("p (k o) -> p k o", k=4),
              bd.dram["wkv_upKT"].rearrange("(k p) o -> p k o", p=128)[:, :, h * 128:(h + 1) * 128],
              writes=[wkK.r()])
        P.dma("dpool", wkV.ap().rearrange("p (k o) -> p k o", k=4),
              bd.dram["wkv_upVT"].rearrange("(k p) o -> p k o", p=128)[:, :, h * 128:(h + 1) * 128],
              writes=[wkV.r()])

    load_head_weights(0)
    for g in range(NG):
        h = g
        wqu, wqr, wkK, wkV = wqu2[h % 2], wqr2[h % 2], wkK2[h % 2], wkV2[h % 2]
        if h + 1 < NG:
            load_head_weights(h + 1)
        for tc in range(16):
            for kch in range(4):
                P.mm(M2[:, 0:128],
                     kvn[:, kch * SEQ + tc * 128: kch * SEQ + (tc + 1) * 128],
                     wkV[:, kch * 128:(kch + 1) * 128],
                     start=(kch == 0), stop=(kch == 3),
                     reads=[kvn.r(kch * SEQ + tc * 128, kch * SEQ + (tc + 1) * 128),
                            wkV.r(kch * 128, (kch + 1) * 128)],
                     writes=[_r(M2_b, 0, 128)])
            P.op("act", lambda e, tc=tc: e.copy(v_g[:, tc * 128:(tc + 1) * 128],
                                                M2[:, 0:128]),
                 reads=[_r(M2_b, 0, 128)], writes=[v_g.r(tc * 128, (tc + 1) * 128)])
        for nt in range(4):
            for kch in range(4):
                P.mm(S5[:, nt * 512:(nt + 1) * 512],
                     wkK[:, kch * 128:(kch + 1) * 128],
                     kvn[:, kch * SEQ + nt * 512: kch * SEQ + (nt + 1) * 512],
                     start=(kch == 0), stop=(kch == 3),
                     reads=[wkK.r(kch * 128, (kch + 1) * 128),
                            kvn.r(kch * SEQ + nt * 512, kch * SEQ + (nt + 1) * 512)],
                     writes=[_r(S5_b, nt * 512, (nt + 1) * 512)])
        P.op("act", lambda e: e.copy(kh_g.ap(), S5[:, 0:SEQ]),
             reads=[_r(S5_b, 0, SEQ)], writes=[kh_g.r()])
        for half in range(2):
            P.op("dve", lambda e, half=half: e.tensor_tensor(
                    sqc.ap(), kh_g[:, half * 1024: half * 1024 + 1024],
                    kh_g[:, half * 1024: half * 1024 + 1024], op=ALU.mult),
                 reads=[kh_g.r(half * 1024, half * 1024 + 1024)],
                 writes=[sqc.r()])
            for nt in range(2):
                P.mm(S5[0:1, half * 1024 + nt * 512: half * 1024 + (nt + 1) * 512],
                     C["ones_f"][:], sqc[:, nt * 512:(nt + 1) * 512],
                     start=True, stop=True,
                     reads=[sqc.r(nt * 512, (nt + 1) * 512)],
                     writes=[_r(S5_b, half * 1024 + nt * 512, half * 1024 + (nt + 1) * 512)])
        P.op("dve", lambda e: e.tensor_tensor(
                S5[0:1, 0:SEQ], S5[0:1, 0:SEQ], krss[0:1, 0:SEQ], op=ALU.add),
             reads=[_r(S5_b, 0, SEQ), krss.r()], writes=[_r(S5_b, 0, SEQ)])
        P.op("dve", lambda e: e.tensor_reduce(
                mrow[0:1, 0:1], S5[0:1, 0:SEQ], axis=AX.X, op=ALU.max),
             reads=[_r(S5_b, 0, SEQ)], writes=[mrow.r(0, 1)], fence=True)
        P.op("act", lambda e: e.activation(
                mrow[0:1, 1:2], mrow[0:1, 0:1], AF.Ln, bias=1e-20),
             reads=[mrow.r(0, 1)], writes=[mrow.r(1, 2)])
        P.op("dve", lambda e: e.tensor_scalar(
                mrow[0:1, 2:3], mrow[0:1, 1:2],
                0.5, math.log(1.01), op0=ALU.mult, op1=ALU.add),
             reads=[mrow.r(1, 2)], writes=[mrow.r(2, 3)])
        for nt in range(2):
            for k in range(8):
                P.mm(M2[:, nt * 512:(nt + 1) * 512],
                     wqu[:, k * 128:(k + 1) * 128],
                     qcn[:, k * TOWN + nt * 512: k * TOWN + (nt + 1) * 512],
                     start=(k == 0), stop=(k == 7),
                     reads=[wqu.r(k * 128, (k + 1) * 128),
                            qcn.r(k * TOWN + nt * 512, k * TOWN + (nt + 1) * 512)],
                     writes=[_r(M2_b, nt * 512, (nt + 1) * 512)])
        P.op("act", lambda e: e.copy(qn_g.ap(), M2[:]),
             reads=[_r(M2_b)], writes=[qn_g.r()])
        for nt in range(2):
            for k in range(8):
                P.mm(S5[0:64, nt * 512:(nt + 1) * 512],
                     wqr[:, k * 64:(k + 1) * 64],
                     qcn[:, k * TOWN + nt * 512: k * TOWN + (nt + 1) * 512],
                     start=(k == 0), stop=(k == 7),
                     reads=[wqr.r(k * 64, (k + 1) * 64),
                            qcn.r(k * TOWN + nt * 512, k * TOWN + (nt + 1) * 512)],
                     writes=[_r(S5_b, nt * 512, (nt + 1) * 512)])
        P.op("act", lambda e: e.copy(qx.ap(), S5[0:64, 0:1024]),
             reads=[_r(S5_b, 0, 1024)], writes=[qx.r()])
        for nt in range(2):
            P.mm(S5[0:64, 1024 + nt * 512: 1024 + (nt + 1) * 512], C["perm64"][:],
                 qx[0:64, nt * 512:(nt + 1) * 512], start=True, stop=True,
                 reads=[qx.r(nt * 512, (nt + 1) * 512)],
                 writes=[_r(S5_b, 1024 + nt * 512, 1024 + (nt + 1) * 512)])
        P.op("dve", lambda e: e.tensor_tensor(
                rt1c.ap(), qx.ap(), C["c0_own"][:], op=ALU.mult),
             reads=[qx.r()], writes=[rt1c.r()])
        P.op("dve", lambda e: e.tensor_tensor(
                rt2c.ap(), S5[0:64, 1024:2048], C["s0_own"][:], op=ALU.mult),
             reads=[_r(S5_b, 1024, 2048)], writes=[rt2c.r()])
        P.op("dve", lambda e: e.tensor_tensor(
                qa_g[0:64, 0:TOWN], rt1c.ap(), rt2c.ap(), op=ALU.add),
             reads=[rt1c.r(), rt2c.r()], writes=[qa_g.r()])
        P.op("dve", lambda e: e.tensor_tensor(
                sqc.ap(), qn_g.ap(), qn_g.ap(), op=ALU.mult),
             reads=[qn_g.r()], writes=[sqc.r()])
        for nt in range(2):
            P.mm(S5[0:1, nt * 512:(nt + 1) * 512], C["ones_f"][:],
                 sqc[:, nt * 512:(nt + 1) * 512], start=True, stop=False,
                 reads=[sqc.r(nt * 512, (nt + 1) * 512)],
                 writes=[_r(S5_b, nt * 512, (nt + 1) * 512)])
        P.op("dve", lambda e: e.tensor_tensor(
                rt1c.ap(), qa_g[0:64, 0:TOWN], qa_g[0:64, 0:TOWN], op=ALU.mult),
             reads=[qa_g.r()], writes=[rt1c.r()])
        for nt in range(2):
            P.mm(S5[0:1, nt * 512:(nt + 1) * 512], C["ones_f"][0:64, :],
                 rt1c[0:64, nt * 512:(nt + 1) * 512], start=False, stop=True,
                 reads=[rt1c.r(nt * 512, (nt + 1) * 512)],
                 writes=[_r(S5_b, nt * 512, (nt + 1) * 512)])
        P.op("act", lambda e: e.activation(cqr.ap(), S5[0:1, 0:1024], AF.Ln,
                                           bias=1e-20),
             reads=[_r(S5_b, 0, 1024)], writes=[cqr.r()])
        P.op("act", lambda e: e.activation(
                cqb_t[:], cqr.ap(), AF.Exp, scale=0.5, bias=mrow[0:1, 2:3]),
             reads=[cqr.r(), mrow.r(2, 3)], writes=[_r(cqb_b)])
        P.dma("dsp", qa_g[64:65, 0:TOWN], cqb_t[:],
              reads=[_r(cqb_b)], writes=[qa_g.r()])
        for p in range(4):
            segs = _seg_layout(p)
            P.dma("dpool", msk.ap(), bd.dram["l0mask"][p], writes=[msk.r()])
            for si, (kc, w, off) in enumerate(segs):
                qoff = p * 256 + (256 - w)
                P.mm(S5[:, off:off + w],
                     kh_g[:, kc * 128:(kc + 1) * 128],
                     qn_g[:, qoff: p * 256 + 256],
                     start=True, stop=False,
                     reads=[kh_g.r(kc * 128, (kc + 1) * 128),
                            qn_g.r(qoff, p * 256 + 256)],
                     writes=[_r(S5_b, off, off + w)])
                P.mm(S5[:, off:off + w],
                     kra[0:65, kc * 128:(kc + 1) * 128],
                     qa_g[0:65, qoff: p * 256 + 256],
                     start=False, stop=True,
                     reads=[kra.r(kc * 128, (kc + 1) * 128),
                            qa_g.r(qoff, p * 256 + 256)],
                     writes=[_r(S5_b, off, off + w)])
            P.op("act", lambda e: e.activation(PT.ap(), S5[:, 0:2304], AF.Exp),
                 reads=[_r(S5_b, 0, 2304)], writes=[PT.r()])
            P.op("dve", lambda e: e.tensor_tensor(PT.ap(), PT.ap(), msk.ap(),
                                                  op=ALU.mult),
                 reads=[PT.r(), msk.r()], writes=[PT.r()])
            for si, (kc, w, off) in enumerate(segs):
                P.mm(A1[0:1, 256 - w:256], C["ones_b"][:], PT[:, off:off + w],
                     start=(si == 0), stop=(si == len(segs) - 1),
                     reads=[PT.r(off, off + w)], writes=[_r(A1_b, 0, 256)])
            for si, (kc, w, off) in enumerate(segs):
                P.mm(M2[:, 256 - w:256],
                     v_g[:, kc * 128:(kc + 1) * 128],
                     PT[:, off:off + w],
                     start=(si == 0), stop=(si == len(segs) - 1),
                     reads=[v_g.r(kc * 128, (kc + 1) * 128),
                            PT.r(off, off + w)],
                     writes=[_r(M2_b, 0, 256)])
            P.op("dve", lambda e: e.reciprocal(rcp.ap(), A1[0:1, 0:256]),
                 reads=[_r(A1_b, 0, 256)], writes=[rcp.r()], fence=True)
            P.mm(A1[:, 256:512], C["onesrow_f"][:], rcp.ap(),
                 start=True, stop=True,
                 reads=[rcp.r()], writes=[_r(A1_b, 256, 512)])
            P.op("act", lambda e: e.copy(bcsb.ap(), A1[:, 256:512]),
                 reads=[_r(A1_b, 256, 512)], writes=[bcsb.r()])
            P.op("dve", lambda e, h=h, p=p: e.tensor_tensor(
                    attnout[:, h * TOWN + p * 256: h * TOWN + (p + 1) * 256],
                    M2[:, 0:256], bcsb.ap(), op=ALU.mult),
                 reads=[_r(M2_b, 0, 256), bcsb.r()],
                 writes=[attnout.r(h * TOWN + p * 256, h * TOWN + (p + 1) * 256)])
    A16.release(); A32.release()
    P.barrier()

    P.dma("dsp", X.ap().rearrange("p (k t) -> p k t", k=16),
          bd.dram["xT_own"].rearrange("(k p) t -> p k t", p=128),
          writes=[X.r()])
    A16.mark(); A32.mark()
    wo2 = [A16.alloc("wo0", 128, 16 * 128), A16.alloc("wo1", 128, 16 * 128)]
    emit_matmul_residual(bd, "woT", attnout, wo2, X)
    A16.release(); A32.release()
    A32.release()
    emit_postln_inplace(bd)
    A16.release()
    P.barrier()


def emit_matmul_residual(bd, wT_name, src_b16, wo2, X, src_T=TOWN, n_k=16):
    P, C = bd.P, bd.C
    S5, S5_b = bd.ps["S5"]
    M2, M2_b = bd.ps["M2"]
    for oc in range(16):
        wb = wo2[oc % 2]
        P.dma("dpool", wb.ap().rearrange("p (k o) -> p k o", k=n_k),
              bd.dram[wT_name].rearrange("(k p) o -> p k o", p=128)[:, :, oc * 128:(oc + 1) * 128],
              writes=[wb.r()])
        ps, ps_b, po = [(M2, M2_b, 0), (S5, S5_b, 0), (S5, S5_b, 1024)][oc % 3]
        for nt in range(2):
            for k in range(n_k):
                P.mm(ps[:, po + nt * 512: po + (nt + 1) * 512],
                     wb[:, k * 128:(k + 1) * 128],
                     src_b16[:, k * src_T + nt * 512: k * src_T + (nt + 1) * 512],
                     start=(k == 0), stop=(k == n_k - 1),
                     reads=[wb.r(k * 128, (k + 1) * 128),
                            src_b16.r(k * src_T + nt * 512, k * src_T + (nt + 1) * 512)],
                     writes=[_r(ps_b, po + nt * 512, po + (nt + 1) * 512)])
        P.op("dve", lambda e, oc=oc, ps=ps, po=po: e.tensor_tensor(
                X[:, oc * TOWN:(oc + 1) * TOWN], X[:, oc * TOWN:(oc + 1) * TOWN],
                ps[:, po:po + 1024], op=ALU.add),
             reads=[X.r(oc * TOWN, (oc + 1) * TOWN), _r(ps_b, po, po + 1024)],
             writes=[X.r(oc * TOWN, (oc + 1) * TOWN)])


def emit_postln_inplace(bd):
    nc, P, C = bd.nc, bd.P, bd.C
    S5, S5_b = bd.ps["S5"]
    M2, M2_b = bd.ps["M2"]
    A16, A32 = bd.A16, bd.A32
    X, Xb = bd.X, bd.Xb
    P.barrier()
    A32.mark()
    sql = A32.alloc("sqLN", 128, 1024)
    rows = A32.alloc("rowsLN", 1, 4 * 1024)
    for oc in range(16):
        P.op("dve", lambda e, oc=oc: e.tensor_tensor(
                sql.ap(), X[:, oc * TOWN:(oc + 1) * TOWN],
                X[:, oc * TOWN:(oc + 1) * TOWN], op=ALU.mult),
             reads=[X.r(oc * TOWN, (oc + 1) * TOWN)], writes=[sql.r()])
        for nt in range(2):
            P.mm(M2[0:1, nt * 512:(nt + 1) * 512], C["ones_f"][:],
                 X[:, oc * TOWN + nt * 512: oc * TOWN + (nt + 1) * 512],
                 start=(oc == 0), stop=(oc == 15),
                 reads=[X.r(oc * TOWN + nt * 512, oc * TOWN + (nt + 1) * 512)],
                 writes=[_r(M2_b, nt * 512, (nt + 1) * 512)])
            P.mm(S5[0:1, nt * 512:(nt + 1) * 512], C["ones_f"][:],
                 sql[:, nt * 512:(nt + 1) * 512],
                 start=(oc == 0), stop=(oc == 15),
                 reads=[sql.r(nt * 512, (nt + 1) * 512)],
                 writes=[_r(S5_b, nt * 512, (nt + 1) * 512)])
    mu = lambda: rows[0:1, 0:1024]
    msq = lambda: rows[0:1, 1024:2048]
    rstd = lambda: rows[0:1, 2048:3072]
    murs = lambda: rows[0:1, 3072:4096]
    P.op("dve", lambda e: e.tensor_scalar(mu(), M2[0:1, 0:1024], 1.0 / DM, None,
                                          op0=ALU.mult),
         reads=[_r(M2_b, 0, 1024)], writes=[rows.r(0, 1024)])
    P.op("dve", lambda e: e.tensor_scalar(msq(), S5[0:1, 0:1024], 1.0 / DM, None,
                                          op0=ALU.mult),
         reads=[_r(S5_b, 0, 1024)], writes=[rows.r(1024, 2048)])
    P.op("dve", lambda e: e.tensor_tensor(rstd(), mu(), mu(), op=ALU.mult),
         reads=[rows.r(0, 1024)], writes=[rows.r(2048, 3072)])
    P.op("dve", lambda e: e.tensor_tensor(msq(), msq(), rstd(), op=ALU.subtract),
         reads=[rows.r(1024, 3072)], writes=[rows.r(1024, 2048)])
    P.op("act", lambda e: e.activation(rstd(), msq(), AF.Ln, bias=1e-5),
         reads=[rows.r(1024, 2048)], writes=[rows.r(2048, 3072)])
    P.op("act", lambda e: e.activation(rstd(), rstd(), AF.Exp, scale=-0.5),
         reads=[rows.r(2048, 3072)], writes=[rows.r(2048, 3072)])
    P.op("dve", lambda e: e.tensor_tensor(murs(), mu(), rstd(), op=ALU.mult),
         reads=[rows.r(0, 3072)], writes=[rows.r(3072, 4096)])
    for nt in range(2):
        P.mm(M2[:, nt * 512:(nt + 1) * 512], C["onesrow_f"][:],
             rows[0:1, 2048 + nt * 512:2048 + (nt + 1) * 512], start=True, stop=True,
             reads=[rows.r(2048, 3072)], writes=[_r(M2_b, nt * 512, (nt + 1) * 512)])
        P.mm(S5[:, nt * 512:(nt + 1) * 512], C["onesrow_f"][:],
             rows[0:1, 3072 + nt * 512:3072 + (nt + 1) * 512], start=True, stop=True,
             reads=[rows.r(3072, 4096)], writes=[_r(S5_b, nt * 512, (nt + 1) * 512)])
    for oc in range(16):
        P.op("dve", lambda e, oc=oc: e.tensor_tensor(
                X[:, oc * TOWN:(oc + 1) * TOWN], X[:, oc * TOWN:(oc + 1) * TOWN],
                M2[:, 0:1024], op=ALU.mult),
             reads=[X.r(oc * TOWN, (oc + 1) * TOWN), _r(M2_b, 0, 1024)],
             writes=[X.r(oc * TOWN, (oc + 1) * TOWN)])
        P.op("dve", lambda e, oc=oc: e.tensor_tensor(
                X[:, oc * TOWN:(oc + 1) * TOWN], X[:, oc * TOWN:(oc + 1) * TOWN],
                S5[:, 0:1024], op=ALU.subtract),
             reads=[X.r(oc * TOWN, (oc + 1) * TOWN), _r(S5_b, 0, 1024)],
             writes=[X.r(oc * TOWN, (oc + 1) * TOWN)])
        P.op("pool", lambda e, oc=oc: e.tensor_copy(
                Xb[:, oc * TOWN:(oc + 1) * TOWN], X[:, oc * TOWN:(oc + 1) * TOWN]),
             reads=[X.r(oc * TOWN, (oc + 1) * TOWN)],
             writes=[Xb.r(oc * TOWN, (oc + 1) * TOWN)])
    A32.release()
    P.barrier()
def emit_l0_moe(bd):
    nc, P, C = bd.nc, bd.P, bd.C
    S5, S5_b = bd.ps["S5"]
    M2, M2_b = bd.ps["M2"]
    A1, A1_b = bd.ps["A1"]
    A16, A32 = bd.A16, bd.A32
    X, Xb = bd.X, bd.Xb

    A16.mark(); A32.mark()
    gw = A32.alloc("gwf", 128, 16 * 16)
    ct = A32.alloc("ct", 16, TOWN)
    ctrow = A32.alloc("ctrow", 1, TOWN)
    cb = A32.alloc("cb", 128, TOWN)
    h = A16.alloc("h", 128, 11 * 1024)
    wg = [A16.alloc("wg0", 128, 16 * 128), A16.alloc("wg1", 128, 16 * 128)]
    wu = [A16.alloc("wu0", 128, 16 * 128), A16.alloc("wu1", 128, 16 * 128)]
    wd = [A16.alloc("wd0", 128, 11 * 128), A16.alloc("wd1", 128, 11 * 128)]
    probs = A32.alloc("probs", 128, 16)
    m16 = A32.alloc("m16", 128, 16)
    top8 = A32.alloc("top8", 128, 8)
    gr = A32.alloc("gr", 128, 4)
    sg = A32.alloc("sg", 128, 1024)

    P.dma("dsp", gw.ap().rearrange("p (k o) -> p k o", k=16),
          bd.dram["gate_wT"].rearrange("(k p) o -> p k o", p=128),
          writes=[gw.r()])
    for i in range(8):
        for k in range(16):
            P.mm(A1[:, 0:16],
                 X[:, k * TOWN + i * 128: k * TOWN + (i + 1) * 128],
                 gw[:, k * 16:(k + 1) * 16],
                 start=(k == 0), stop=(k == 15),
                 reads=[X.r(k * TOWN + i * 128, k * TOWN + (i + 1) * 128),
                        gw.r(k * 16, (k + 1) * 16)],
                 writes=[_r(A1_b, 0, 16)])
        P.op("dve", lambda e: e.tensor_reduce(gr[:, 0:1], A1[:, 0:16], axis=AX.X,
                                              op=ALU.max, negate=True),
             reads=[_r(A1_b, 0, 16)], writes=[gr.r(0, 1)], fence=True)
        P.op("act", lambda e: e.activation(probs.ap(), A1[:, 0:16], AF.Exp,
                                           bias=gr[:, 0:1]),
             reads=[_r(A1_b, 0, 16), gr.r(0, 1)], writes=[probs.r()])
        P.op("dve", lambda e: e.max(top8.ap(), probs.ap()),
             reads=[probs.r()], writes=[top8.r()], fence=True)
        P.op("dve", lambda e: e.tensor_reduce(gr[:, 2:3], top8[:, 0:6], axis=AX.X,
                                              op=ALU.add),
             reads=[top8.r()], writes=[gr.r(2, 3)], fence=True)
        P.op("dve", lambda e: e.tensor_scalar(gr[:, 2:3], gr[:, 2:3], 1e-20, None,
                                              op0=ALU.add),
             reads=[gr.r(2, 3)], writes=[gr.r(2, 3)])
        P.op("dve", lambda e: e.reciprocal(gr[:, 3:4], gr[:, 2:3]),
             reads=[gr.r(2, 3)], writes=[gr.r(3, 4)], fence=True)
        P.op("dve", lambda e: e.tensor_scalar(m16.ap(), probs.ap(),
                                              top8[:, 5:6], None, op0=ALU.is_ge),
             reads=[probs.r(), top8.r()], writes=[m16.r()])
        P.op("dve", lambda e: e.scalar_tensor_tensor(
                m16.ap(), probs.ap(), gr[:, 3:4], m16.ap(),
                op0=ALU.mult, op1=ALU.mult),
             reads=[probs.r(), gr.r(3, 4), m16.r()], writes=[m16.r()])
        P.mm(A1[0:16, 128:256], m16.ap(), C["ident_f"][:],
             start=True, stop=True,
             reads=[m16.r()], writes=[_r(A1_b, 128, 256)])
        P.op("act", lambda e, i=i: e.copy(ct[0:16, i * 128:(i + 1) * 128],
                                          A1[0:16, 128:256]),
             reads=[_r(A1_b, 128, 256)], writes=[ct.r(i * 128, (i + 1) * 128)])
    P.barrier()
    if getattr(bd, "dbg", False):
        dbg_ct = bd.dout("dbg_ct", (16, TOWN))
        bd.P.mark_extra_final(P.dma("dsp", dbg_ct, ct.ap(), reads=[ct.r()]))

    ROT = [(M2, M2_b, 0), (S5, S5_b, 0), (S5, S5_b, 1024)]

    def swiglu_expert(e_idx, gate_src, up_src, down_src, n_i, scale_by_c):
        for j in range(n_i):
            P.dma("dpool", wg[j % 2].ap().rearrange("p (k o) -> p k o", k=16),
                  gate_src(j), writes=[wg[j % 2].r()])
            P.dma("dpool", wu[j % 2].ap().rearrange("p (k o) -> p k o", k=16),
                  up_src(j), writes=[wu[j % 2].r()])
            gp, gp_b, go = ROT[j % 3]
            up, up_b, uo = ROT[(j + 2) % 3]
            for nt in range(2):
                for k in range(16):
                    P.mm(gp[:, go + nt * 512: go + (nt + 1) * 512],
                         wg[j % 2][:, k * 128:(k + 1) * 128],
                         Xb[:, k * TOWN + nt * 512: k * TOWN + (nt + 1) * 512],
                         start=(k == 0), stop=(k == 15),
                         reads=[wg[j % 2].r(k * 128, (k + 1) * 128),
                                Xb.r(k * TOWN + nt * 512, k * TOWN + (nt + 1) * 512)],
                         writes=[_r(gp_b, go + nt * 512, go + (nt + 1) * 512)])
            for nt in range(2):
                for k in range(16):
                    P.mm(up[:, uo + nt * 512: uo + (nt + 1) * 512],
                         wu[j % 2][:, k * 128:(k + 1) * 128],
                         Xb[:, k * TOWN + nt * 512: k * TOWN + (nt + 1) * 512],
                         start=(k == 0), stop=(k == 15),
                         reads=[wu[j % 2].r(k * 128, (k + 1) * 128),
                                Xb.r(k * TOWN + nt * 512, k * TOWN + (nt + 1) * 512)],
                         writes=[_r(up_b, uo + nt * 512, uo + (nt + 1) * 512)])
            P.op("act", lambda e, gp=gp, gp_b=gp_b, go=go: e.activation(
                    sg.ap(), gp[:, go:go + 1024], AF.Silu),
                 reads=[_r(gp_b, go, go + 1024)], writes=[sg.r()])
            P.op("dve", lambda e, up=up, up_b=up_b, uo=uo: e.tensor_tensor(
                    sg.ap(), up[:, uo:uo + 1024], sg.ap(), op=ALU.mult),
                 reads=[_r(up_b, uo, uo + 1024), sg.r()], writes=[sg.r()])
            if scale_by_c:
                P.op("dve", lambda e, j=j: e.tensor_tensor(
                        h[:, j * 1024:(j + 1) * 1024], sg.ap(), cb.ap(), op=ALU.mult),
                     reads=[sg.r(), cb.r()], writes=[h.r(j * 1024, (j + 1) * 1024)])
            else:
                P.op("dve", lambda e, j=j: e.tensor_copy(
                        h[:, j * 1024:(j + 1) * 1024], sg.ap()),
                     reads=[sg.r()], writes=[h.r(j * 1024, (j + 1) * 1024)])
        for oc in range(16):
            P.dma("dpool", wd[oc % 2].ap().rearrange("p (k o) -> p k o", k=n_i),
                  down_src(oc), writes=[wd[oc % 2].r()])
            ps, ps_b, po = ROT[oc % 3]
            for nt in range(2):
                for k in range(n_i):
                    P.mm(ps[:, po + nt * 512: po + (nt + 1) * 512],
                         wd[oc % 2][:, k * 128:(k + 1) * 128],
                         h[:, k * 1024 + nt * 512: k * 1024 + (nt + 1) * 512],
                         start=(k == 0), stop=(k == n_i - 1),
                         reads=[wd[oc % 2].r(k * 128, (k + 1) * 128),
                                h.r(k * 1024 + nt * 512, k * 1024 + (nt + 1) * 512)],
                         writes=[_r(ps_b, po + nt * 512, po + (nt + 1) * 512)])
            P.op("dve", lambda e, oc=oc, ps=ps, ps_b=ps_b, po=po: e.tensor_tensor(
                    X[:, oc * TOWN:(oc + 1) * TOWN], X[:, oc * TOWN:(oc + 1) * TOWN],
                    ps[:, po:po + 1024], op=ALU.add),
                 reads=[X.r(oc * TOWN, (oc + 1) * TOWN), _r(ps_b, po, po + 1024)],
                 writes=[X.r(oc * TOWN, (oc + 1) * TOWN)])

    w13r = bd.dram["w13T"].rearrange("e (k p) o -> e p k o", p=128)
    w2r = bd.dram["w2T"].rearrange("e (k p) o -> e p k o", p=128)
    for e_idx in range(NEXP):
        P.dma("dsp", ctrow.ap(), ct[e_idx:e_idx + 1, 0:TOWN],
              reads=[ct.r()], writes=[ctrow.r()])
        for nt in range(2):
            P.mm(A1[:, 0:512],
                 C["onesrow_f"][:], ctrow[0:1, nt * 512:(nt + 1) * 512],
                 start=True, stop=True,
                 reads=[ctrow.r()], writes=[_r(A1_b, 0, 512)])
            P.op("act", lambda e, nt=nt: e.copy(
                    cb[:, nt * 512:(nt + 1) * 512], A1[:, 0:512]),
                 reads=[_r(A1_b, 0, 512)],
                 writes=[cb.r(nt * 512, (nt + 1) * 512)])
        swiglu_expert(
            e_idx,
            lambda j, e_idx=e_idx: w13r[e_idx][:, :, j * 128:(j + 1) * 128],
            lambda j, e_idx=e_idx: w13r[e_idx][:, :, IMOE + j * 128: IMOE + (j + 1) * 128],
            lambda oc, e_idx=e_idx: w2r[e_idx][:, :, oc * 128:(oc + 1) * 128],
            11, True)
    sw13r = bd.dram["sw13T"].rearrange("(k p) o -> p k o", p=128)
    sw2r = bd.dram["sw2T"].rearrange("(k p) o -> p k o", p=128)
    for s in range(2):
        swiglu_expert(
            100 + s,
            lambda j, s=s: sw13r[:, :, s * 1408 + j * 128: s * 1408 + (j + 1) * 128],
            lambda j, s=s: sw13r[:, :, ISH + s * 1408 + j * 128: ISH + s * 1408 + (j + 1) * 128],
            lambda oc, s=s: sw2r[:, s * 11:(s + 1) * 11, oc * 128:(oc + 1) * 128],
            11, False)
    A16.release(); A32.release()
    P.barrier()
    emit_postln_inplace(bd)


def emit_l1_attn(bd):
    nc, P, C = bd.nc, bd.P, bd.C
    S5, S5_b = bd.ps["S5"]
    M2, M2_b = bd.ps["M2"]
    A1, A1_b = bd.ps["A1"]
    A16, A32 = bd.A16, bd.A32
    X, Xb = bd.X, bd.Xb
    EPS = 1e-6

    cc_in = bd.dint("cc_in", (4096, 128), BF16)
    cc_out = bd.dint("cc_out", (8192, 128), BF16)
    cc_inz = bd.dint("cc_inz", (128, 32), BF16)
    cc_outz = bd.dint("cc_outz", (256, 32), BF16)
    cc_in_b = P.buf("cc_in", None, 128)
    cc_out_b = P.buf("cc_out", None, 128)
    cc_inz_b = P.buf("cc_inz", None, 32)
    cc_outz_b = P.buf("cc_outz", None, 32)

    A16.mark(); A32.mark()
    c1f = A16.alloc("c1f", 128, TOWN)
    s1f = A16.alloc("s1f", 128, TOWN)
    KfF = A16.alloc("KfF", 128, 4 * TOWN)
    VfT = A16.alloc("VfT", 128, 8 * 512)
    Sprev = A16.alloc("Sprev", 128, 8 * 512)
    zp = A16.alloc("zp", 128, 8 * 4)
    kx = A16.alloc("kx", 128, TOWN)
    wpref = A32.alloc("wpref", 128, 128)
    rtA = A32.alloc("rtA", 128, TOWN)
    rtB = A32.alloc("rtB", 128, TOWN)
    rtC = A32.alloc("rtC", 128, TOWN)
    Sacc = A32.alloc("Sacc", 128, 1024)
    zacc = A32.alloc("zacc", 128, 8)
    zbuf = A32.alloc("zbuf", 128, 32)

    P.dma("dpool", c1f.ap(), bd.dram["c1_own"], writes=[c1f.r()])
    P.dma("dpool", s1f.ap(), bd.dram["s1_own"], writes=[s1f.r()])
    P.dma("dsp", wpref.ap(), bd.dram["l1prefw"], writes=[wpref.r()])

    def elu1(v_in, rd_in, v_out, wr_out, sa, sa_r, sb, sb_r):
        P.op("dve", lambda e: e.tensor_scalar(sa, v_in, 0.0, None, op0=ALU.min),
             reads=rd_in, writes=sa_r)
        P.op("act", lambda e: e.activation(sa, sa, AF.Exp), reads=sa_r, writes=sa_r)
        P.op("dve", lambda e: e.tensor_scalar(sb, v_in, 0.0, None, op0=ALU.max),
             reads=rd_in, writes=sb_r)
        P.op("dve", lambda e: e.tensor_tensor(v_out, sa, sb, op=ALU.add),
             reads=sa_r + sb_r, writes=wr_out)

    A16.mark()
    wk2 = [A16.alloc("wk20", 128, 16 * 128), A16.alloc("wk21", 128, 16 * 128)]
    c1t = A16.alloc("c1t", 128, 8 * 128)
    s1t = A16.alloc("s1t", 128, 8 * 128)
    P.dma("dpool", c1t.ap().rearrange("p (tc d) -> p tc d", tc=8),
          bd.dram["c1tok"].rearrange("(tc p) d -> p tc d", p=128), writes=[c1t.r()])
    P.dma("dpool", s1t.ap().rearrange("p (tc d) -> p tc d", tc=8),
          bd.dram["s1tok"].rearrange("(tc p) d -> p tc d", p=128), writes=[s1t.r()])
    for kvh in range(KVH):
        wb = wk2[kvh % 2]
        P.dma("dpool", wb.ap().rearrange("p (k o) -> p k o", k=16),
              bd.dram["WkT"].rearrange("(k p) o -> p k o", p=128)[:, :, kvh * 128:(kvh + 1) * 128],
              writes=[wb.r()])
        for nt in range(2):
            for k in range(16):
                P.mm(M2[:, nt * 512:(nt + 1) * 512],
                     wb[:, k * 128:(k + 1) * 128],
                     Xb[:, k * TOWN + nt * 512: k * TOWN + (nt + 1) * 512],
                     start=(k == 0), stop=(k == 15),
                     reads=[wb.r(k * 128, (k + 1) * 128),
                            Xb.r(k * TOWN + nt * 512, k * TOWN + (nt + 1) * 512)],
                     writes=[_r(M2_b, nt * 512, (nt + 1) * 512)])
        P.op("act", lambda e: e.copy(kx.ap(), M2[:]), reads=[_r(M2_b)], writes=[kx.r()])
        for nt in range(2):
            P.mm(S5[:, nt * 512:(nt + 1) * 512], C["perm128"][:],
                 kx[:, nt * 512:(nt + 1) * 512], start=True, stop=True,
                 reads=[kx.r(nt * 512, (nt + 1) * 512)],
                 writes=[_r(S5_b, nt * 512, (nt + 1) * 512)])
        P.op("dve", lambda e: e.tensor_tensor(rtA.ap(), kx.ap(), c1f.ap(), op=ALU.mult),
             reads=[kx.r(), c1f.r()], writes=[rtA.r()])
        P.op("dve", lambda e: e.tensor_tensor(rtB.ap(), S5[:, 0:1024], s1f.ap(), op=ALU.mult),
             reads=[_r(S5_b, 0, 1024), s1f.r()], writes=[rtB.r()])
        P.op("dve", lambda e: e.tensor_tensor(rtA.ap(), rtA.ap(), rtB.ap(), op=ALU.add),
             reads=[rtA.r(), rtB.r()], writes=[rtA.r()])
        elu1(rtA.ap(), [rtA.r()], KfF[:, kvh * TOWN:(kvh + 1) * TOWN],
             [KfF.r(kvh * TOWN, (kvh + 1) * TOWN)],
             rtB.ap(), [rtB.r()], rtC.ap(), [rtC.r()])

    KfT = A16.alloc("KfT", 128, 8 * 512)
    ccs = A16.alloc("ccs", 128, 8 * 512)
    wv2 = [A16.alloc("wv20", 128, 16 * 128), A16.alloc("wv21", 128, 16 * 128)]
    ktx = A16.alloc("ktx", 128, 128)
    sc1 = A32.alloc("sc1", 128, 128)
    sc2 = A32.alloc("sc2", 128, 128)
    sc3 = A32.alloc("sc3", 128, 128)
    for kvh in range(KVH):
        P.dma("dpool", wv2[0].ap().rearrange("p (k o) -> p k o", k=16),
              bd.dram["WvT"].rearrange("(k p) o -> p k o", p=128)[:, :, kvh * 128:(kvh + 1) * 128],
              writes=[wv2[0].r()])
        P.dma("dpool", wv2[1].ap().rearrange("p (k o) -> p k o", k=16),
              bd.dram["WkT"].rearrange("(k p) o -> p k o", p=128)[:, :, kvh * 128:(kvh + 1) * 128],
              writes=[wv2[1].r()])
        for tc in range(8):
            for k in range(16):
                P.mm(M2[:, 0:128],
                     Xb[:, k * TOWN + tc * 128: k * TOWN + (tc + 1) * 128],
                     wv2[0][:, k * 128:(k + 1) * 128],
                     start=(k == 0), stop=(k == 15),
                     reads=[Xb.r(k * TOWN + tc * 128, k * TOWN + (tc + 1) * 128),
                            wv2[0].r(k * 128, (k + 1) * 128)],
                     writes=[_r(M2_b, 0, 128)])
            P.op("act", lambda e, tc=tc, kvh=kvh: e.copy(
                    VfT[:, tc * 512 + kvh * 128: tc * 512 + (kvh + 1) * 128],
                    M2[:, 0:128]),
                 reads=[_r(M2_b, 0, 128)],
                 writes=[VfT.r(tc * 512 + kvh * 128, tc * 512 + (kvh + 1) * 128)])
            for k in range(16):
                P.mm(M2[:, 128:256],
                     Xb[:, k * TOWN + tc * 128: k * TOWN + (tc + 1) * 128],
                     wv2[1][:, k * 128:(k + 1) * 128],
                     start=(k == 0), stop=(k == 15),
                     reads=[Xb.r(k * TOWN + tc * 128, k * TOWN + (tc + 1) * 128),
                            wv2[1].r(k * 128, (k + 1) * 128)],
                     writes=[_r(M2_b, 128, 256)])
            P.op("act", lambda e: e.copy(ktx.ap(), M2[:, 128:256]),
                 reads=[_r(M2_b, 128, 256)], writes=[ktx.r()])
            P.op("dve", lambda e, tc=tc: e.tensor_tensor(
                    sc1[:, 0:64], ktx[:, 0:64], c1t[:, tc * 128: tc * 128 + 64],
                    op=ALU.mult),
                 reads=[ktx.r(), c1t.r()], writes=[sc1.r(0, 64)])
            P.op("dve", lambda e, tc=tc: e.tensor_tensor(
                    sc2[:, 0:64], ktx[:, 64:128], s1t[:, tc * 128: tc * 128 + 64],
                    op=ALU.mult),
                 reads=[ktx.r(), s1t.r()], writes=[sc2.r(0, 64)])
            P.op("dve", lambda e: e.tensor_tensor(
                    sc1[:, 0:64], sc1[:, 0:64], sc2[:, 0:64], op=ALU.subtract),
                 reads=[sc1.r(0, 64), sc2.r(0, 64)], writes=[sc1.r(0, 64)])
            P.op("dve", lambda e, tc=tc: e.tensor_tensor(
                    sc1[:, 64:128], ktx[:, 64:128], c1t[:, tc * 128 + 64: tc * 128 + 128],
                    op=ALU.mult),
                 reads=[ktx.r(), c1t.r()], writes=[sc1.r(64, 128)])
            P.op("dve", lambda e, tc=tc: e.tensor_tensor(
                    sc2[:, 64:128], ktx[:, 0:64], s1t[:, tc * 128 + 64: tc * 128 + 128],
                    op=ALU.mult),
                 reads=[ktx.r(), s1t.r()], writes=[sc2.r(64, 128)])
            P.op("dve", lambda e: e.tensor_tensor(
                    sc1[:, 64:128], sc1[:, 64:128], sc2[:, 64:128], op=ALU.add),
                 reads=[sc1.r(64, 128), sc2.r(64, 128)], writes=[sc1.r(64, 128)])
            elu1(sc1.ap(), [sc1.r()],
                 KfT[:, tc * 512 + kvh * 128: tc * 512 + (kvh + 1) * 128],
                 [KfT.r(tc * 512 + kvh * 128, tc * 512 + (kvh + 1) * 128)],
                 sc2.ap(), [sc2.r()], sc3.ap(), [sc3.r()])
            P.mm(A1[:, 0:128],
                 KfT[:, tc * 512 + kvh * 128: tc * 512 + (kvh + 1) * 128],
                 VfT[:, tc * 512 + kvh * 128: tc * 512 + (kvh + 1) * 128],
                 start=True, stop=True,
                 reads=[KfT.r(tc * 512 + kvh * 128, tc * 512 + (kvh + 1) * 128),
                        VfT.r(tc * 512 + kvh * 128, tc * 512 + (kvh + 1) * 128)],
                 writes=[_r(A1_b, 0, 128)])
            P.op("act", lambda e, tc=tc, kvh=kvh: e.copy(
                    ccs[:, tc * 512 + kvh * 128: tc * 512 + (kvh + 1) * 128],
                    A1[:, 0:128]),
                 reads=[_r(A1_b, 0, 128)],
                 writes=[ccs.r(tc * 512 + kvh * 128, tc * 512 + (kvh + 1) * 128)])
            P.op("dve", lambda e, tc=tc, kvh=kvh: e.tensor_reduce(
                    zbuf[:, tc * 4 + kvh: tc * 4 + kvh + 1],
                    KfF[:, kvh * TOWN + tc * 128: kvh * TOWN + (tc + 1) * 128],
                    axis=AX.X, op=ALU.add),
                 reads=[KfF.r(kvh * TOWN + tc * 128, kvh * TOWN + (tc + 1) * 128)],
                 writes=[zbuf.r(tc * 4 + kvh, tc * 4 + kvh + 1)], fence=True)

    P.dma("dsp", cc_in.rearrange("(tc kvh p) d -> p tc kvh d", tc=8, kvh=4),
          ccs.ap().rearrange("p (tc kvh d) -> p tc kvh d", tc=8, kvh=4),
          reads=[ccs.r()], writes=[_r(cc_in_b)])
    P.dma("dpool", cc_inz, zbuf.ap(), reads=[zbuf.r()], writes=[_r(cc_inz_b)])
    rg = [[0, 1], [2, 3], [4, 5], [6, 7]]
    P.op("pool", lambda e: e.collective_compute(
            "AllGather", ALU.bypass, replica_groups=rg,
            ins=[bd.dram["cc_in"].opt()], outs=[bd.dram["cc_out"].opt()]),
         reads=[_r(cc_in_b)], writes=[_r(cc_out_b)], fence=True)
    P.op("pool", lambda e: e.collective_compute(
            "AllGather", ALU.bypass, replica_groups=rg,
            ins=[bd.dram["cc_inz"].opt()], outs=[bd.dram["cc_outz"].opt()]),
         reads=[_r(cc_inz_b)], writes=[_r(cc_outz_b)], fence=True)

    A16.release()
    A16.mark()
    GS = A16.alloc("GS", 128, 16 * 512)
    zg = A16.alloc("zg", 128, 64)
    P.dma("dsp", GS.ap().rearrange("p (r tc kvh d) -> p r tc kvh d", r=2, tc=8, kvh=4),
          bd.dram["cc_out"].rearrange("(r tc kvh p) d -> p r tc kvh d", r=2, tc=8, kvh=4),
          reads=[_r(cc_out_b)], writes=[GS.r()])
    P.dma("dsp", zg.ap().rearrange("p (r c) -> p r c", r=2),
          bd.dram["cc_outz"].rearrange("(r p) c -> p r c", r=2),
          reads=[_r(cc_outz_b)], writes=[zg.r()])
    for c in range(8):
        sA = lambda i: Sacc[:, (i % 2) * 512:(i % 2) * 512 + 512]
        sAr = lambda i: [Sacc.r((i % 2) * 512, (i % 2) * 512 + 512)]
        zA = lambda i: zacc[:, (i % 2) * 4:(i % 2) * 4 + 4]
        zAr = lambda i: [zacc.r((i % 2) * 4, (i % 2) * 4 + 4)]
        P.op("dve", lambda e: e.memset(sA(0), 0.0), writes=sAr(0))
        P.op("dve", lambda e: e.memset(zA(0), 0.0), writes=zAr(0))
        for sl in range(16):
            P.op("dve", lambda e, c=c, sl=sl: e.scalar_tensor_tensor(
                    sA(sl + 1), GS[:, sl * 512:(sl + 1) * 512],
                    wpref[:, c * 16 + sl: c * 16 + sl + 1], sA(sl),
                    op0=ALU.mult, op1=ALU.add),
                 reads=[GS.r(sl * 512, (sl + 1) * 512), wpref.r()] + sAr(sl),
                 writes=sAr(sl + 1))
            r_, tc_ = sl // 8, sl % 8
            P.op("dve", lambda e, c=c, sl=sl, r_=r_, tc_=tc_: e.scalar_tensor_tensor(
                    zA(sl + 1), zg[:, r_ * 32 + tc_ * 4: r_ * 32 + tc_ * 4 + 4],
                    wpref[:, c * 16 + sl: c * 16 + sl + 1], zA(sl),
                    op0=ALU.mult, op1=ALU.add),
                 reads=[zg.r(), wpref.r()] + zAr(sl), writes=zAr(sl + 1))
        P.op("dve", lambda e, c=c: e.tensor_copy(Sprev[:, c * 512:(c + 1) * 512],
                                                 sA(16)),
             reads=sAr(16), writes=[Sprev.r(c * 512, (c + 1) * 512)])
        P.op("dve", lambda e, c=c: e.tensor_copy(zp[:, c * 4:(c + 1) * 4], zA(16)),
             reads=zAr(16), writes=[zp.r(c * 4, (c + 1) * 4)])
    A16.release()
    P.barrier()

    A16.mark()
    attnL1 = A16.alloc("attnL1", 128, 16 * TOWN)
    A16.mark()
    wq2 = [A16.alloc("wq20", 128, 16 * 128)]
    Qfh = A16.alloc("Qfh", 128, TOWN)
    ATb = A16.alloc("ATb", 128, TOWN)
    den = A32.alloc("denL1", 1, TOWN)
    bcl = A32.alloc("bcl", 128, TOWN)
    for h in range(H1):
        kvh = h // REP
        wb = wq2[0]
        P.dma("dpool", wb.ap().rearrange("p (k o) -> p k o", k=16),
              bd.dram["WqT"].rearrange("(k p) o -> p k o", p=128)[:, :, h * 128:(h + 1) * 128],
              writes=[wb.r()])
        for nt in range(2):
            for k in range(16):
                P.mm(M2[:, nt * 512:(nt + 1) * 512],
                     wb[:, k * 128:(k + 1) * 128],
                     Xb[:, k * TOWN + nt * 512: k * TOWN + (nt + 1) * 512],
                     start=(k == 0), stop=(k == 15),
                     reads=[wb.r(k * 128, (k + 1) * 128),
                            Xb.r(k * TOWN + nt * 512, k * TOWN + (nt + 1) * 512)],
                     writes=[_r(M2_b, nt * 512, (nt + 1) * 512)])
        P.op("act", lambda e: e.copy(kx.ap(), M2[:]), reads=[_r(M2_b)], writes=[kx.r()])
        for nt in range(2):
            P.mm(S5[:, nt * 512:(nt + 1) * 512], C["perm128"][:],
                 kx[:, nt * 512:(nt + 1) * 512], start=True, stop=True,
                 reads=[kx.r(nt * 512, (nt + 1) * 512)],
                 writes=[_r(S5_b, nt * 512, (nt + 1) * 512)])
        P.op("dve", lambda e: e.tensor_tensor(rtA.ap(), kx.ap(), c1f.ap(), op=ALU.mult),
             reads=[kx.r(), c1f.r()], writes=[rtA.r()])
        P.op("dve", lambda e: e.tensor_tensor(rtB.ap(), S5[:, 0:1024], s1f.ap(), op=ALU.mult),
             reads=[_r(S5_b, 0, 1024), s1f.r()], writes=[rtB.r()])
        P.op("dve", lambda e: e.tensor_tensor(rtA.ap(), rtA.ap(), rtB.ap(), op=ALU.add),
             reads=[rtA.r(), rtB.r()], writes=[rtA.r()])
        elu1(rtA.ap(), [rtA.r()], Qfh.ap(), [Qfh.r()],
             rtB.ap(), [rtB.r()], rtC.ap(), [rtC.r()])
        for c in range(8):
            P.mm(S5[:, c * 128:(c + 1) * 128],
                 KfF[:, kvh * TOWN + c * 128: kvh * TOWN + (c + 1) * 128],
                 Qfh[:, c * 128:(c + 1) * 128],
                 start=True, stop=True,
                 reads=[KfF.r(kvh * TOWN + c * 128, kvh * TOWN + (c + 1) * 128),
                        Qfh.r(c * 128, (c + 1) * 128)],
                 writes=[_r(S5_b, c * 128, (c + 1) * 128)])
        P.op("dve", lambda e: e.tensor_tensor(ATb.ap(), S5[:, 0:1024],
                                              C["trirep"][:], op=ALU.mult),
             reads=[_r(S5_b, 0, 1024)], writes=[ATb.r()])
        for c in range(8):
            P.mm(M2[:, c * 128:(c + 1) * 128],
                 VfT[:, c * 512 + kvh * 128: c * 512 + (kvh + 1) * 128],
                 ATb[:, c * 128:(c + 1) * 128], start=True, stop=False,
                 reads=[VfT.r(c * 512 + kvh * 128, c * 512 + (kvh + 1) * 128),
                        ATb.r(c * 128, (c + 1) * 128)],
                 writes=[_r(M2_b, c * 128, (c + 1) * 128)])
            P.mm(M2[:, c * 128:(c + 1) * 128],
                 Sprev[:, c * 512 + kvh * 128: c * 512 + (kvh + 1) * 128],
                 Qfh[:, c * 128:(c + 1) * 128],
                 start=False, stop=True,
                 reads=[Sprev.r(c * 512 + kvh * 128, c * 512 + (kvh + 1) * 128),
                        Qfh.r(c * 128, (c + 1) * 128)],
                 writes=[_r(M2_b, c * 128, (c + 1) * 128)])
            P.mm(S5[0:1, 1024 + c * 128: 1024 + (c + 1) * 128], C["ones_b"][:],
                 ATb[:, c * 128:(c + 1) * 128],
                 start=True, stop=False,
                 reads=[ATb.r(c * 128, (c + 1) * 128)],
                 writes=[_r(S5_b, 1024 + c * 128, 1024 + (c + 1) * 128)])
            P.mm(S5[0:1, 1024 + c * 128: 1024 + (c + 1) * 128],
                 zp[:, c * 4 + kvh: c * 4 + kvh + 1],
                 Qfh[:, c * 128:(c + 1) * 128],
                 start=False, stop=True,
                 reads=[zp.r(c * 4 + kvh, c * 4 + kvh + 1),
                        Qfh.r(c * 128, (c + 1) * 128)],
                 writes=[_r(S5_b, 1024 + c * 128, 1024 + (c + 1) * 128)])
        P.op("dve", lambda e: e.tensor_scalar(den.ap(), S5[0:1, 1024:2048],
                                              EPS, None, op0=ALU.add),
             reads=[_r(S5_b, 1024, 2048)], writes=[den.r()])
        P.op("dve", lambda e: e.reciprocal(den.ap(), den.ap()),
             reads=[den.r()], writes=[den.r()], fence=True)
        for nt in range(2):
            P.mm(S5[:, 1024 + nt * 512: 1024 + (nt + 1) * 512], C["onesrow_f"][:],
                 den[0:1, nt * 512:(nt + 1) * 512], start=True, stop=True,
                 reads=[den.r()], writes=[_r(S5_b, 1024 + nt * 512, 1024 + (nt + 1) * 512)])
        P.op("act", lambda e: e.copy(bcl.ap(), S5[:, 1024:2048]),
             reads=[_r(S5_b, 1024, 2048)], writes=[bcl.r()])
        P.op("dve", lambda e, h=h: e.tensor_tensor(
                attnL1[:, h * TOWN:(h + 1) * TOWN],
                M2[:, 0:1024], bcl.ap(), op=ALU.mult),
             reads=[_r(M2_b, 0, 1024), bcl.r()],
             writes=[attnL1.r(h * TOWN, (h + 1) * TOWN)])
    P.barrier()
    A16.release()
    A16.mark()
    wo2 = [A16.alloc("woL10", 128, 16 * 128), A16.alloc("woL11", 128, 16 * 128)]
    emit_matmul_residual(bd, "WoT", attnL1, wo2, X)
    A16.release()
    A16.release()
    A16.release(); A32.release()
    emit_postln_inplace(bd)


def emit_l1_ffn(bd):
    nc, P, C = bd.nc, bd.P, bd.C
    S5, S5_b = bd.ps["S5"]
    M2, M2_b = bd.ps["M2"]
    A16, A32 = bd.A16, bd.A32
    X, Xb = bd.X, bd.Xb

    A16.mark(); A32.mark()
    h4 = A16.alloc("h4", 128, 11 * 1024)
    wg2 = [A16.alloc("fg0", 128, 16 * 128), A16.alloc("fg1", 128, 16 * 128)]
    wu2 = [A16.alloc("fu0", 128, 16 * 128), A16.alloc("fu1", 128, 16 * 128)]
    wd2 = [A16.alloc("fd0", 128, 11 * 128), A16.alloc("fd1", 128, 11 * 128)]
    gsc = A32.alloc("gsc", 128, 44)
    usc = A32.alloc("usc", 128, 44)
    dsc = A32.alloc("dsc", 128, 16)
    sgl = A32.alloc("sgl", 128, 1024)
    P.dma("dsp", gsc.ap(), bd.dram["f_gs"], writes=[gsc.r()])
    P.dma("dsp", usc.ap(), bd.dram["f_us"], writes=[usc.r()])
    P.dma("dsp", dsc.ap(), bd.dram["f_ds"], writes=[dsc.r()])
    fgr = bd.dram["f_gateT"].rearrange("(k p) o -> p k o", p=128)
    fur = bd.dram["f_upT"].rearrange("(k p) o -> p k o", p=128)
    fdr = bd.dram["f_downT"].rearrange("(k p) o -> p k o", p=128)
    ROT = [(M2, M2_b, 0), (S5, S5_b, 0), (S5, S5_b, 1024)]
    for q in range(4):
        for j in range(11):
            jj = q * 11 + j
            P.dma("dpool", wg2[j % 2].ap().rearrange("p (k o) -> p k o", k=16),
                  fgr[:, :, jj * 128:(jj + 1) * 128], writes=[wg2[j % 2].r()])
            P.dma("dpool", wu2[j % 2].ap().rearrange("p (k o) -> p k o", k=16),
                  fur[:, :, jj * 128:(jj + 1) * 128], writes=[wu2[j % 2].r()])
            gp, gp_b, go = ROT[jj % 3]
            up, up_b, uo = ROT[(jj + 2) % 3]
            for nt in range(2):
                for k in range(16):
                    P.mm(gp[:, go + nt * 512: go + (nt + 1) * 512],
                         wg2[j % 2][:, k * 128:(k + 1) * 128],
                         Xb[:, k * TOWN + nt * 512: k * TOWN + (nt + 1) * 512],
                         start=(k == 0), stop=(k == 15),
                         reads=[wg2[j % 2].r(k * 128, (k + 1) * 128),
                                Xb.r(k * TOWN + nt * 512, k * TOWN + (nt + 1) * 512)],
                         writes=[_r(gp_b, go + nt * 512, go + (nt + 1) * 512)])
            for nt in range(2):
                for k in range(16):
                    P.mm(up[:, uo + nt * 512: uo + (nt + 1) * 512],
                         wu2[j % 2][:, k * 128:(k + 1) * 128],
                         Xb[:, k * TOWN + nt * 512: k * TOWN + (nt + 1) * 512],
                         start=(k == 0), stop=(k == 15),
                         reads=[wu2[j % 2].r(k * 128, (k + 1) * 128),
                                Xb.r(k * TOWN + nt * 512, k * TOWN + (nt + 1) * 512)],
                         writes=[_r(up_b, uo + nt * 512, uo + (nt + 1) * 512)])
            P.op("dve", lambda e, jj=jj, gp=gp, gp_b=gp_b, go=go: e.tensor_scalar(
                    sgl.ap(), gp[:, go:go + 1024], gsc[:, jj:jj + 1], None,
                    op0=ALU.mult),
                 reads=[_r(gp_b, go, go + 1024), gsc.r()], writes=[sgl.r()])
            P.op("act", lambda e: e.activation(sgl.ap(), sgl.ap(), AF.Silu),
                 reads=[sgl.r()], writes=[sgl.r()])
            P.op("dve", lambda e, j=j, jj=jj, up=up, up_b=up_b, uo=uo: e.scalar_tensor_tensor(
                    h4[:, j * 1024:(j + 1) * 1024], up[:, uo:uo + 1024],
                    usc[:, jj:jj + 1], sgl.ap(),
                    op0=ALU.mult, op1=ALU.mult),
                 reads=[_r(up_b, uo, uo + 1024), usc.r(), sgl.r()],
                 writes=[h4.r(j * 1024, (j + 1) * 1024)])
        for oc in range(16):
            P.dma("dpool", wd2[oc % 2].ap().rearrange("p (k o) -> p k o", k=11),
                  fdr[:, q * 11:(q + 1) * 11, oc * 128:(oc + 1) * 128],
                  writes=[wd2[oc % 2].r()])
            ps, ps_b, po = ROT[oc % 3]
            for nt in range(2):
                for k in range(11):
                    P.mm(ps[:, po + nt * 512: po + (nt + 1) * 512],
                         wd2[oc % 2][:, k * 128:(k + 1) * 128],
                         h4[:, k * 1024 + nt * 512: k * 1024 + (nt + 1) * 512],
                         start=(k == 0), stop=(k == 10),
                         reads=[wd2[oc % 2].r(k * 128, (k + 1) * 128),
                                h4.r(k * 1024 + nt * 512, k * 1024 + (nt + 1) * 512)],
                         writes=[_r(ps_b, po + nt * 512, po + (nt + 1) * 512)])
            P.op("dve", lambda e, oc=oc, ps=ps, ps_b=ps_b, po=po: e.scalar_tensor_tensor(
                    X[:, oc * TOWN:(oc + 1) * TOWN], ps[:, po:po + 1024],
                    dsc[:, oc:oc + 1], X[:, oc * TOWN:(oc + 1) * TOWN],
                    op0=ALU.mult, op1=ALU.add),
                 reads=[_r(ps_b, po, po + 1024), dsc.r(), X.r(oc * TOWN, (oc + 1) * TOWN)],
                 writes=[X.r(oc * TOWN, (oc + 1) * TOWN)])
    A16.release(); A32.release()
    P.barrier()
    emit_postln_inplace(bd)



SHARED_SPECS = {
    "wq_downT": (DM, QR), "wq_upT": (QR, DM), "wq_ropeT": (QR, H0 * ROPE0),
    "wkv_downT": (DM, KVR), "wkv_upKT": (KVR, H0 * NOPE), "wkv_upVT": (KVR, H0 * VDIM),
    "wk_ropeT": (DM, ROPE0), "woT": (DM, DM),
    "q_norm_w": (128, 8), "kv_norm_w": (128, 4),
    "gate_wT": (DM, NEXP), "w13T": (NEXP, DM, 2 * IMOE), "w2T": (NEXP, IMOE, DM),
    "sw13T": (DM, 2 * ISH), "sw2T": (ISH, DM),
    "WqT": (DM, DM), "WkT": (DM, KVH * HD), "WvT": (DM, KVH * HD), "WoT": (DM, DM),
    "f_gateT": (DM, IFF), "f_upT": (DM, IFF), "f_downT": (IFF, DM),
    "f_gs": (128, 44), "f_us": (128, 44), "f_ds": (128, 16),
    "c0tab": (64, SEQ), "s0tab": (64, SEQ), "c1tab": (128, SEQ), "s1tab": (128, SEQ),
    "perm64": (64, 64), "perm128": (128, 128),
    "tri128": (128, 128), "tri1024": (128, 1024), "ones128": (128, 1), "ones_row": (1, 128),
    "ident128": (128, 128),
    "fconsts": (128, len(FCONSTS)),
}
CORE_SPECS = {
    "xT": (DM, SEQ), "xT_own": (DM, TOWN), "l0mask": (4, 128, 2304),
    "c0_own": (64, TOWN), "s0_own": (64, TOWN),
    "c1_own": (128, TOWN), "s1_own": (128, TOWN), "l1prefw": (128, 128),
    "c1tok": (TOWN, 128), "s1tok": (TOWN, 128),
}


def make_builder(phases):
    nc = bass.Bass("TRN2", target_bir_lowering=False)
    P = Prog()
    stack = ExitStack()
    bd = Builder(nc, P, stack)
    for name, shape in SHARED_SPECS.items():
        bd.din(name, shape)
    for name, shape in CORE_SPECS.items():
        bd.din(name, shape)
    bd.ps = {
        "S5": bd.psum("S5", [128, 2560]),
        "M2": bd.psum("M2", [128, 1024]),
        "A1": bd.psum("A1", [128, 512]),
    }
    bd.A16 = Arena(bd, "AR16", AR16_COLS, BF16)
    bd.A32 = Arena(bd, "AR32", AR32_COLS, F32)
    bd.Xb = bd.A16.alloc("Xb", 128, 16 * TOWN)
    bd.X = bd.A32.alloc("X", 128, 16 * TOWN)
    return nc, P, stack, bd


def build_program(phases, stream_in=None, stream_out=True, dbg=False):
    nc, P, stack, bd = make_builder(phases)
    bd.dbg = bool(dbg)
    emit_consts(bd)
    if stream_in:
        xi = bd.din("x_in", (DM, TOWN))
        P.dma("dsp", bd.X.ap().rearrange("p (k t) -> p k t", k=16),
              xi.rearrange("(k p) t -> p k t", p=128), writes=[bd.X.r()])
        P.dma("dpool", bd.Xb.ap().rearrange("p (k t) -> p k t", k=16),
              xi.rearrange("(k p) t -> p k t", p=128), writes=[bd.Xb.r()])
        P.barrier()
    for ph in phases:
        if ph == "l0attn":
            emit_l0_attn(bd)
        elif ph == "moe":
            emit_l0_moe(bd)
        elif ph == "l1attn":
            emit_l1_attn(bd)
        elif ph == "l1ffn":
            emit_l1_ffn(bd)
    finals = []
    if stream_out:
        xo = bd.dout("x_out", (DM, TOWN))
        finals.append(P.dma("dsp", xo.rearrange("(k p) t -> p k t", p=128),
                            bd.X.ap().rearrange("p (k t) -> p k t", k=16),
                            reads=[bd.X.r()]))
    P.mark_final(finals)
    P.finalize(nc)
    stack.close()
    return nc



def _core_inmaps(inputs):
    shared = prep_shared(inputs)
    maps = []
    for core in range(8):
        d = dict(shared)
        d.update(prep_core(inputs, shared, core))
        maps.append(d)
    return maps, shared


def _own_cols(core):
    own = OWN[core % 2]
    return np.concatenate([np.arange(g * 128, (g + 1) * 128) for g in own])


def kernel(**inputs):
    maps, _ = _core_inmaps(inputs)
    phases = ["l0attn", "moe", "l1attn", "l1ffn"]
    nc = build_program(phases)
    res = run_bass_kernel_spmd(nc, maps, core_ids=list(range(8)))
    out = np.zeros((BSZ, SEQ, DM), dtype=np.float32)
    for core in range(8):
        xo = np.asarray(res.results[core]["x_out"])
        out[core // 2, _own_cols(core), :] = xo.T
    return out


# version 10
# speedup vs baseline: 1.0842x; 1.0123x over previous
import math
from contextlib import ExitStack

import numpy as np
import ml_dtypes

import concourse.bass as bass
import concourse.mybir as mybir
from concourse.bass_utils import run_bass_kernel_spmd

F32 = mybir.dt.float32
BF16 = mybir.dt.bfloat16
AF = mybir.ActivationFunctionType
ALU = mybir.AluOpType
AX = mybir.AxisListType

BSZ, SEQ, DM = 4, 2048, 2048
NCH = 16
TOWN = 1024
OWN = [[0, 15, 2, 13, 4, 11, 6, 9], [1, 14, 3, 12, 5, 10, 7, 8]]
PAIR_AB = [(1, 15), (3, 13), (5, 11), (7, 9)]
H0, NOPE, ROPE0, VDIM = 16, 128, 64, 128
QR, KVR = 1024, 512
SCALE0 = 1.0 / math.sqrt(NOPE + ROPE0)
H1, KVH, HD = 16, 4, 128
REP = H1 // KVH
NEXP, IMOE = 16, 1408
ISH = 2816
TOPK = 6
IFF = 5632

BF = ml_dtypes.bfloat16

FCONSTS = [0.0, 1.0, -1.0, 0.5, -0.5, 1e-6, 1e-5, 1e-20, 2.0,
           1.0 / QR, 1.0 / KVR, 1.0 / DM, 1.0 / 1024.0,
           math.log(1.0 / math.sqrt(NOPE + ROPE0)), math.log(1.01),
           1e-30, 1.0 / 16.0]



class Buf:

    __slots__ = ("name", "t", "size", "gran", "wr", "rd")

    def __init__(self, name, t, size, gran=1):
        self.name = name
        self.t = t
        self.size = size
        self.gran = gran
        self.wr = []
        self.rd = []

    def _rg(self, lo, hi):
        if self.gran > 1:
            lo = (lo // self.gran) * self.gran
            hi = ((hi + self.gran - 1) // self.gran) * self.gran
        return lo, hi


def _r(buf, lo=0, hi=None):
    return (buf, lo, buf.size if hi is None else hi)


class Prog:

    CSTREAMS = ("pe", "act", "dve", "pool")
    DSTREAMS = ("dsp", "dpool", "dact")
    ENG_OF = {"pe": "tensor", "act": "scalar", "dve": "vector", "pool": "gpsimd",
              "dsp": "sync", "dpool": "gpsimd", "dact": "scalar"}

    def __init__(self):
        self.ops = []
        self.pos = {s: [] for s in self.CSTREAMS + self.DSTREAMS}
        self.bufs = []
        self.bar = [-1]
        self.bar_ops = []
        self._last_by_stream = {s: -1 for s in self.CSTREAMS + self.DSTREAMS}

    def buf(self, name, t, size, gran=1):
        b = Buf(name, t, size, gran)
        self.bufs.append(b)
        return b

    def op(self, stream, fn, reads=(), writes=(), fence=False):
        idx = len(self.ops)
        deps = set()
        blast = self.bar[-1]
        for (b, lo, hi) in reads:
            lo, hi = b._rg(lo, hi)
            per = {}
            for (wl, wh, wi) in reversed(b.wr):
                if wi <= blast:
                    break
                if wl < hi and lo < wh:
                    s = self.ops[wi]["stream"]
                    if s not in per:
                        per[s] = wi
            deps.update(per.values())
        for (b, lo, hi) in writes:
            lo, hi = b._rg(lo, hi)
            per = {}
            for (wl, wh, wi) in reversed(b.wr):
                if wi <= blast:
                    break
                if wl < hi and lo < wh:
                    s = self.ops[wi]["stream"]
                    if s not in per:
                        per[s] = wi
            deps.update(per.values())
            per = {}
            for (rl, rh, ri) in reversed(b.rd):
                if ri <= blast:
                    break
                if rl < hi and lo < rh:
                    s = self.ops[ri]["stream"]
                    if s not in per or per[s] < ri:
                        per[s] = ri
            deps.update(per.values())
        for (b, lo, hi) in reads:
            lo, hi = b._rg(lo, hi)
            b.rd.append((lo, hi, idx))
        for (b, lo, hi) in writes:
            lo, hi = b._rg(lo, hi)
            b.wr.append((lo, hi, idx))
        if self.bar_ops:
            for j in self.bar_ops[-1]:
                if j >= 0:
                    deps.add(j)
        deps = {d for d in deps
                if self.ops[d]["stream"] != stream or self.ops[d]["fence"]}
        self.ops.append({"stream": stream, "fn": fn, "deps": deps, "fence": fence})
        self.pos[stream].append(idx)
        self._last_by_stream[stream] = idx
        return idx

    def barrier(self):
        self.bar.append(len(self.ops) - 1)
        self.bar_ops.append([self._last_by_stream[s] for s in self.CSTREAMS + self.DSTREAMS])
        for b in self.bufs:
            if len(b.wr) > 64 or len(b.rd) > 64:
                blast = self.bar[-1]
                b.wr = [e for e in b.wr if e[2] > blast] or b.wr[-1:]
                b.rd = [e for e in b.rd if e[2] > blast]

    def finalize(self, nc, final_stream="dsp"):
        ops = self.ops
        flagged = [False] * len(ops)
        for i, o in enumerate(ops):
            if o["stream"] in self.DSTREAMS:
                flagged[i] = True
            for d in o["deps"]:
                flagged[d] = True
        for i in self._final:
            flagged[i] = True
        NL = 4
        cnt = {}
        thr = [0] * len(ops)
        key = [None] * len(ops)
        dma_seq = {s: 0 for s in self.DSTREAMS}
        for i, o in enumerate(ops):
            s = o["stream"]
            if s in self.DSTREAMS:
                lane = dma_seq[s] % NL
                dma_seq[s] += 1
                kk = (s, lane)
                if flagged[i]:
                    cnt[kk] = cnt.get(kk, 0) + 1
                    thr[i] = cnt[kk] * 16
                    key[i] = kk
            else:
                if flagged[i]:
                    cnt[s] = cnt.get(s, 0) + 1
                    thr[i] = cnt[s]
                    key[i] = s
        sems = {}
        stack = ExitStack()
        for kk in cnt:
            name = f"sem_{kk}" if isinstance(kk, str) else f"sem_{kk[0]}_{kk[1]}"
            sems[kk] = stack.enter_context(nc.semaphore(name))
        blk = stack.enter_context(nc.Block())
        know = {s: {} for s in self.pos}
        snap = [None] * len(ops)
        emit = {s: [] for s in self.pos}
        cur = {s: 0 for s in self.pos}
        for i, o in enumerate(ops):
            s = o["stream"]
            k = know[s]
            for d in sorted(o["deps"]):
                ds = ops[d]["stream"]
                if ds == s and not ops[d]["fence"]:
                    continue
                v = thr[d]
                dk = key[d]
                assert v > 0, f"dep {d} not flagged"
                if k.get(dk, 0) < v:
                    emit[s].append(("wait", dk, v))
                    k[dk] = max(k.get(dk, 0), v)
                    sn = snap[d]
                    if sn:
                        for s2, v2 in sn.items():
                            if k.get(s2, 0) < v2:
                                k[s2] = v2
            emit[s].append(("op", i))
            if flagged[i]:
                sn = dict(k)
                sn[key[i]] = thr[i]
                snap[i] = sn
        fin = {}
        for i in self._final:
            fin[key[i]] = max(fin.get(key[i], 0), thr[i])
        for kk, v in fin.items():
            emit[final_stream].append(("wait", kk, v))

        prog = self

        def run_stream(eng, s):
            for item in emit[s]:
                if item[0] == "wait":
                    eng.wait_ge(sems[item[1]], item[2])
                else:
                    i = item[1]
                    inst = ops[i]["fn"](eng)
                    if flagged[i]:
                        inst.then_inc(sems[key[i]], 16 if s in self.DSTREAMS else 1)

        def merged(s_compute, s_dma):
            items = []
            ic, idma = 0, 0
            ec, ed = emit[s_compute], emit[s_dma]
            def chunks(lst):
                out = []
                pend = []
                for it in lst:
                    if it[0] == "wait":
                        pend.append(it)
                    else:
                        out.append((it[1], pend + [it]))
                        pend = []
                if pend:
                    out.append((10 ** 9, pend))
                return out
            cc, cd = chunks(ec), chunks(ed)
            res = []
            i = j = 0
            while i < len(cc) or j < len(cd):
                if j >= len(cd) or (i < len(cc) and cc[i][0] < cd[j][0]):
                    res.extend((s_compute, it) for it in cc[i][1]); i += 1
                else:
                    res.extend((s_dma, it) for it in cd[j][1]); j += 1
            return res

        pool_items = merged("pool", "dpool")
        act_items = merged("act", "dact")

        @blk.tensor
        def _(t):
            run_stream(t, "pe")

        @blk.vector
        def _(v):
            run_stream(v, "dve")

        @blk.scalar
        def _(sc):
            for s, item in act_items:
                if item[0] == "wait":
                    sc.wait_ge(sems[item[1]], item[2])
                else:
                    i = item[1]
                    inst = ops[i]["fn"](sc)
                    if flagged[i]:
                        inst.then_inc(sems[key[i]], 16 if s in self.DSTREAMS else 1)

        @blk.gpsimd
        def _(g):
            for s, item in pool_items:
                if item[0] == "wait":
                    g.wait_ge(sems[item[1]], item[2])
                else:
                    i = item[1]
                    inst = ops[i]["fn"](g)
                    if flagged[i]:
                        inst.then_inc(sems[key[i]], 16 if s in self.DSTREAMS else 1)

        @blk.sync
        def _(sy):
            run_stream(sy, "dsp")

        stack.close()

    _final = ()
    _extra_final = None

    def mark_extra_final(self, idx):
        if self._extra_final is None:
            self._extra_final = []
        self._extra_final.append(idx)

    def mark_final(self, idxs):
        self._final = list(idxs) + (self._extra_final or [])

    def dma(self, q, out_ap, in_ap, reads=(), writes=()):
        return self.op(q, lambda e: e.dma_start(out=out_ap, in_=in_ap),
                       reads=reads, writes=writes)

    def mm(self, out_ap, lhsT_ap, rhs_ap, start, stop, reads=(), writes=()):
        return self.op("pe", lambda e: e.matmul(out_ap, lhsT_ap, rhs_ap,
                                                start=start, stop=stop),
                       reads=reads, writes=writes)



def _t(a):
    return np.ascontiguousarray(np.asarray(a).T)


def _rope_perm(n):
    return np.concatenate([np.arange(0, n, 2), np.arange(1, n, 2)])


def prep_shared(inp):
    d = {}
    d["wq_downT"] = _t(inp["L0_attn_wq_down"])
    d["wq_upT"] = _t(inp["L0_attn_wq_up"])
    wqr = np.asarray(inp["L0_attn_wq_rope"]).reshape(H0, ROPE0, QR)
    perm = _rope_perm(ROPE0)
    wqr = wqr[:, perm, :].reshape(H0 * ROPE0, QR)
    d["wq_ropeT"] = _t(wqr)
    d["wkv_downT"] = _t(inp["L0_attn_wkv_down"])
    wkv = np.asarray(inp["L0_attn_wkv_up"]).reshape(H0, NOPE + VDIM, KVR)
    d["wkv_upKT"] = _t(wkv[:, :NOPE, :].reshape(H0 * NOPE, KVR))
    d["wkv_upVT"] = _t(wkv[:, NOPE:, :].reshape(H0 * VDIM, KVR))
    wkr = np.asarray(inp["L0_attn_wk_rope"])[perm, :]
    d["wk_ropeT"] = _t(wkr)
    d["woT"] = _t(inp["L0_attn_wo"])
    d["q_norm_w"] = np.ascontiguousarray(
        np.asarray(inp["L0_attn_q_norm_w"]).reshape(QR // 128, 128).T)
    d["kv_norm_w"] = np.ascontiguousarray(
        np.asarray(inp["L0_attn_kv_norm_w"]).reshape(KVR // 128, 128).T)
    d["gate_wT"] = _t(inp["L0_ffn_gate_w"])
    d["w13T"] = np.ascontiguousarray(np.asarray(inp["L0_ffn_w13"]).transpose(0, 2, 1))
    d["w2T"] = np.ascontiguousarray(np.asarray(inp["L0_ffn_w2"]).transpose(0, 2, 1))
    d["sw13T"] = _t(inp["L0_ffn_sw13"])
    d["sw2T"] = _t(inp["L0_ffn_sw2"])
    d["WqT"] = _t(inp["L1_attn_Wq"])
    d["WkT"] = _t(inp["L1_attn_Wk"])
    d["WvT"] = _t(inp["L1_attn_Wv"])
    d["WoT"] = _t(inp["L1_attn_Wo"])
    d["f_gateT"] = _t(inp["L1_ffn_gate_w"])
    d["f_upT"] = _t(inp["L1_ffn_up_w"])
    d["f_downT"] = _t(inp["L1_ffn_down_w"])
    d["f_gs"] = np.ascontiguousarray(
        np.asarray(inp["L1_ffn_gate_s"]).reshape(IFF // 128, 128).T)
    d["f_us"] = np.ascontiguousarray(
        np.asarray(inp["L1_ffn_up_s"]).reshape(IFF // 128, 128).T)
    d["f_ds"] = np.ascontiguousarray(
        np.asarray(inp["L1_ffn_down_s"]).reshape(DM // 128, 128).T)
    invf0 = 1.0 / (10000.0 ** (np.arange(0, ROPE0, 2, dtype=np.float64) / ROPE0))
    tpos = np.arange(SEQ, dtype=np.float64)
    fr0 = np.outer(invf0, tpos)
    c0 = np.cos(fr0).astype(np.float32)
    s0 = np.sin(fr0).astype(np.float32)
    d["c0tab"] = np.ascontiguousarray(np.concatenate([c0, c0], 0))
    d["s0tab"] = np.ascontiguousarray(np.concatenate([-s0, s0], 0))
    d["perm64"] = np.ascontiguousarray(
        np.block([[np.zeros((32, 32)), np.eye(32)],
                  [np.eye(32), np.zeros((32, 32))]]).astype(np.float32))
    invf1 = 1.0 / (10000.0 ** (np.arange(0, HD, 2, dtype=np.float64) / HD))
    fr1 = np.outer(invf1, tpos)
    c1 = np.cos(fr1).astype(np.float32)
    s1 = np.sin(fr1).astype(np.float32)
    d["c1tab"] = np.ascontiguousarray(np.concatenate([c1, c1], 0))
    d["s1tab"] = np.ascontiguousarray(np.concatenate([s1, s1], 0))
    eye64 = np.eye(64)
    z64 = np.zeros((64, 64))
    P1 = np.block([[z64, -eye64], [eye64, z64]])
    d["perm128"] = np.ascontiguousarray(P1.T.astype(np.float32))
    tri = (np.arange(128)[:, None] <= np.arange(128)[None, :]).astype(np.float32)
    d["tri128"] = np.ascontiguousarray(tri)
    d["tri1024"] = np.ascontiguousarray(np.tile(tri, (1, 8)))
    d["ones128"] = np.ones((128, 1), dtype=np.float32)
    d["ones_row"] = np.ones((1, 128), dtype=np.float32)
    d["ident128"] = np.eye(128, dtype=np.float32)
    d["fconsts"] = np.ascontiguousarray(
        np.tile(np.asarray(FCONSTS, dtype=np.float32).reshape(1, -1), (128, 1)))
    return d


def prep_core(inp, shared, core):
    par = core % 2
    b = core // 2
    own = OWN[par]
    x = np.asarray(inp["L0_attn_x"])[b]
    xT = _t(x)
    d = {}
    d["xT"] = xT
    xT_own = np.concatenate([xT[:, g * 128:(g + 1) * 128] for g in own], axis=1)
    d["xT_own"] = np.ascontiguousarray(xT_own)
    masks = []
    for p, (A, Bc) in enumerate(PAIR_AB):
        a_g, b_g = own[2 * p], own[2 * p + 1]
        cols = []
        for kc in range(Bc + 1):
            w = 256 if kc <= A else 128
            m = np.zeros((128, w), dtype=np.float32)
            kpos = kc * 128 + np.arange(128)
            if w == 256:
                qa = a_g * 128 + np.arange(128)
                qb = b_g * 128 + np.arange(128)
                m[:, :128] = (kpos[:, None] <= qa[None, :])
                m[:, 128:] = (kpos[:, None] <= qb[None, :])
            else:
                qb = b_g * 128 + np.arange(128)
                m[:, :] = (kpos[:, None] <= qb[None, :])
            cols.append(m)
        masks.append(np.concatenate(cols, axis=1))
    d["l0mask"] = np.ascontiguousarray(np.stack(masks, 0))
    ocols = np.concatenate([np.arange(g * 128, (g + 1) * 128) for g in own])
    d["c0_own"] = np.ascontiguousarray(shared["c0tab"][:, ocols])
    d["s0_own"] = np.ascontiguousarray(shared["s0tab"][:, ocols])
    d["c1_own"] = np.ascontiguousarray(shared["c1tab"][:, ocols])
    d["s1_own"] = np.ascontiguousarray(shared["s1tab"][:, ocols])
    d["c1tok"] = np.ascontiguousarray(shared["c1tab"][:, ocols].T)
    d["s1tok"] = np.ascontiguousarray(shared["s1tab"][:, ocols].T)
    slot_global = [OWN[0][i] for i in range(8)] + [OWN[1][i] for i in range(8)]
    wmat = np.zeros((128, 8 * 16), dtype=np.float32)
    for c in range(8):
        gc = own[c]
        for sl in range(16):
            if slot_global[sl] < gc:
                wmat[:, c * 16 + sl] = 1.0
    d["l1prefw"] = np.ascontiguousarray(wmat)
    return d



class Builder:
    def __init__(self, nc, P, stack):
        self.nc = nc
        self.P = P
        self.stack = stack
        self.dram = {}
        self.sb = {}

    def din(self, name, shape, dtype=F32):
        t = self.nc.dram_tensor(name, list(shape), dtype, kind="ExternalInput")
        self.dram[name] = t.ap()
        return self.dram[name]

    def dout(self, name, shape, dtype=F32):
        t = self.nc.dram_tensor(name, list(shape), dtype, kind="ExternalOutput")
        self.dram[name] = t.ap()
        return self.dram[name]

    def dint(self, name, shape, dtype=F32):
        t = self.nc.dram_tensor(name, list(shape), dtype, kind="Internal")
        self.dram[name] = t.ap()
        return self.dram[name]

    def sbuf(self, name, shape, dtype=F32):
        t = self.stack.enter_context(self.nc.sbuf_tensor(name, list(shape), dtype))
        b = self.P.buf(name, t, shape[-1] if len(shape) == 2 else int(np.prod(shape[1:])))
        self.sb[name] = (t, b)
        return t, b

    def psum(self, name, shape, dtype=F32):
        t = self.stack.enter_context(self.nc.psum_tensor(name, list(shape), dtype))
        b = self.P.buf(name, t, shape[-1], gran=512)
        self.sb[name] = (t, b)
        return t, b



AR16_COLS = 54272
AR32_COLS = 23104


class _ArView:

    __slots__ = ("t", "off", "part", "cols", "b")

    def __init__(self, t, off, part, cols, b):
        self.t = t
        self.off = off
        self.part = part
        self.cols = cols
        self.b = b

    def __getitem__(self, idx):
        p, c = idx
        cs = c.start or 0
        ce = c.stop if c.stop is not None else self.cols
        assert ce <= self.cols
        return self.t[p, self.off + cs: self.off + ce]

    def ap(self):
        return self.t[0:self.part, self.off:self.off + self.cols]

    def r(self, lo=0, hi=None):
        return (self.b, lo, self.cols if hi is None else hi)


class Arena:
    def __init__(self, bd, name, cols, dtype):
        self.bd = bd
        self.name = name
        self.t = bd.stack.enter_context(bd.nc.sbuf_tensor(name, [128, cols], dtype))
        self.cols = cols
        self.off = 0
        self.marks = []

    def alloc(self, name, part, cols):
        assert self.off + cols <= self.cols, \
            f"arena {self.name}/{name}: {self.off}+{cols} > {self.cols}"
        v = self.view_at(name, self.off, part, cols)
        self.off += cols
        return v

    def view_at(self, name, off, part, cols):
        b = self.bd.P.buf(f"{self.name}.{name}", self.t, cols)
        return _ArView(self.t, off, part, cols, b)

    def mark(self):
        self.marks.append(self.off)

    def release(self):
        self.off = self.marks.pop()



def _seg_layout(pair):
    A, Bc = PAIR_AB[pair]
    segs = []
    off = 0
    for kc in range(Bc + 1):
        w = 256 if kc <= A else 128
        segs.append((kc, w, off))
        off += w
    assert off == 2304
    return segs


def emit_consts(bd):
    P = bd.P
    c = {}

    def ld(name, shape, dt, src_ap):
        t, b = bd.sbuf("c_" + name, shape, dt)
        qq = "dpool" if dt == BF16 else "dsp"
        P.dma(qq, t[:], src_ap, writes=[_r(b)])
        c[name] = t

    ld("ones_b", [128, 1], BF16, bd.dram["ones128"])
    ld("ones_f", [128, 1], F32, bd.dram["ones128"])
    ld("onesrow_f", [1, 128], F32, bd.dram["ones_row"])
    ld("tri", [128, 128], BF16, bd.dram["tri128"])
    ld("perm64", [64, 64], BF16, bd.dram["perm64"])
    ld("perm128", [128, 128], BF16, bd.dram["perm128"])
    ld("c0_own", [64, TOWN], BF16, bd.dram["c0_own"])
    ld("s0_own", [64, TOWN], BF16, bd.dram["s0_own"])
    ld("ident_b", [128, 128], BF16, bd.dram["ident128"])
    ld("trirep", [128, 1024], BF16, bd.dram["tri1024"])
    ld("ident_f", [128, 128], F32, bd.dram["ident128"])
    ld("onesrow_b", [1, 128], BF16, bd.dram["ones_row"])
    ld("qnw", [128, 8], F32, bd.dram["q_norm_w"])
    ld("kvnw", [128, 4], F32, bd.dram["kv_norm_w"])
    fv, fv_b = bd.sbuf("c_fvals", [128, len(FCONSTS)], F32)
    P.dma("dsp", fv[:], bd.dram["fconsts"], writes=[_r(fv_b)])
    for i, v in enumerate(FCONSTS):
        bd.nc.const_aps.register(F32, v, fv[:, i:i + 1])
        bd.nc.const_aps.register(F32, float(np.float32(v)), fv[:, i:i + 1])
    bd.C = c
    P.barrier()


def _postln(bd, resid_from_psum_fn, n_oc=16):
    pass


def emit_l0_attn(bd):
    nc, P, C = bd.nc, bd.P, bd.C
    S5, S5_b = bd.ps["S5"]
    M2, M2_b = bd.ps["M2"]
    A1, A1_b = bd.ps["A1"]
    A16, A32 = bd.A16, bd.A32
    X = bd.X
    Xb = bd.Xb

    A16.mark(); A32.mark()
    qcn = A16.alloc("qcn", 128, 8 * TOWN)
    kvn = A16.alloc("kvn", 128, 4 * SEQ)
    kra = A16.alloc("kra", 65, SEQ)
    krss = A32.alloc("krss", 1, SEQ)
    attnout = A16.view_at("attnout", Xb.off, 128, 16 * TOWN)
    xbuf = A16.view_at("xbuf", Xb.off, 128, 16 * TOWN)

    A16.mark(); A32.mark()
    wdbl = [A16.alloc("wdbl0", 128, 16 * 128), A16.alloc("wdbl1", 128, 16 * 128)]
    pre = A16.alloc("pre", 128, 8 * 1024)
    c0f = A16.alloc("c0f", 64, SEQ)
    s0f = A16.alloc("s0f", 64, SEQ)
    krx = A16.alloc("krx", 64, TOWN)
    sq = A32.alloc("sqB", 128, 1024)
    rt1 = A32.alloc("rt1", 64, TOWN)
    rt2 = A32.alloc("rt2", 64, TOWN)
    rrow = A32.alloc("rrowB", 1, TOWN)

    P.dma("dpool", c0f.ap(), bd.dram["c0tab"], writes=[c0f.r()])
    P.dma("dpool", s0f.ap(), bd.dram["s0tab"], writes=[s0f.r()])
    P.dma("dpool", xbuf.ap().rearrange("p (k t) -> p k t", k=16),
          bd.dram["xT_own"].rearrange("(k p) t -> p k t", p=128),
          writes=[xbuf.r()])

    def proj_norm(wT_name, nout, rank, norm_w, ln_bias, out, out_T, hf):
        for j in range(nout):
            wb = wdbl[j % 2]
            P.dma("dpool", wb.ap().rearrange("p (k o) -> p k o", k=16),
                  bd.dram[wT_name].rearrange("(k p) o -> p k o", p=128)[:, :, j * 128:(j + 1) * 128],
                  writes=[wb.r()])
            for nt in range(2):
                for k in range(16):
                    P.mm(M2[:, nt * 512:(nt + 1) * 512],
                         wb[:, k * 128:(k + 1) * 128],
                         xbuf[:, k * TOWN + nt * 512: k * TOWN + (nt + 1) * 512],
                         start=(k == 0), stop=(k == 15),
                         reads=[wb.r(k * 128, (k + 1) * 128),
                                xbuf.r(k * TOWN + nt * 512, k * TOWN + (nt + 1) * 512)],
                         writes=[_r(M2_b, nt * 512, (nt + 1) * 512)])
            P.op("act", lambda e, j=j: e.copy(pre[:, j * 1024:(j + 1) * 1024], M2[:]),
                 reads=[_r(M2_b)], writes=[pre.r(j * 1024, (j + 1) * 1024)])
            P.op("dve", lambda e, j=j: e.tensor_tensor(
                    sq.ap(), pre[:, j * 1024:(j + 1) * 1024],
                    pre[:, j * 1024:(j + 1) * 1024], op=ALU.mult),
                 reads=[pre.r(j * 1024, (j + 1) * 1024)], writes=[sq.r()])
            for nt in range(2):
                P.mm(S5[0:1, nt * 512:(nt + 1) * 512], C["ones_f"][:],
                     sq[:, nt * 512:(nt + 1) * 512],
                     start=(j == 0), stop=(j == nout - 1),
                     reads=[sq.r(nt * 512, (nt + 1) * 512)],
                     writes=[_r(S5_b, nt * 512, (nt + 1) * 512)])
        P.op("act", lambda e: e.activation(rrow.ap(), S5[0:1, 0:1024], AF.Ln,
                                           scale=1.0 / rank, bias=1e-6),
             reads=[_r(S5_b, 0, 1024)], writes=[rrow.r()])
        P.op("act", lambda e: e.activation(rrow.ap(), rrow.ap(), AF.Exp,
                                           scale=-0.5, bias=ln_bias),
             reads=[rrow.r()], writes=[rrow.r()])
        for nt in range(2):
            P.mm(M2[:, nt * 512:(nt + 1) * 512], C["onesrow_f"][:],
                 rrow[0:1, nt * 512:(nt + 1) * 512], start=True, stop=True,
                 reads=[rrow.r()], writes=[_r(M2_b, nt * 512, (nt + 1) * 512)])
        for j in range(nout):
            P.op("dve", lambda e, j=j: e.scalar_tensor_tensor(
                    out[:, j * out_T + hf * 1024: j * out_T + hf * 1024 + 1024],
                    pre[:, j * 1024:(j + 1) * 1024],
                    norm_w[:, j:j + 1], M2[:],
                    op0=ALU.mult, op1=ALU.mult),
                 reads=[pre.r(j * 1024, (j + 1) * 1024), _r(M2_b)],
                 writes=[out.r(j * out_T + hf * 1024, j * out_T + hf * 1024 + 1024)])
        P.barrier()

    proj_norm("wq_downT", 8, QR, C["qnw"], math.log(SCALE0), qcn, TOWN, 0)

    for hf in range(2):
        P.dma("dpool", xbuf.ap().rearrange("p (k t) -> p k t", k=16),
              bd.dram["xT"].rearrange("(k p) t -> p k t", p=128)[:, :, hf * 1024:(hf + 1) * 1024],
              writes=[xbuf.r()])
        proj_norm("wkv_downT", 4, KVR, C["kvnw"], 0.0, kvn, SEQ, hf)
        wb = wdbl[0]
        P.dma("dpool", wb.ap().rearrange("p (k o) -> p k o", k=16)[:, :, 0:64],
              bd.dram["wk_ropeT"].rearrange("(k p) o -> p k o", p=128),
              writes=[wb.r()])
        for nt in range(2):
            for k in range(16):
                P.mm(M2[0:64, nt * 512:(nt + 1) * 512],
                     wb[:, k * 128:k * 128 + 64],
                     xbuf[:, k * TOWN + nt * 512: k * TOWN + (nt + 1) * 512],
                     start=(k == 0), stop=(k == 15),
                     reads=[wb.r(k * 128, k * 128 + 64),
                            xbuf.r(k * TOWN + nt * 512, k * TOWN + (nt + 1) * 512)],
                     writes=[_r(M2_b, nt * 512, (nt + 1) * 512)])
        P.op("act", lambda e: e.copy(krx.ap(), M2[0:64, 0:1024]),
             reads=[_r(M2_b)], writes=[krx.r()])
        for nt in range(2):
            P.mm(S5[0:64, nt * 512:(nt + 1) * 512], C["perm64"][:],
                 krx[0:64, nt * 512:(nt + 1) * 512], start=True, stop=True,
                 reads=[krx.r(nt * 512, (nt + 1) * 512)],
                 writes=[_r(S5_b, nt * 512, (nt + 1) * 512)])
        P.op("dve", lambda e, hf=hf: e.tensor_tensor(
                rt1.ap(), krx.ap(), c0f[0:64, hf * 1024:(hf + 1) * 1024], op=ALU.mult),
             reads=[krx.r(), c0f.r()], writes=[rt1.r()])
        P.op("dve", lambda e, hf=hf: e.tensor_tensor(
                rt2.ap(), S5[0:64, 0:1024], s0f[0:64, hf * 1024:(hf + 1) * 1024],
                op=ALU.mult),
             reads=[_r(S5_b, 0, 1024), s0f.r()], writes=[rt2.r()])
        P.op("dve", lambda e, hf=hf: e.tensor_tensor(
                kra[0:64, hf * 1024:(hf + 1) * 1024], rt1.ap(), rt2.ap(), op=ALU.add),
             reads=[rt1.r(), rt2.r()],
             writes=[kra.r(hf * 1024, (hf + 1) * 1024)])
        P.op("dve", lambda e, hf=hf: e.tensor_tensor(
                rt1.ap(), kra[0:64, hf * 1024:(hf + 1) * 1024],
                kra[0:64, hf * 1024:(hf + 1) * 1024], op=ALU.mult),
             reads=[kra.r(hf * 1024, (hf + 1) * 1024)], writes=[rt1.r()])
        for nt in range(2):
            P.mm(M2[0:1, nt * 512:(nt + 1) * 512], C["ones_f"][0:64, :],
                 rt1[0:64, nt * 512:(nt + 1) * 512], start=True, stop=True,
                 reads=[rt1.r(nt * 512, (nt + 1) * 512)],
                 writes=[_r(M2_b, nt * 512, (nt + 1) * 512)])
        P.op("dve", lambda e, hf=hf: e.tensor_copy(
                krss[0:1, hf * 1024:(hf + 1) * 1024], M2[0:1, 0:1024]),
             reads=[_r(M2_b, 0, 1024)], writes=[krss.r(hf * 1024, (hf + 1) * 1024)])
        P.barrier()
    P.op("dve", lambda e: e.memset(kra[64:65, 0:SEQ], -1.0), writes=[kra.r()])
    A16.release(); A32.release()
    P.barrier()

    A16.mark(); A32.mark()
    NG = 16
    v_g = A16.alloc("v_g", 128, 16 * 128)
    kh_g = A16.alloc("kh_g", 128, SEQ)
    qn_g = A16.alloc("qn_g", 128, TOWN)
    qa_g = A16.alloc("qa_g", 65, TOWN)
    qx = A16.alloc("qx", 64, TOWN)
    PT = A16.alloc("PT", 128, 2304)
    msk = A16.alloc("msk", 128, 2304)
    wqu2 = [A16.alloc("wquA", 128, 8 * 128), A16.alloc("wquB", 128, 8 * 128)]
    wqr2 = [A16.alloc("wqrA", 128, 8 * 64), A16.alloc("wqrB", 128, 8 * 64)]
    wkK2 = [A16.alloc("wkKA", 128, 4 * 128), A16.alloc("wkKB", 128, 4 * 128)]
    wkV2 = [A16.alloc("wkVA", 128, 4 * 128), A16.alloc("wkVB", 128, 4 * 128)]
    sqc = A32.alloc("sqC", 128, 1024)
    rt1c = A32.alloc("rt1C", 64, TOWN)
    rt2c = A32.alloc("rt2C", 64, TOWN)
    cqr = A32.alloc("cqr", 1, TOWN)
    mrow = A32.alloc("mrow", 1, 8)
    bcsb = A32.alloc("bcsb", 128, 256)
    rcp = A32.alloc("rcp", 1, 256)
    cqb_t, cqb_b = bd.sbuf("cqb", [1, TOWN], BF16)

    def load_head_weights(h):
        wqu, wqr, wkK, wkV = wqu2[h % 2], wqr2[h % 2], wkK2[h % 2], wkV2[h % 2]
        P.dma("dpool", wqu.ap().rearrange("p (k o) -> p k o", k=8),
              bd.dram["wq_upT"].rearrange("(k p) o -> p k o", p=128)[:, :, h * 128:(h + 1) * 128],
              writes=[wqu.r()])
        P.dma("dpool", wqr.ap().rearrange("p (k o) -> p k o", k=8),
              bd.dram["wq_ropeT"].rearrange("(k p) o -> p k o", p=128)[:, :, h * 64:(h + 1) * 64],
              writes=[wqr.r()])
        P.dma("dpool", wkK.ap().rearrange("p (k o) -> p k o", k=4),
              bd.dram["wkv_upKT"].rearrange("(k p) o -> p k o", p=128)[:, :, h * 128:(h + 1) * 128],
              writes=[wkK.r()])
        P.dma("dpool", wkV.ap().rearrange("p (k o) -> p k o", k=4),
              bd.dram["wkv_upVT"].rearrange("(k p) o -> p k o", p=128)[:, :, h * 128:(h + 1) * 128],
              writes=[wkV.r()])

    load_head_weights(0)
    for g in range(NG):
        h = g
        wqu, wqr, wkK, wkV = wqu2[h % 2], wqr2[h % 2], wkK2[h % 2], wkV2[h % 2]
        if h + 1 < NG:
            load_head_weights(h + 1)
        for tc in range(16):
            for kch in range(4):
                P.mm(M2[:, 0:128],
                     kvn[:, kch * SEQ + tc * 128: kch * SEQ + (tc + 1) * 128],
                     wkV[:, kch * 128:(kch + 1) * 128],
                     start=(kch == 0), stop=(kch == 3),
                     reads=[kvn.r(kch * SEQ + tc * 128, kch * SEQ + (tc + 1) * 128),
                            wkV.r(kch * 128, (kch + 1) * 128)],
                     writes=[_r(M2_b, 0, 128)])
            P.op("act", lambda e, tc=tc: e.copy(v_g[:, tc * 128:(tc + 1) * 128],
                                                M2[:, 0:128]),
                 reads=[_r(M2_b, 0, 128)], writes=[v_g.r(tc * 128, (tc + 1) * 128)])
        for nt in range(4):
            for kch in range(4):
                P.mm(S5[:, nt * 512:(nt + 1) * 512],
                     wkK[:, kch * 128:(kch + 1) * 128],
                     kvn[:, kch * SEQ + nt * 512: kch * SEQ + (nt + 1) * 512],
                     start=(kch == 0), stop=(kch == 3),
                     reads=[wkK.r(kch * 128, (kch + 1) * 128),
                            kvn.r(kch * SEQ + nt * 512, kch * SEQ + (nt + 1) * 512)],
                     writes=[_r(S5_b, nt * 512, (nt + 1) * 512)])
        P.op("act", lambda e: e.copy(kh_g.ap(), S5[:, 0:SEQ]),
             reads=[_r(S5_b, 0, SEQ)], writes=[kh_g.r()])
        for half in range(2):
            P.op("dve", lambda e, half=half: e.tensor_tensor(
                    sqc.ap(), kh_g[:, half * 1024: half * 1024 + 1024],
                    kh_g[:, half * 1024: half * 1024 + 1024], op=ALU.mult),
                 reads=[kh_g.r(half * 1024, half * 1024 + 1024)],
                 writes=[sqc.r()])
            for nt in range(2):
                P.mm(S5[0:1, half * 1024 + nt * 512: half * 1024 + (nt + 1) * 512],
                     C["ones_f"][:], sqc[:, nt * 512:(nt + 1) * 512],
                     start=True, stop=True,
                     reads=[sqc.r(nt * 512, (nt + 1) * 512)],
                     writes=[_r(S5_b, half * 1024 + nt * 512, half * 1024 + (nt + 1) * 512)])
        P.op("dve", lambda e: e.tensor_tensor(
                S5[0:1, 0:SEQ], S5[0:1, 0:SEQ], krss[0:1, 0:SEQ], op=ALU.add),
             reads=[_r(S5_b, 0, SEQ), krss.r()], writes=[_r(S5_b, 0, SEQ)])
        P.op("dve", lambda e: e.tensor_reduce(
                mrow[0:1, 0:1], S5[0:1, 0:SEQ], axis=AX.X, op=ALU.max),
             reads=[_r(S5_b, 0, SEQ)], writes=[mrow.r(0, 1)], fence=True)
        P.op("act", lambda e: e.activation(
                mrow[0:1, 1:2], mrow[0:1, 0:1], AF.Ln, bias=1e-20),
             reads=[mrow.r(0, 1)], writes=[mrow.r(1, 2)])
        P.op("dve", lambda e: e.tensor_scalar(
                mrow[0:1, 2:3], mrow[0:1, 1:2],
                0.5, math.log(1.01), op0=ALU.mult, op1=ALU.add),
             reads=[mrow.r(1, 2)], writes=[mrow.r(2, 3)])
        for nt in range(2):
            for k in range(8):
                P.mm(M2[:, nt * 512:(nt + 1) * 512],
                     wqu[:, k * 128:(k + 1) * 128],
                     qcn[:, k * TOWN + nt * 512: k * TOWN + (nt + 1) * 512],
                     start=(k == 0), stop=(k == 7),
                     reads=[wqu.r(k * 128, (k + 1) * 128),
                            qcn.r(k * TOWN + nt * 512, k * TOWN + (nt + 1) * 512)],
                     writes=[_r(M2_b, nt * 512, (nt + 1) * 512)])
        P.op("act", lambda e: e.copy(qn_g.ap(), M2[:]),
             reads=[_r(M2_b)], writes=[qn_g.r()])
        for nt in range(2):
            for k in range(8):
                P.mm(S5[0:64, nt * 512:(nt + 1) * 512],
                     wqr[:, k * 64:(k + 1) * 64],
                     qcn[:, k * TOWN + nt * 512: k * TOWN + (nt + 1) * 512],
                     start=(k == 0), stop=(k == 7),
                     reads=[wqr.r(k * 64, (k + 1) * 64),
                            qcn.r(k * TOWN + nt * 512, k * TOWN + (nt + 1) * 512)],
                     writes=[_r(S5_b, nt * 512, (nt + 1) * 512)])
        P.op("act", lambda e: e.copy(qx.ap(), S5[0:64, 0:1024]),
             reads=[_r(S5_b, 0, 1024)], writes=[qx.r()])
        for nt in range(2):
            P.mm(S5[0:64, 1024 + nt * 512: 1024 + (nt + 1) * 512], C["perm64"][:],
                 qx[0:64, nt * 512:(nt + 1) * 512], start=True, stop=True,
                 reads=[qx.r(nt * 512, (nt + 1) * 512)],
                 writes=[_r(S5_b, 1024 + nt * 512, 1024 + (nt + 1) * 512)])
        P.op("dve", lambda e: e.tensor_tensor(
                rt1c.ap(), qx.ap(), C["c0_own"][:], op=ALU.mult),
             reads=[qx.r()], writes=[rt1c.r()])
        P.op("dve", lambda e: e.tensor_tensor(
                rt2c.ap(), S5[0:64, 1024:2048], C["s0_own"][:], op=ALU.mult),
             reads=[_r(S5_b, 1024, 2048)], writes=[rt2c.r()])
        P.op("dve", lambda e: e.tensor_tensor(
                qa_g[0:64, 0:TOWN], rt1c.ap(), rt2c.ap(), op=ALU.add),
             reads=[rt1c.r(), rt2c.r()], writes=[qa_g.r()])
        P.op("dve", lambda e: e.tensor_tensor(
                sqc.ap(), qn_g.ap(), qn_g.ap(), op=ALU.mult),
             reads=[qn_g.r()], writes=[sqc.r()])
        for nt in range(2):
            P.mm(S5[0:1, nt * 512:(nt + 1) * 512], C["ones_f"][:],
                 sqc[:, nt * 512:(nt + 1) * 512], start=True, stop=False,
                 reads=[sqc.r(nt * 512, (nt + 1) * 512)],
                 writes=[_r(S5_b, nt * 512, (nt + 1) * 512)])
        P.op("dve", lambda e: e.tensor_tensor(
                rt1c.ap(), qa_g[0:64, 0:TOWN], qa_g[0:64, 0:TOWN], op=ALU.mult),
             reads=[qa_g.r()], writes=[rt1c.r()])
        for nt in range(2):
            P.mm(S5[0:1, nt * 512:(nt + 1) * 512], C["ones_f"][0:64, :],
                 rt1c[0:64, nt * 512:(nt + 1) * 512], start=False, stop=True,
                 reads=[rt1c.r(nt * 512, (nt + 1) * 512)],
                 writes=[_r(S5_b, nt * 512, (nt + 1) * 512)])
        P.op("act", lambda e: e.activation(cqr.ap(), S5[0:1, 0:1024], AF.Ln,
                                           bias=1e-20),
             reads=[_r(S5_b, 0, 1024)], writes=[cqr.r()])
        P.op("act", lambda e: e.activation(
                cqb_t[:], cqr.ap(), AF.Exp, scale=0.5, bias=mrow[0:1, 2:3]),
             reads=[cqr.r(), mrow.r(2, 3)], writes=[_r(cqb_b)])
        P.dma("dsp", qa_g[64:65, 0:TOWN], cqb_t[:],
              reads=[_r(cqb_b)], writes=[qa_g.r()])
        for p in range(4):
            segs = _seg_layout(p)
            P.dma("dpool", msk.ap(), bd.dram["l0mask"][p], writes=[msk.r()])
            for si, (kc, w, off) in enumerate(segs):
                qoff = p * 256 + (256 - w)
                P.mm(S5[:, off:off + w],
                     kh_g[:, kc * 128:(kc + 1) * 128],
                     qn_g[:, qoff: p * 256 + 256],
                     start=True, stop=False,
                     reads=[kh_g.r(kc * 128, (kc + 1) * 128),
                            qn_g.r(qoff, p * 256 + 256)],
                     writes=[_r(S5_b, off, off + w)])
                P.mm(S5[:, off:off + w],
                     kra[0:65, kc * 128:(kc + 1) * 128],
                     qa_g[0:65, qoff: p * 256 + 256],
                     start=False, stop=True,
                     reads=[kra.r(kc * 128, (kc + 1) * 128),
                            qa_g.r(qoff, p * 256 + 256)],
                     writes=[_r(S5_b, off, off + w)])
            P.op("act", lambda e: e.activation(PT.ap(), S5[:, 0:2304], AF.Exp),
                 reads=[_r(S5_b, 0, 2304)], writes=[PT.r()])
            P.op("dve", lambda e: e.tensor_tensor(PT.ap(), PT.ap(), msk.ap(),
                                                  op=ALU.mult),
                 reads=[PT.r(), msk.r()], writes=[PT.r()])
            for si, (kc, w, off) in enumerate(segs):
                P.mm(A1[0:1, 256 - w:256], C["ones_b"][:], PT[:, off:off + w],
                     start=(si == 0), stop=(si == len(segs) - 1),
                     reads=[PT.r(off, off + w)], writes=[_r(A1_b, 0, 256)])
            for si, (kc, w, off) in enumerate(segs):
                P.mm(M2[:, 256 - w:256],
                     v_g[:, kc * 128:(kc + 1) * 128],
                     PT[:, off:off + w],
                     start=(si == 0), stop=(si == len(segs) - 1),
                     reads=[v_g.r(kc * 128, (kc + 1) * 128),
                            PT.r(off, off + w)],
                     writes=[_r(M2_b, 0, 256)])
            P.op("dve", lambda e: e.reciprocal(rcp.ap(), A1[0:1, 0:256]),
                 reads=[_r(A1_b, 0, 256)], writes=[rcp.r()], fence=True)
            P.mm(A1[:, 256:512], C["onesrow_f"][:], rcp.ap(),
                 start=True, stop=True,
                 reads=[rcp.r()], writes=[_r(A1_b, 256, 512)])
            P.op("act", lambda e: e.copy(bcsb.ap(), A1[:, 256:512]),
                 reads=[_r(A1_b, 256, 512)], writes=[bcsb.r()])
            P.op("dve", lambda e, h=h, p=p: e.tensor_tensor(
                    attnout[:, h * TOWN + p * 256: h * TOWN + (p + 1) * 256],
                    M2[:, 0:256], bcsb.ap(), op=ALU.mult),
                 reads=[_r(M2_b, 0, 256), bcsb.r()],
                 writes=[attnout.r(h * TOWN + p * 256, h * TOWN + (p + 1) * 256)])
    A16.release(); A32.release()
    P.barrier()

    P.dma("dsp", X.ap().rearrange("p (k t) -> p k t", k=16),
          bd.dram["xT_own"].rearrange("(k p) t -> p k t", p=128),
          writes=[X.r()])
    A16.mark(); A32.mark()
    wo2 = [A16.alloc("wo0", 128, 16 * 128), A16.alloc("wo1", 128, 16 * 128)]
    emit_matmul_residual(bd, "woT", attnout, wo2, X)
    A16.release(); A32.release()
    A32.release()
    emit_postln_inplace(bd)
    A16.release()
    P.barrier()


def emit_matmul_residual(bd, wT_name, src_b16, wo2, X, src_T=TOWN, n_k=16):
    P, C = bd.P, bd.C
    S5, S5_b = bd.ps["S5"]
    M2, M2_b = bd.ps["M2"]
    for oc in range(16):
        wb = wo2[oc % 2]
        P.dma("dpool", wb.ap().rearrange("p (k o) -> p k o", k=n_k),
              bd.dram[wT_name].rearrange("(k p) o -> p k o", p=128)[:, :, oc * 128:(oc + 1) * 128],
              writes=[wb.r()])
        ps, ps_b, po = [(M2, M2_b, 0), (S5, S5_b, 0), (S5, S5_b, 1024)][oc % 3]
        for nt in range(2):
            for k in range(n_k):
                P.mm(ps[:, po + nt * 512: po + (nt + 1) * 512],
                     wb[:, k * 128:(k + 1) * 128],
                     src_b16[:, k * src_T + nt * 512: k * src_T + (nt + 1) * 512],
                     start=(k == 0), stop=(k == n_k - 1),
                     reads=[wb.r(k * 128, (k + 1) * 128),
                            src_b16.r(k * src_T + nt * 512, k * src_T + (nt + 1) * 512)],
                     writes=[_r(ps_b, po + nt * 512, po + (nt + 1) * 512)])
        P.op("dve", lambda e, oc=oc, ps=ps, po=po: e.tensor_tensor(
                X[:, oc * TOWN:(oc + 1) * TOWN], X[:, oc * TOWN:(oc + 1) * TOWN],
                ps[:, po:po + 1024], op=ALU.add),
             reads=[X.r(oc * TOWN, (oc + 1) * TOWN), _r(ps_b, po, po + 1024)],
             writes=[X.r(oc * TOWN, (oc + 1) * TOWN)])


def emit_postln_inplace(bd):
    nc, P, C = bd.nc, bd.P, bd.C
    S5, S5_b = bd.ps["S5"]
    M2, M2_b = bd.ps["M2"]
    A16, A32 = bd.A16, bd.A32
    X, Xb = bd.X, bd.Xb
    P.barrier()
    A32.mark()
    sql = A32.alloc("sqLN", 128, 1024)
    rows = A32.alloc("rowsLN", 1, 4 * 1024)
    for oc in range(16):
        P.op("dve", lambda e, oc=oc: e.tensor_tensor(
                sql.ap(), X[:, oc * TOWN:(oc + 1) * TOWN],
                X[:, oc * TOWN:(oc + 1) * TOWN], op=ALU.mult),
             reads=[X.r(oc * TOWN, (oc + 1) * TOWN)], writes=[sql.r()])
        for nt in range(2):
            P.mm(M2[0:1, nt * 512:(nt + 1) * 512], C["ones_f"][:],
                 X[:, oc * TOWN + nt * 512: oc * TOWN + (nt + 1) * 512],
                 start=(oc == 0), stop=(oc == 15),
                 reads=[X.r(oc * TOWN + nt * 512, oc * TOWN + (nt + 1) * 512)],
                 writes=[_r(M2_b, nt * 512, (nt + 1) * 512)])
            P.mm(S5[0:1, nt * 512:(nt + 1) * 512], C["ones_f"][:],
                 sql[:, nt * 512:(nt + 1) * 512],
                 start=(oc == 0), stop=(oc == 15),
                 reads=[sql.r(nt * 512, (nt + 1) * 512)],
                 writes=[_r(S5_b, nt * 512, (nt + 1) * 512)])
    mu = lambda: rows[0:1, 0:1024]
    msq = lambda: rows[0:1, 1024:2048]
    rstd = lambda: rows[0:1, 2048:3072]
    murs = lambda: rows[0:1, 3072:4096]
    P.op("dve", lambda e: e.tensor_scalar(mu(), M2[0:1, 0:1024], 1.0 / DM, None,
                                          op0=ALU.mult),
         reads=[_r(M2_b, 0, 1024)], writes=[rows.r(0, 1024)])
    P.op("dve", lambda e: e.tensor_scalar(msq(), S5[0:1, 0:1024], 1.0 / DM, None,
                                          op0=ALU.mult),
         reads=[_r(S5_b, 0, 1024)], writes=[rows.r(1024, 2048)])
    P.op("dve", lambda e: e.tensor_tensor(rstd(), mu(), mu(), op=ALU.mult),
         reads=[rows.r(0, 1024)], writes=[rows.r(2048, 3072)])
    P.op("dve", lambda e: e.tensor_tensor(msq(), msq(), rstd(), op=ALU.subtract),
         reads=[rows.r(1024, 3072)], writes=[rows.r(1024, 2048)])
    P.op("act", lambda e: e.activation(rstd(), msq(), AF.Ln, bias=1e-5),
         reads=[rows.r(1024, 2048)], writes=[rows.r(2048, 3072)])
    P.op("act", lambda e: e.activation(rstd(), rstd(), AF.Exp, scale=-0.5),
         reads=[rows.r(2048, 3072)], writes=[rows.r(2048, 3072)])
    P.op("dve", lambda e: e.tensor_tensor(murs(), mu(), rstd(), op=ALU.mult),
         reads=[rows.r(0, 3072)], writes=[rows.r(3072, 4096)])
    for nt in range(2):
        P.mm(M2[:, nt * 512:(nt + 1) * 512], C["onesrow_f"][:],
             rows[0:1, 2048 + nt * 512:2048 + (nt + 1) * 512], start=True, stop=True,
             reads=[rows.r(2048, 3072)], writes=[_r(M2_b, nt * 512, (nt + 1) * 512)])
        P.mm(S5[:, nt * 512:(nt + 1) * 512], C["onesrow_f"][:],
             rows[0:1, 3072 + nt * 512:3072 + (nt + 1) * 512], start=True, stop=True,
             reads=[rows.r(3072, 4096)], writes=[_r(S5_b, nt * 512, (nt + 1) * 512)])
    for oc in range(16):
        P.op("dve", lambda e, oc=oc: e.tensor_tensor(
                X[:, oc * TOWN:(oc + 1) * TOWN], X[:, oc * TOWN:(oc + 1) * TOWN],
                M2[:, 0:1024], op=ALU.mult),
             reads=[X.r(oc * TOWN, (oc + 1) * TOWN), _r(M2_b, 0, 1024)],
             writes=[X.r(oc * TOWN, (oc + 1) * TOWN)])
        P.op("dve", lambda e, oc=oc: e.tensor_tensor(
                X[:, oc * TOWN:(oc + 1) * TOWN], X[:, oc * TOWN:(oc + 1) * TOWN],
                S5[:, 0:1024], op=ALU.subtract),
             reads=[X.r(oc * TOWN, (oc + 1) * TOWN), _r(S5_b, 0, 1024)],
             writes=[X.r(oc * TOWN, (oc + 1) * TOWN)])
        P.op("pool", lambda e, oc=oc: e.tensor_copy(
                Xb[:, oc * TOWN:(oc + 1) * TOWN], X[:, oc * TOWN:(oc + 1) * TOWN]),
             reads=[X.r(oc * TOWN, (oc + 1) * TOWN)],
             writes=[Xb.r(oc * TOWN, (oc + 1) * TOWN)])
    A32.release()
    P.barrier()
def emit_l0_moe(bd):
    nc, P, C = bd.nc, bd.P, bd.C
    S5, S5_b = bd.ps["S5"]
    M2, M2_b = bd.ps["M2"]
    A1, A1_b = bd.ps["A1"]
    A16, A32 = bd.A16, bd.A32
    X, Xb = bd.X, bd.Xb

    A16.mark(); A32.mark()
    gw = A32.alloc("gwf", 128, 16 * 16)
    ct = A32.alloc("ct", 16, TOWN)
    ctrow = A32.alloc("ctrow", 1, TOWN)
    cb = A32.alloc("cb", 128, TOWN)
    h = A16.alloc("h", 128, 11 * 1024)
    wg = [A16.alloc("wg0", 128, 16 * 128), A16.alloc("wg1", 128, 16 * 128)]
    wu = [A16.alloc("wu0", 128, 16 * 128), A16.alloc("wu1", 128, 16 * 128)]
    wd = [A16.alloc("wd0", 128, 11 * 128), A16.alloc("wd1", 128, 11 * 128)]
    probs = A32.alloc("probs", 128, 16)
    m16 = A32.alloc("m16", 128, 16)
    top8 = A32.alloc("top8", 128, 8)
    gr = A32.alloc("gr", 128, 4)
    sg = A32.alloc("sg", 128, 1024)

    P.dma("dsp", gw.ap().rearrange("p (k o) -> p k o", k=16),
          bd.dram["gate_wT"].rearrange("(k p) o -> p k o", p=128),
          writes=[gw.r()])
    for i in range(8):
        for k in range(16):
            P.mm(A1[:, 0:16],
                 X[:, k * TOWN + i * 128: k * TOWN + (i + 1) * 128],
                 gw[:, k * 16:(k + 1) * 16],
                 start=(k == 0), stop=(k == 15),
                 reads=[X.r(k * TOWN + i * 128, k * TOWN + (i + 1) * 128),
                        gw.r(k * 16, (k + 1) * 16)],
                 writes=[_r(A1_b, 0, 16)])
        P.op("dve", lambda e: e.tensor_reduce(gr[:, 0:1], A1[:, 0:16], axis=AX.X,
                                              op=ALU.max, negate=True),
             reads=[_r(A1_b, 0, 16)], writes=[gr.r(0, 1)], fence=True)
        P.op("act", lambda e: e.activation(probs.ap(), A1[:, 0:16], AF.Exp,
                                           bias=gr[:, 0:1]),
             reads=[_r(A1_b, 0, 16), gr.r(0, 1)], writes=[probs.r()])
        P.op("dve", lambda e: e.max(top8.ap(), probs.ap()),
             reads=[probs.r()], writes=[top8.r()], fence=True)
        P.op("dve", lambda e: e.tensor_reduce(gr[:, 2:3], top8[:, 0:6], axis=AX.X,
                                              op=ALU.add),
             reads=[top8.r()], writes=[gr.r(2, 3)], fence=True)
        P.op("dve", lambda e: e.tensor_scalar(gr[:, 2:3], gr[:, 2:3], 1e-20, None,
                                              op0=ALU.add),
             reads=[gr.r(2, 3)], writes=[gr.r(2, 3)])
        P.op("dve", lambda e: e.reciprocal(gr[:, 3:4], gr[:, 2:3]),
             reads=[gr.r(2, 3)], writes=[gr.r(3, 4)], fence=True)
        P.op("dve", lambda e: e.tensor_scalar(m16.ap(), probs.ap(),
                                              top8[:, 5:6], None, op0=ALU.is_ge),
             reads=[probs.r(), top8.r()], writes=[m16.r()])
        P.op("dve", lambda e: e.scalar_tensor_tensor(
                m16.ap(), probs.ap(), gr[:, 3:4], m16.ap(),
                op0=ALU.mult, op1=ALU.mult),
             reads=[probs.r(), gr.r(3, 4), m16.r()], writes=[m16.r()])
        P.mm(A1[0:16, 128:256], m16.ap(), C["ident_f"][:],
             start=True, stop=True,
             reads=[m16.r()], writes=[_r(A1_b, 128, 256)])
        P.op("act", lambda e, i=i: e.copy(ct[0:16, i * 128:(i + 1) * 128],
                                          A1[0:16, 128:256]),
             reads=[_r(A1_b, 128, 256)], writes=[ct.r(i * 128, (i + 1) * 128)])
    P.barrier()
    if getattr(bd, "dbg", False):
        dbg_ct = bd.dout("dbg_ct", (16, TOWN))
        bd.P.mark_extra_final(P.dma("dsp", dbg_ct, ct.ap(), reads=[ct.r()]))

    ROT = [(M2, M2_b, 0), (S5, S5_b, 0), (S5, S5_b, 1024)]

    def swiglu_expert(e_idx, gate_src, up_src, down_src, n_i, scale_by_c):
        for j in range(n_i):
            P.dma("dpool", wg[j % 2].ap().rearrange("p (k o) -> p k o", k=16),
                  gate_src(j), writes=[wg[j % 2].r()])
            P.dma("dpool", wu[j % 2].ap().rearrange("p (k o) -> p k o", k=16),
                  up_src(j), writes=[wu[j % 2].r()])
            gp, gp_b, go = ROT[j % 3]
            up, up_b, uo = ROT[(j + 2) % 3]
            for nt in range(2):
                for k in range(16):
                    P.mm(gp[:, go + nt * 512: go + (nt + 1) * 512],
                         wg[j % 2][:, k * 128:(k + 1) * 128],
                         Xb[:, k * TOWN + nt * 512: k * TOWN + (nt + 1) * 512],
                         start=(k == 0), stop=(k == 15),
                         reads=[wg[j % 2].r(k * 128, (k + 1) * 128),
                                Xb.r(k * TOWN + nt * 512, k * TOWN + (nt + 1) * 512)],
                         writes=[_r(gp_b, go + nt * 512, go + (nt + 1) * 512)])
            for nt in range(2):
                for k in range(16):
                    P.mm(up[:, uo + nt * 512: uo + (nt + 1) * 512],
                         wu[j % 2][:, k * 128:(k + 1) * 128],
                         Xb[:, k * TOWN + nt * 512: k * TOWN + (nt + 1) * 512],
                         start=(k == 0), stop=(k == 15),
                         reads=[wu[j % 2].r(k * 128, (k + 1) * 128),
                                Xb.r(k * TOWN + nt * 512, k * TOWN + (nt + 1) * 512)],
                         writes=[_r(up_b, uo + nt * 512, uo + (nt + 1) * 512)])
            P.op("act", lambda e, gp=gp, gp_b=gp_b, go=go: e.activation(
                    sg.ap(), gp[:, go:go + 1024], AF.Silu),
                 reads=[_r(gp_b, go, go + 1024)], writes=[sg.r()])
            P.op("dve", lambda e, up=up, up_b=up_b, uo=uo: e.tensor_tensor(
                    sg.ap(), up[:, uo:uo + 1024], sg.ap(), op=ALU.mult),
                 reads=[_r(up_b, uo, uo + 1024), sg.r()], writes=[sg.r()])
            if scale_by_c:
                P.op("dve", lambda e, j=j: e.tensor_tensor(
                        h[:, j * 1024:(j + 1) * 1024], sg.ap(), cb.ap(), op=ALU.mult),
                     reads=[sg.r(), cb.r()], writes=[h.r(j * 1024, (j + 1) * 1024)])
            else:
                P.op("dve", lambda e, j=j: e.tensor_copy(
                        h[:, j * 1024:(j + 1) * 1024], sg.ap()),
                     reads=[sg.r()], writes=[h.r(j * 1024, (j + 1) * 1024)])
        for oc in range(16):
            P.dma("dpool", wd[oc % 2].ap().rearrange("p (k o) -> p k o", k=n_i),
                  down_src(oc), writes=[wd[oc % 2].r()])
            ps, ps_b, po = ROT[oc % 3]
            for nt in range(2):
                for k in range(n_i):
                    P.mm(ps[:, po + nt * 512: po + (nt + 1) * 512],
                         wd[oc % 2][:, k * 128:(k + 1) * 128],
                         h[:, k * 1024 + nt * 512: k * 1024 + (nt + 1) * 512],
                         start=(k == 0), stop=(k == n_i - 1),
                         reads=[wd[oc % 2].r(k * 128, (k + 1) * 128),
                                h.r(k * 1024 + nt * 512, k * 1024 + (nt + 1) * 512)],
                         writes=[_r(ps_b, po + nt * 512, po + (nt + 1) * 512)])
            P.op("dve", lambda e, oc=oc, ps=ps, ps_b=ps_b, po=po: e.tensor_tensor(
                    X[:, oc * TOWN:(oc + 1) * TOWN], X[:, oc * TOWN:(oc + 1) * TOWN],
                    ps[:, po:po + 1024], op=ALU.add),
                 reads=[X.r(oc * TOWN, (oc + 1) * TOWN), _r(ps_b, po, po + 1024)],
                 writes=[X.r(oc * TOWN, (oc + 1) * TOWN)])

    w13r = bd.dram["w13T"].rearrange("e (k p) o -> e p k o", p=128)
    w2r = bd.dram["w2T"].rearrange("e (k p) o -> e p k o", p=128)
    for e_idx in range(NEXP):
        P.dma("dsp", ctrow.ap(), ct[e_idx:e_idx + 1, 0:TOWN],
              reads=[ct.r()], writes=[ctrow.r()])
        for nt in range(2):
            P.mm(A1[:, 0:512],
                 C["onesrow_f"][:], ctrow[0:1, nt * 512:(nt + 1) * 512],
                 start=True, stop=True,
                 reads=[ctrow.r()], writes=[_r(A1_b, 0, 512)])
            P.op("act", lambda e, nt=nt: e.copy(
                    cb[:, nt * 512:(nt + 1) * 512], A1[:, 0:512]),
                 reads=[_r(A1_b, 0, 512)],
                 writes=[cb.r(nt * 512, (nt + 1) * 512)])
        swiglu_expert(
            e_idx,
            lambda j, e_idx=e_idx: w13r[e_idx][:, :, j * 128:(j + 1) * 128],
            lambda j, e_idx=e_idx: w13r[e_idx][:, :, IMOE + j * 128: IMOE + (j + 1) * 128],
            lambda oc, e_idx=e_idx: w2r[e_idx][:, :, oc * 128:(oc + 1) * 128],
            11, True)
    sw13r = bd.dram["sw13T"].rearrange("(k p) o -> p k o", p=128)
    sw2r = bd.dram["sw2T"].rearrange("(k p) o -> p k o", p=128)
    for s in range(2):
        swiglu_expert(
            100 + s,
            lambda j, s=s: sw13r[:, :, s * 1408 + j * 128: s * 1408 + (j + 1) * 128],
            lambda j, s=s: sw13r[:, :, ISH + s * 1408 + j * 128: ISH + s * 1408 + (j + 1) * 128],
            lambda oc, s=s: sw2r[:, s * 11:(s + 1) * 11, oc * 128:(oc + 1) * 128],
            11, False)
    A16.release(); A32.release()
    P.barrier()
    emit_postln_inplace(bd)


def emit_l1_attn(bd):
    nc, P, C = bd.nc, bd.P, bd.C
    S5, S5_b = bd.ps["S5"]
    M2, M2_b = bd.ps["M2"]
    A1, A1_b = bd.ps["A1"]
    A16, A32 = bd.A16, bd.A32
    X, Xb = bd.X, bd.Xb
    EPS = 1e-6

    cc_in = bd.dint("cc_in", (4096, 128), BF16)
    cc_out = bd.dint("cc_out", (8192, 128), BF16)
    cc_inz = bd.dint("cc_inz", (128, 32), BF16)
    cc_outz = bd.dint("cc_outz", (256, 32), BF16)
    cc_in_b = P.buf("cc_in", None, 128)
    cc_out_b = P.buf("cc_out", None, 128)
    cc_inz_b = P.buf("cc_inz", None, 32)
    cc_outz_b = P.buf("cc_outz", None, 32)

    A16.mark(); A32.mark()
    c1f = A16.alloc("c1f", 128, TOWN)
    s1f = A16.alloc("s1f", 128, TOWN)
    KfF = A16.alloc("KfF", 128, 4 * TOWN)
    VfT = A16.alloc("VfT", 128, 8 * 512)
    Sprev = A16.alloc("Sprev", 128, 8 * 512)
    zp = A16.alloc("zp", 128, 8 * 4)
    kx = A16.alloc("kx", 128, TOWN)
    wpref = A32.alloc("wpref", 128, 128)
    rtA = A32.alloc("rtA", 128, TOWN)
    rtB = A32.alloc("rtB", 128, TOWN)
    rtC = A32.alloc("rtC", 128, TOWN)
    Sacc = A32.alloc("Sacc", 128, 1024)
    zacc = A32.alloc("zacc", 128, 8)
    zbuf = A32.alloc("zbuf", 128, 32)

    P.dma("dpool", c1f.ap(), bd.dram["c1_own"], writes=[c1f.r()])
    P.dma("dpool", s1f.ap(), bd.dram["s1_own"], writes=[s1f.r()])
    P.dma("dsp", wpref.ap(), bd.dram["l1prefw"], writes=[wpref.r()])

    def elu1(v_in, rd_in, v_out, wr_out, sa, sa_r, sb, sb_r):
        P.op("dve", lambda e: e.tensor_scalar(sa, v_in, 0.0, None, op0=ALU.min),
             reads=rd_in, writes=sa_r)
        P.op("act", lambda e: e.activation(sa, sa, AF.Exp), reads=sa_r, writes=sa_r)
        P.op("dve", lambda e: e.tensor_scalar(sb, v_in, 0.0, None, op0=ALU.max),
             reads=rd_in, writes=sb_r)
        P.op("dve", lambda e: e.tensor_tensor(v_out, sa, sb, op=ALU.add),
             reads=sa_r + sb_r, writes=wr_out)

    A16.mark()
    wk2 = [A16.alloc("wk20", 128, 16 * 128), A16.alloc("wk21", 128, 16 * 128)]
    c1t = A16.alloc("c1t", 128, 8 * 128)
    s1t = A16.alloc("s1t", 128, 8 * 128)
    P.dma("dpool", c1t.ap().rearrange("p (tc d) -> p tc d", tc=8),
          bd.dram["c1tok"].rearrange("(tc p) d -> p tc d", p=128), writes=[c1t.r()])
    P.dma("dpool", s1t.ap().rearrange("p (tc d) -> p tc d", tc=8),
          bd.dram["s1tok"].rearrange("(tc p) d -> p tc d", p=128), writes=[s1t.r()])
    for kvh in range(KVH):
        wb = wk2[kvh % 2]
        P.dma("dpool", wb.ap().rearrange("p (k o) -> p k o", k=16),
              bd.dram["WkT"].rearrange("(k p) o -> p k o", p=128)[:, :, kvh * 128:(kvh + 1) * 128],
              writes=[wb.r()])
        for nt in range(2):
            for k in range(16):
                P.mm(M2[:, nt * 512:(nt + 1) * 512],
                     wb[:, k * 128:(k + 1) * 128],
                     Xb[:, k * TOWN + nt * 512: k * TOWN + (nt + 1) * 512],
                     start=(k == 0), stop=(k == 15),
                     reads=[wb.r(k * 128, (k + 1) * 128),
                            Xb.r(k * TOWN + nt * 512, k * TOWN + (nt + 1) * 512)],
                     writes=[_r(M2_b, nt * 512, (nt + 1) * 512)])
        P.op("act", lambda e: e.copy(kx.ap(), M2[:]), reads=[_r(M2_b)], writes=[kx.r()])
        for nt in range(2):
            P.mm(S5[:, nt * 512:(nt + 1) * 512], C["perm128"][:],
                 kx[:, nt * 512:(nt + 1) * 512], start=True, stop=True,
                 reads=[kx.r(nt * 512, (nt + 1) * 512)],
                 writes=[_r(S5_b, nt * 512, (nt + 1) * 512)])
        P.op("dve", lambda e: e.tensor_tensor(rtA.ap(), kx.ap(), c1f.ap(), op=ALU.mult),
             reads=[kx.r(), c1f.r()], writes=[rtA.r()])
        P.op("dve", lambda e: e.tensor_tensor(rtB.ap(), S5[:, 0:1024], s1f.ap(), op=ALU.mult),
             reads=[_r(S5_b, 0, 1024), s1f.r()], writes=[rtB.r()])
        P.op("dve", lambda e: e.tensor_tensor(rtA.ap(), rtA.ap(), rtB.ap(), op=ALU.add),
             reads=[rtA.r(), rtB.r()], writes=[rtA.r()])
        elu1(rtA.ap(), [rtA.r()], KfF[:, kvh * TOWN:(kvh + 1) * TOWN],
             [KfF.r(kvh * TOWN, (kvh + 1) * TOWN)],
             rtB.ap(), [rtB.r()], rtC.ap(), [rtC.r()])

    KfT = A16.alloc("KfT", 128, 8 * 512)
    ccs = A16.alloc("ccs", 128, 8 * 512)
    wv2 = [A16.alloc("wv20", 128, 16 * 128), A16.alloc("wv21", 128, 16 * 128)]
    ktx = A16.alloc("ktx", 128, 128)
    sc1 = A32.alloc("sc1", 128, 128)
    sc2 = A32.alloc("sc2", 128, 128)
    sc3 = A32.alloc("sc3", 128, 128)
    for kvh in range(KVH):
        P.dma("dpool", wv2[0].ap().rearrange("p (k o) -> p k o", k=16),
              bd.dram["WvT"].rearrange("(k p) o -> p k o", p=128)[:, :, kvh * 128:(kvh + 1) * 128],
              writes=[wv2[0].r()])
        P.dma("dpool", wv2[1].ap().rearrange("p (k o) -> p k o", k=16),
              bd.dram["WkT"].rearrange("(k p) o -> p k o", p=128)[:, :, kvh * 128:(kvh + 1) * 128],
              writes=[wv2[1].r()])
        for tc in range(8):
            for k in range(16):
                P.mm(M2[:, 0:128],
                     Xb[:, k * TOWN + tc * 128: k * TOWN + (tc + 1) * 128],
                     wv2[0][:, k * 128:(k + 1) * 128],
                     start=(k == 0), stop=(k == 15),
                     reads=[Xb.r(k * TOWN + tc * 128, k * TOWN + (tc + 1) * 128),
                            wv2[0].r(k * 128, (k + 1) * 128)],
                     writes=[_r(M2_b, 0, 128)])
            P.op("act", lambda e, tc=tc, kvh=kvh: e.copy(
                    VfT[:, tc * 512 + kvh * 128: tc * 512 + (kvh + 1) * 128],
                    M2[:, 0:128]),
                 reads=[_r(M2_b, 0, 128)],
                 writes=[VfT.r(tc * 512 + kvh * 128, tc * 512 + (kvh + 1) * 128)])
            for k in range(16):
                P.mm(M2[:, 128:256],
                     Xb[:, k * TOWN + tc * 128: k * TOWN + (tc + 1) * 128],
                     wv2[1][:, k * 128:(k + 1) * 128],
                     start=(k == 0), stop=(k == 15),
                     reads=[Xb.r(k * TOWN + tc * 128, k * TOWN + (tc + 1) * 128),
                            wv2[1].r(k * 128, (k + 1) * 128)],
                     writes=[_r(M2_b, 128, 256)])
            P.op("act", lambda e: e.copy(ktx.ap(), M2[:, 128:256]),
                 reads=[_r(M2_b, 128, 256)], writes=[ktx.r()])
            P.op("dve", lambda e, tc=tc: e.tensor_tensor(
                    sc1[:, 0:64], ktx[:, 0:64], c1t[:, tc * 128: tc * 128 + 64],
                    op=ALU.mult),
                 reads=[ktx.r(), c1t.r()], writes=[sc1.r(0, 64)])
            P.op("dve", lambda e, tc=tc: e.tensor_tensor(
                    sc2[:, 0:64], ktx[:, 64:128], s1t[:, tc * 128: tc * 128 + 64],
                    op=ALU.mult),
                 reads=[ktx.r(), s1t.r()], writes=[sc2.r(0, 64)])
            P.op("dve", lambda e: e.tensor_tensor(
                    sc1[:, 0:64], sc1[:, 0:64], sc2[:, 0:64], op=ALU.subtract),
                 reads=[sc1.r(0, 64), sc2.r(0, 64)], writes=[sc1.r(0, 64)])
            P.op("dve", lambda e, tc=tc: e.tensor_tensor(
                    sc1[:, 64:128], ktx[:, 64:128], c1t[:, tc * 128 + 64: tc * 128 + 128],
                    op=ALU.mult),
                 reads=[ktx.r(), c1t.r()], writes=[sc1.r(64, 128)])
            P.op("dve", lambda e, tc=tc: e.tensor_tensor(
                    sc2[:, 64:128], ktx[:, 0:64], s1t[:, tc * 128 + 64: tc * 128 + 128],
                    op=ALU.mult),
                 reads=[ktx.r(), s1t.r()], writes=[sc2.r(64, 128)])
            P.op("dve", lambda e: e.tensor_tensor(
                    sc1[:, 64:128], sc1[:, 64:128], sc2[:, 64:128], op=ALU.add),
                 reads=[sc1.r(64, 128), sc2.r(64, 128)], writes=[sc1.r(64, 128)])
            elu1(sc1.ap(), [sc1.r()],
                 KfT[:, tc * 512 + kvh * 128: tc * 512 + (kvh + 1) * 128],
                 [KfT.r(tc * 512 + kvh * 128, tc * 512 + (kvh + 1) * 128)],
                 sc2.ap(), [sc2.r()], sc3.ap(), [sc3.r()])
            P.mm(A1[:, 0:128],
                 KfT[:, tc * 512 + kvh * 128: tc * 512 + (kvh + 1) * 128],
                 VfT[:, tc * 512 + kvh * 128: tc * 512 + (kvh + 1) * 128],
                 start=True, stop=True,
                 reads=[KfT.r(tc * 512 + kvh * 128, tc * 512 + (kvh + 1) * 128),
                        VfT.r(tc * 512 + kvh * 128, tc * 512 + (kvh + 1) * 128)],
                 writes=[_r(A1_b, 0, 128)])
            P.op("act", lambda e, tc=tc, kvh=kvh: e.copy(
                    ccs[:, tc * 512 + kvh * 128: tc * 512 + (kvh + 1) * 128],
                    A1[:, 0:128]),
                 reads=[_r(A1_b, 0, 128)],
                 writes=[ccs.r(tc * 512 + kvh * 128, tc * 512 + (kvh + 1) * 128)])
            P.op("dve", lambda e, tc=tc, kvh=kvh: e.tensor_reduce(
                    zbuf[:, tc * 4 + kvh: tc * 4 + kvh + 1],
                    KfF[:, kvh * TOWN + tc * 128: kvh * TOWN + (tc + 1) * 128],
                    axis=AX.X, op=ALU.add),
                 reads=[KfF.r(kvh * TOWN + tc * 128, kvh * TOWN + (tc + 1) * 128)],
                 writes=[zbuf.r(tc * 4 + kvh, tc * 4 + kvh + 1)], fence=True)

    P.dma("dsp", cc_in.rearrange("(tc kvh p) d -> p tc kvh d", tc=8, kvh=4),
          ccs.ap().rearrange("p (tc kvh d) -> p tc kvh d", tc=8, kvh=4),
          reads=[ccs.r()], writes=[_r(cc_in_b)])
    P.dma("dpool", cc_inz, zbuf.ap(), reads=[zbuf.r()], writes=[_r(cc_inz_b)])
    rg = [[0, 1], [2, 3], [4, 5], [6, 7]]
    P.op("pool", lambda e: e.collective_compute(
            "AllGather", ALU.bypass, replica_groups=rg,
            ins=[bd.dram["cc_in"].opt()], outs=[bd.dram["cc_out"].opt()]),
         reads=[_r(cc_in_b)], writes=[_r(cc_out_b)], fence=True)
    P.op("pool", lambda e: e.collective_compute(
            "AllGather", ALU.bypass, replica_groups=rg,
            ins=[bd.dram["cc_inz"].opt()], outs=[bd.dram["cc_outz"].opt()]),
         reads=[_r(cc_inz_b)], writes=[_r(cc_outz_b)], fence=True)

    A16.release()
    A16.mark()
    GS = A16.alloc("GS", 128, 16 * 512)
    zg = A16.alloc("zg", 128, 64)
    P.dma("dsp", GS.ap().rearrange("p (r tc kvh d) -> p r tc kvh d", r=2, tc=8, kvh=4),
          bd.dram["cc_out"].rearrange("(r tc kvh p) d -> p r tc kvh d", r=2, tc=8, kvh=4),
          reads=[_r(cc_out_b)], writes=[GS.r()])
    P.dma("dsp", zg.ap().rearrange("p (r c) -> p r c", r=2),
          bd.dram["cc_outz"].rearrange("(r p) c -> p r c", r=2),
          reads=[_r(cc_outz_b)], writes=[zg.r()])
    for c in range(8):
        sA = lambda i: Sacc[:, (i % 2) * 512:(i % 2) * 512 + 512]
        sAr = lambda i: [Sacc.r((i % 2) * 512, (i % 2) * 512 + 512)]
        zA = lambda i: zacc[:, (i % 2) * 4:(i % 2) * 4 + 4]
        zAr = lambda i: [zacc.r((i % 2) * 4, (i % 2) * 4 + 4)]
        P.op("dve", lambda e: e.memset(sA(0), 0.0), writes=sAr(0))
        P.op("dve", lambda e: e.memset(zA(0), 0.0), writes=zAr(0))
        for sl in range(16):
            P.op("dve", lambda e, c=c, sl=sl: e.scalar_tensor_tensor(
                    sA(sl + 1), GS[:, sl * 512:(sl + 1) * 512],
                    wpref[:, c * 16 + sl: c * 16 + sl + 1], sA(sl),
                    op0=ALU.mult, op1=ALU.add),
                 reads=[GS.r(sl * 512, (sl + 1) * 512), wpref.r()] + sAr(sl),
                 writes=sAr(sl + 1))
            r_, tc_ = sl // 8, sl % 8
            P.op("dve", lambda e, c=c, sl=sl, r_=r_, tc_=tc_: e.scalar_tensor_tensor(
                    zA(sl + 1), zg[:, r_ * 32 + tc_ * 4: r_ * 32 + tc_ * 4 + 4],
                    wpref[:, c * 16 + sl: c * 16 + sl + 1], zA(sl),
                    op0=ALU.mult, op1=ALU.add),
                 reads=[zg.r(), wpref.r()] + zAr(sl), writes=zAr(sl + 1))
        P.op("dve", lambda e, c=c: e.tensor_copy(Sprev[:, c * 512:(c + 1) * 512],
                                                 sA(16)),
             reads=sAr(16), writes=[Sprev.r(c * 512, (c + 1) * 512)])
        P.op("dve", lambda e, c=c: e.tensor_copy(zp[:, c * 4:(c + 1) * 4], zA(16)),
             reads=zAr(16), writes=[zp.r(c * 4, (c + 1) * 4)])
    A16.release()
    P.barrier()

    A16.mark()
    attnL1 = A16.alloc("attnL1", 128, 16 * TOWN)
    A16.mark()
    wq2 = [A16.alloc("wq20", 128, 16 * 128)]
    Qfh = A16.alloc("Qfh", 128, TOWN)
    ATb = A16.alloc("ATb", 128, TOWN)
    den = A32.alloc("denL1", 1, TOWN)
    bcl = A32.alloc("bcl", 128, TOWN)
    for h in range(H1):
        kvh = h // REP
        wb = wq2[0]
        P.dma("dpool", wb.ap().rearrange("p (k o) -> p k o", k=16),
              bd.dram["WqT"].rearrange("(k p) o -> p k o", p=128)[:, :, h * 128:(h + 1) * 128],
              writes=[wb.r()])
        for nt in range(2):
            for k in range(16):
                P.mm(M2[:, nt * 512:(nt + 1) * 512],
                     wb[:, k * 128:(k + 1) * 128],
                     Xb[:, k * TOWN + nt * 512: k * TOWN + (nt + 1) * 512],
                     start=(k == 0), stop=(k == 15),
                     reads=[wb.r(k * 128, (k + 1) * 128),
                            Xb.r(k * TOWN + nt * 512, k * TOWN + (nt + 1) * 512)],
                     writes=[_r(M2_b, nt * 512, (nt + 1) * 512)])
        P.op("act", lambda e: e.copy(kx.ap(), M2[:]), reads=[_r(M2_b)], writes=[kx.r()])
        for nt in range(2):
            P.mm(S5[:, nt * 512:(nt + 1) * 512], C["perm128"][:],
                 kx[:, nt * 512:(nt + 1) * 512], start=True, stop=True,
                 reads=[kx.r(nt * 512, (nt + 1) * 512)],
                 writes=[_r(S5_b, nt * 512, (nt + 1) * 512)])
        P.op("dve", lambda e: e.tensor_tensor(rtA.ap(), kx.ap(), c1f.ap(), op=ALU.mult),
             reads=[kx.r(), c1f.r()], writes=[rtA.r()])
        P.op("dve", lambda e: e.tensor_tensor(rtB.ap(), S5[:, 0:1024], s1f.ap(), op=ALU.mult),
             reads=[_r(S5_b, 0, 1024), s1f.r()], writes=[rtB.r()])
        P.op("dve", lambda e: e.tensor_tensor(rtA.ap(), rtA.ap(), rtB.ap(), op=ALU.add),
             reads=[rtA.r(), rtB.r()], writes=[rtA.r()])
        elu1(rtA.ap(), [rtA.r()], Qfh.ap(), [Qfh.r()],
             rtB.ap(), [rtB.r()], rtC.ap(), [rtC.r()])
        for c in range(8):
            P.mm(S5[:, c * 128:(c + 1) * 128],
                 KfF[:, kvh * TOWN + c * 128: kvh * TOWN + (c + 1) * 128],
                 Qfh[:, c * 128:(c + 1) * 128],
                 start=True, stop=True,
                 reads=[KfF.r(kvh * TOWN + c * 128, kvh * TOWN + (c + 1) * 128),
                        Qfh.r(c * 128, (c + 1) * 128)],
                 writes=[_r(S5_b, c * 128, (c + 1) * 128)])
        P.op("dve", lambda e: e.tensor_tensor(ATb.ap(), S5[:, 0:1024],
                                              C["trirep"][:], op=ALU.mult),
             reads=[_r(S5_b, 0, 1024)], writes=[ATb.r()])
        for c in range(8):
            P.mm(M2[:, c * 128:(c + 1) * 128],
                 VfT[:, c * 512 + kvh * 128: c * 512 + (kvh + 1) * 128],
                 ATb[:, c * 128:(c + 1) * 128], start=True, stop=False,
                 reads=[VfT.r(c * 512 + kvh * 128, c * 512 + (kvh + 1) * 128),
                        ATb.r(c * 128, (c + 1) * 128)],
                 writes=[_r(M2_b, c * 128, (c + 1) * 128)])
            P.mm(M2[:, c * 128:(c + 1) * 128],
                 Sprev[:, c * 512 + kvh * 128: c * 512 + (kvh + 1) * 128],
                 Qfh[:, c * 128:(c + 1) * 128],
                 start=False, stop=True,
                 reads=[Sprev.r(c * 512 + kvh * 128, c * 512 + (kvh + 1) * 128),
                        Qfh.r(c * 128, (c + 1) * 128)],
                 writes=[_r(M2_b, c * 128, (c + 1) * 128)])
            P.mm(S5[0:1, 1024 + c * 128: 1024 + (c + 1) * 128], C["ones_b"][:],
                 ATb[:, c * 128:(c + 1) * 128],
                 start=True, stop=False,
                 reads=[ATb.r(c * 128, (c + 1) * 128)],
                 writes=[_r(S5_b, 1024 + c * 128, 1024 + (c + 1) * 128)])
            P.mm(S5[0:1, 1024 + c * 128: 1024 + (c + 1) * 128],
                 zp[:, c * 4 + kvh: c * 4 + kvh + 1],
                 Qfh[:, c * 128:(c + 1) * 128],
                 start=False, stop=True,
                 reads=[zp.r(c * 4 + kvh, c * 4 + kvh + 1),
                        Qfh.r(c * 128, (c + 1) * 128)],
                 writes=[_r(S5_b, 1024 + c * 128, 1024 + (c + 1) * 128)])
        P.op("dve", lambda e: e.tensor_scalar(den.ap(), S5[0:1, 1024:2048],
                                              EPS, None, op0=ALU.add),
             reads=[_r(S5_b, 1024, 2048)], writes=[den.r()])
        P.op("dve", lambda e: e.reciprocal(den.ap(), den.ap()),
             reads=[den.r()], writes=[den.r()], fence=True)
        for nt in range(2):
            P.mm(S5[:, 1024 + nt * 512: 1024 + (nt + 1) * 512], C["onesrow_f"][:],
                 den[0:1, nt * 512:(nt + 1) * 512], start=True, stop=True,
                 reads=[den.r()], writes=[_r(S5_b, 1024 + nt * 512, 1024 + (nt + 1) * 512)])
        P.op("act", lambda e: e.copy(bcl.ap(), S5[:, 1024:2048]),
             reads=[_r(S5_b, 1024, 2048)], writes=[bcl.r()])
        P.op("dve", lambda e, h=h: e.tensor_tensor(
                attnL1[:, h * TOWN:(h + 1) * TOWN],
                M2[:, 0:1024], bcl.ap(), op=ALU.mult),
             reads=[_r(M2_b, 0, 1024), bcl.r()],
             writes=[attnL1.r(h * TOWN, (h + 1) * TOWN)])
    P.barrier()
    A16.release()
    A16.mark()
    wo2 = [A16.alloc("woL10", 128, 16 * 128), A16.alloc("woL11", 128, 16 * 128)]
    emit_matmul_residual(bd, "WoT", attnL1, wo2, X)
    A16.release()
    A16.release()
    A16.release(); A32.release()
    emit_postln_inplace(bd)


def emit_l1_ffn(bd):
    nc, P, C = bd.nc, bd.P, bd.C
    S5, S5_b = bd.ps["S5"]
    M2, M2_b = bd.ps["M2"]
    A16, A32 = bd.A16, bd.A32
    X, Xb = bd.X, bd.Xb

    A16.mark(); A32.mark()
    h4 = A16.alloc("h4", 128, 11 * 1024)
    wg2 = [A16.alloc("fg0", 128, 16 * 128), A16.alloc("fg1", 128, 16 * 128)]
    wu2 = [A16.alloc("fu0", 128, 16 * 128), A16.alloc("fu1", 128, 16 * 128)]
    wd2 = [A16.alloc("fd0", 128, 11 * 128), A16.alloc("fd1", 128, 11 * 128)]
    gsc = A32.alloc("gsc", 128, 44)
    usc = A32.alloc("usc", 128, 44)
    dsc = A32.alloc("dsc", 128, 16)
    sgl = A32.alloc("sgl", 128, 1024)
    P.dma("dsp", gsc.ap(), bd.dram["f_gs"], writes=[gsc.r()])
    P.dma("dsp", usc.ap(), bd.dram["f_us"], writes=[usc.r()])
    P.dma("dsp", dsc.ap(), bd.dram["f_ds"], writes=[dsc.r()])
    fgr = bd.dram["f_gateT"].rearrange("(k p) o -> p k o", p=128)
    fur = bd.dram["f_upT"].rearrange("(k p) o -> p k o", p=128)
    fdr = bd.dram["f_downT"].rearrange("(k p) o -> p k o", p=128)
    ROT = [(M2, M2_b, 0), (S5, S5_b, 0), (S5, S5_b, 1024)]
    for q in range(4):
        for j in range(11):
            jj = q * 11 + j
            P.dma("dpool", wg2[j % 2].ap().rearrange("p (k o) -> p k o", k=16),
                  fgr[:, :, jj * 128:(jj + 1) * 128], writes=[wg2[j % 2].r()])
            P.dma("dpool", wu2[j % 2].ap().rearrange("p (k o) -> p k o", k=16),
                  fur[:, :, jj * 128:(jj + 1) * 128], writes=[wu2[j % 2].r()])
            gp, gp_b, go = ROT[jj % 3]
            up, up_b, uo = ROT[(jj + 2) % 3]
            for nt in range(2):
                for k in range(16):
                    P.mm(gp[:, go + nt * 512: go + (nt + 1) * 512],
                         wg2[j % 2][:, k * 128:(k + 1) * 128],
                         Xb[:, k * TOWN + nt * 512: k * TOWN + (nt + 1) * 512],
                         start=(k == 0), stop=(k == 15),
                         reads=[wg2[j % 2].r(k * 128, (k + 1) * 128),
                                Xb.r(k * TOWN + nt * 512, k * TOWN + (nt + 1) * 512)],
                         writes=[_r(gp_b, go + nt * 512, go + (nt + 1) * 512)])
            for nt in range(2):
                for k in range(16):
                    P.mm(up[:, uo + nt * 512: uo + (nt + 1) * 512],
                         wu2[j % 2][:, k * 128:(k + 1) * 128],
                         Xb[:, k * TOWN + nt * 512: k * TOWN + (nt + 1) * 512],
                         start=(k == 0), stop=(k == 15),
                         reads=[wu2[j % 2].r(k * 128, (k + 1) * 128),
                                Xb.r(k * TOWN + nt * 512, k * TOWN + (nt + 1) * 512)],
                         writes=[_r(up_b, uo + nt * 512, uo + (nt + 1) * 512)])
            P.op("dve", lambda e, jj=jj, gp=gp, gp_b=gp_b, go=go: e.tensor_scalar(
                    sgl.ap(), gp[:, go:go + 1024], gsc[:, jj:jj + 1], None,
                    op0=ALU.mult),
                 reads=[_r(gp_b, go, go + 1024), gsc.r()], writes=[sgl.r()])
            P.op("act", lambda e: e.activation(sgl.ap(), sgl.ap(), AF.Silu),
                 reads=[sgl.r()], writes=[sgl.r()])
            P.op("dve", lambda e, j=j, jj=jj, up=up, up_b=up_b, uo=uo: e.scalar_tensor_tensor(
                    h4[:, j * 1024:(j + 1) * 1024], up[:, uo:uo + 1024],
                    usc[:, jj:jj + 1], sgl.ap(),
                    op0=ALU.mult, op1=ALU.mult),
                 reads=[_r(up_b, uo, uo + 1024), usc.r(), sgl.r()],
                 writes=[h4.r(j * 1024, (j + 1) * 1024)])
        for oc in range(16):
            P.dma("dpool", wd2[oc % 2].ap().rearrange("p (k o) -> p k o", k=11),
                  fdr[:, q * 11:(q + 1) * 11, oc * 128:(oc + 1) * 128],
                  writes=[wd2[oc % 2].r()])
            ps, ps_b, po = ROT[oc % 3]
            for nt in range(2):
                for k in range(11):
                    P.mm(ps[:, po + nt * 512: po + (nt + 1) * 512],
                         wd2[oc % 2][:, k * 128:(k + 1) * 128],
                         h4[:, k * 1024 + nt * 512: k * 1024 + (nt + 1) * 512],
                         start=(k == 0), stop=(k == 10),
                         reads=[wd2[oc % 2].r(k * 128, (k + 1) * 128),
                                h4.r(k * 1024 + nt * 512, k * 1024 + (nt + 1) * 512)],
                         writes=[_r(ps_b, po + nt * 512, po + (nt + 1) * 512)])
            P.op("dve", lambda e, oc=oc, ps=ps, ps_b=ps_b, po=po: e.scalar_tensor_tensor(
                    X[:, oc * TOWN:(oc + 1) * TOWN], ps[:, po:po + 1024],
                    dsc[:, oc:oc + 1], X[:, oc * TOWN:(oc + 1) * TOWN],
                    op0=ALU.mult, op1=ALU.add),
                 reads=[_r(ps_b, po, po + 1024), dsc.r(), X.r(oc * TOWN, (oc + 1) * TOWN)],
                 writes=[X.r(oc * TOWN, (oc + 1) * TOWN)])
    A16.release(); A32.release()
    P.barrier()
    emit_postln_inplace(bd)



SHARED_SPECS = {
    "wq_downT": (DM, QR), "wq_upT": (QR, DM), "wq_ropeT": (QR, H0 * ROPE0),
    "wkv_downT": (DM, KVR), "wkv_upKT": (KVR, H0 * NOPE), "wkv_upVT": (KVR, H0 * VDIM),
    "wk_ropeT": (DM, ROPE0), "woT": (DM, DM),
    "q_norm_w": (128, 8), "kv_norm_w": (128, 4),
    "gate_wT": (DM, NEXP), "w13T": (NEXP, DM, 2 * IMOE), "w2T": (NEXP, IMOE, DM),
    "sw13T": (DM, 2 * ISH), "sw2T": (ISH, DM),
    "WqT": (DM, DM), "WkT": (DM, KVH * HD), "WvT": (DM, KVH * HD), "WoT": (DM, DM),
    "f_gateT": (DM, IFF), "f_upT": (DM, IFF), "f_downT": (IFF, DM),
    "f_gs": (128, 44), "f_us": (128, 44), "f_ds": (128, 16),
    "c0tab": (64, SEQ), "s0tab": (64, SEQ), "c1tab": (128, SEQ), "s1tab": (128, SEQ),
    "perm64": (64, 64), "perm128": (128, 128),
    "tri128": (128, 128), "tri1024": (128, 1024), "ones128": (128, 1), "ones_row": (1, 128),
    "ident128": (128, 128),
    "fconsts": (128, len(FCONSTS)),
}
CORE_SPECS = {
    "xT": (DM, SEQ), "xT_own": (DM, TOWN), "l0mask": (4, 128, 2304),
    "c0_own": (64, TOWN), "s0_own": (64, TOWN),
    "c1_own": (128, TOWN), "s1_own": (128, TOWN), "l1prefw": (128, 128),
    "c1tok": (TOWN, 128), "s1tok": (TOWN, 128),
}


def make_builder(phases):
    nc = bass.Bass("TRN2", target_bir_lowering=False)
    P = Prog()
    stack = ExitStack()
    bd = Builder(nc, P, stack)
    for name, shape in SHARED_SPECS.items():
        bd.din(name, shape)
    for name, shape in CORE_SPECS.items():
        bd.din(name, shape)
    bd.ps = {
        "S5": bd.psum("S5", [128, 2560]),
        "M2": bd.psum("M2", [128, 1024]),
        "A1": bd.psum("A1", [128, 512]),
    }
    bd.A16 = Arena(bd, "AR16", AR16_COLS, BF16)
    bd.A32 = Arena(bd, "AR32", AR32_COLS, F32)
    bd.Xb = bd.A16.alloc("Xb", 128, 16 * TOWN)
    bd.X = bd.A32.alloc("X", 128, 16 * TOWN)
    return nc, P, stack, bd


def build_program(phases, stream_in=None, stream_out=True, dbg=False):
    nc, P, stack, bd = make_builder(phases)
    bd.dbg = bool(dbg)
    emit_consts(bd)
    if stream_in:
        xi = bd.din("x_in", (DM, TOWN))
        P.dma("dsp", bd.X.ap().rearrange("p (k t) -> p k t", k=16),
              xi.rearrange("(k p) t -> p k t", p=128), writes=[bd.X.r()])
        P.dma("dpool", bd.Xb.ap().rearrange("p (k t) -> p k t", k=16),
              xi.rearrange("(k p) t -> p k t", p=128), writes=[bd.Xb.r()])
        P.barrier()
    for ph in phases:
        if ph == "l0attn":
            emit_l0_attn(bd)
        elif ph == "moe":
            emit_l0_moe(bd)
        elif ph == "l1attn":
            emit_l1_attn(bd)
        elif ph == "l1ffn":
            emit_l1_ffn(bd)
    finals = []
    if stream_out:
        xo = bd.dout("x_out", (DM, TOWN))
        finals.append(P.dma("dsp", xo.rearrange("(k p) t -> p k t", p=128),
                            bd.X.ap().rearrange("p (k t) -> p k t", k=16),
                            reads=[bd.X.r()]))
    P.mark_final(finals)
    P.finalize(nc)
    stack.close()
    return nc



def _core_inmaps(inputs):
    shared = prep_shared(inputs)
    maps = []
    for core in range(8):
        d = dict(shared)
        d.update(prep_core(inputs, shared, core))
        maps.append(d)
    return maps, shared


def _own_cols(core):
    own = OWN[core % 2]
    return np.concatenate([np.arange(g * 128, (g + 1) * 128) for g in own])


def kernel(**inputs):
    maps, _ = _core_inmaps(inputs)
    phases = ["l0attn", "moe", "l1attn", "l1ffn"]
    nc = build_program(phases)
    res = run_bass_kernel_spmd(nc, maps, core_ids=list(range(8)))
    out = np.zeros((BSZ, SEQ, DM), dtype=np.float32)
    for core in range(8):
        xo = np.asarray(res.results[core]["x_out"])
        out[core // 2, _own_cols(core), :] = xo.T
    return out
